# version 01
# baseline (speedup 1.0000x reference)
import math
import numpy as np
import concourse.bass as bass
import concourse.mybir as mybir
from concourse.bass_utils import run_bass_kernel_spmd

F32 = mybir.dt.float32
BF16 = mybir.dt.bfloat16
AF = mybir.ActivationFunctionType
ALU = mybir.AluOpType
AX = mybir.AxisListType

D = 2048
NDB = 16
CH = 512
NH = 16
DHD = 128
HID1 = 5461
NKB1 = 43
NE = 8
MI = 1408
SI = 5632
EPS1 = 1e-5
EPS2 = 1e-6
CINV = DHD ** -0.5
LN2 = math.log(2.0)
LN1E6 = math.log(1.0e6)
TWO_PI = 2.0 * math.pi
GAMMA = [1.0 - 2.0 ** (-5.0 - h) for h in range(NH)]
LNG = [math.log(g) for g in GAMMA]

KW = NDB * CH
VW = 4 * D
SW = CH
GW = KW + VW + SW
NMC = 68


def ceil_div(a, b):
    return (a + b - 1) // b


class Prog:
    def __init__(self, nc, block):
        self.nc = nc
        self.block = block
        self.sec = {"T": block.tensor, "V": block.vector, "A": block.scalar,
                    "G": block.gpsimd, "S": block.sync}
        self.sem = {}
        self.cnt = {}
        for nm in ("cT", "cV", "cA", "cG", "dS", "dA", "dG", "cc"):
            self.sem[nm] = nc.alloc_semaphore(nm)
            self.cnt[nm] = 0
        self.seen = {e: {} for e in self.sec}

    def emit(self, eng, fn, waits=(), inc=None, inc_by=1):
        need = []
        seen = self.seen[eng]
        for t in waits:
            if t is None:
                continue
            nm, val = t
            if seen.get(nm, 0) >= val:
                continue
            seen[nm] = val
            need.append((nm, val))
        tok = None
        if inc is not None:
            self.cnt[inc] += inc_by
            tok = (inc, self.cnt[inc])
        sems = self.sem

        def run(e):
            for nm, val in need:
                e.wait_ge(sems[nm], val)
            inst = fn(e)
            if inc is not None:
                inst.then_inc(sems[inc], inc_by)
        self.sec[eng](run)
        return tok

    ESEM = {"T": "cT", "V": "cV", "A": "cA", "G": "cG"}
    DSEM = {"S": "dS", "A": "dA", "G": "dG"}

    def op(self, eng, fn, waits=(), inc=True):
        return self.emit(eng, fn, waits, inc=self.ESEM[eng] if inc else None)

    def getsem(self, nm):
        if nm not in self.sem:
            self.sem[nm] = self.nc.alloc_semaphore(nm)
            self.cnt[nm] = 0
        return nm

    def dma(self, eng, out, in_, waits=(), sem=None):
        if sem is not None:
            sem = self.getsem(sem)
        return self.emit(eng, lambda e: e.dma_start(out=out, in_=in_),
                         waits, inc=sem or self.DSEM[eng], inc_by=16)

    def full(self, nm):
        return (nm, self.cnt.get(nm, 0))

    def join(self):
        toks = [(nm, self.cnt[nm]) for nm in self.cnt]
        for eng in ("T", "V", "A", "G", "S"):
            self.emit(eng, lambda e: e.nop(), waits=toks)


class Slot:
    def __init__(self, ap):
        self.ap = ap
        self.ready = None
        self.users = []
        self.qsem = None

    def fill_waits(self):
        w = list(self.users)
        self.users = []
        return w

    def used(self, tok):
        if tok is not None:
            self.users.append(tok)


class Ring:
    def __init__(self, slots):
        self.slots = slots
        self.i = 0

    def next(self):
        s = self.slots[self.i % len(self.slots)]
        self.i += 1
        return s


def build_program(stage=99, debug=()):
    nc = bass.Bass("TRN2", target_bir_lowering=False, debug=False)

    dt = {}
    dt["x_fm"] = nc.dram_tensor("x_fm", [NDB, 128, 2 * CH], F32, kind="ExternalInput")
    for nm, sh in [
        ("a_wq", [D, D]), ("a_wk", [D, D]), ("a_wv", [D, D]), ("a_wg", [D, D]),
        ("a_wo", [D, D]), ("a_w1", [D, HID1]), ("a_w2", [D, HID1]), ("a_w3", [HID1, D]),
        ("f_wq", [D, D]), ("f_wk", [D, D]), ("f_wv", [D, D]), ("f_wo", [D, D]),
        ("f_router", [D, NE]),
        ("f_wge", [NE, D, MI]), ("f_wue", [NE, D, MI]), ("f_wde", [NE, MI, D]),
        ("f_shg", [D, SI]), ("f_shu", [D, SI]), ("f_shd", [SI, D]),
        ("f_sheg", [D, 1]),
    ]:
        dt[nm] = nc.dram_tensor(nm, sh, F32, kind="ExternalInput")
    for nm in ("n1w", "n1b", "n2w", "n2b", "lnf1", "lnf2", "bq2", "bk2", "bv2"):
        dt[nm] = nc.dram_tensor(nm, [128, NDB], F32, kind="ExternalInput")
    dt["pos"] = nc.dram_tensor("pos", [1, 2 * CH], F32, kind="ExternalInput")
    dt["rsmeta"] = nc.dram_tensor("rsmeta", [1, 2 * NMC], F32, kind="ExternalInput")
    dt["padmeta"] = nc.dram_tensor("padmeta", [1, 16], F32, kind="ExternalInput")
    dt["blerow"] = nc.dram_tensor("blerow", [2, 128], F32, kind="ExternalInput")

    y_out = nc.dram_tensor("y", [NDB, 128, 2 * CH], F32, kind="ExternalOutput")

    gin = [nc.dram_tensor(f"gin{i}", [128, GW], BF16) for i in range(4)]
    gout = [nc.dram_tensor(f"gout{i}", [4 * 128, GW], BF16) for i in range(4)]
    qspill = nc.dram_tensor("qspill", [2, 128, NDB * CH], BF16)
    hgspill = nc.dram_tensor("hgspill", [2, 128, NDB * CH], BF16)
    xmidspill = nc.dram_tensor("xmidspill", [NDB, 128, 2 * CH], F32)

    TAPSHAPES = {
        "h1": [NDB, 128, 2 * CH], "k1": [NDB, 128, 2 * CH], "v1": [2, 4, 128, D],
        "q1": [NDB, 128, 2 * CH], "hg": [NDB, 128, 2 * CH],
        "retz": [NDB, 128, 2 * CH], "xwo": [NDB, 128, 2 * CH],
        "h2": [NDB, 128, 2 * CH], "xmid": [NDB, 128, 2 * CH],
        "q2": [NDB, 128, 2 * CH], "k2": [NDB, 128, 2 * CH], "v2": [2, 4, 128, D],
        "ctx2": [NDB, 128, 2 * CH], "xwo2": [NDB, 128, 2 * CH],
        "h2r": [NDB, 128, 2 * CH], "comb": [8, 2 * CH],
        "gin1": [2, 128, GW], "gin2": [2, 128, GW],
        "rope": [4, 128, 512], "att2l": [16, 512], "att2b": [128, 16],
        "att2k": [1, 128],
        "moe": [NDB, 128, 2 * CH],
    }
    taps = {t: nc.dram_tensor("tap_" + t, TAPSHAPES[t], F32, kind="ExternalOutput")
            for t in debug}

    x_sb = nc.alloc_sbuf_tensor("x_sb", [128, NDB * 2 * CH], F32)
    h_sb = nc.alloc_sbuf_tensor("h_sb", [128, NDB * 2 * CH], BF16)
    arena = nc.alloc_sbuf_tensor("arena", [128, 44 * 512], BF16)
    wr_sb = nc.alloc_sbuf_tensor("wr_sb", [128, 7 * 512], BF16)
    fs1 = nc.alloc_sbuf_tensor("fs1", [128, 512], F32)
    fs2 = nc.alloc_sbuf_tensor("fs2", [128, 512], F32)
    fs3 = nc.alloc_sbuf_tensor("fs3", [128, 512], F32)
    cosq = [nc.alloc_sbuf_tensor(f"cos{u}", [128, 512], BF16) for u in range(2)]
    sinq = [nc.alloc_sbuf_tensor(f"sin{u}", [128, 512], BF16) for u in range(2)]
    tri = nc.alloc_sbuf_tensor("tri", [128, 896], BF16)
    grow = nc.alloc_sbuf_tensor("grow", [128, NH * NMC], F32)
    gcols = nc.alloc_sbuf_tensor("gcols", [NH, 512], F32)
    st16 = gcols
    st8f = nc.alloc_sbuf_tensor("st8f", [1, 2112], F32)
    ones1r = nc.alloc_sbuf_tensor("ones1r", [1, 128], F32)
    e16 = nc.alloc_sbuf_tensor("e16", [16, 16 * 128], F32)
    onesst = nc.alloc_sbuf_tensor("onesst", [128, 1], BF16)
    lind = nc.alloc_sbuf_tensor("lind", [128, 16 * 16], BF16)
    iotac = nc.alloc_sbuf_tensor("iotac", [128, 1], F32)
    lng_sb = nc.alloc_sbuf_tensor("lng_sb", [16, 1], F32)
    invf = nc.alloc_sbuf_tensor("invf", [128, 1], F32)
    metars = nc.alloc_sbuf_tensor("metars", [1, 2 * NMC], F32)
    metapad = nc.alloc_sbuf_tensor("metapad", [1, 16], F32)
    blerow = nc.alloc_sbuf_tensor("ble_sb", [1, 256], BF16)
    kstatr = nc.alloc_sbuf_tensor("kstatr", [1, 128], BF16)
    mrow = nc.alloc_sbuf_tensor("mrow", [1, 288], F32)
    brow = nc.alloc_sbuf_tensor("brow", [1, 16], F32)
    bbase = nc.alloc_sbuf_tensor("bbase", [128, 16], F32)
    basb = nc.alloc_sbuf_tensor("basb", [128, 48], F32)
    vecs = {}
    for nm in ("n1w", "n1b", "n2w", "n2b", "lnf1", "lnf2", "bq2", "bkc2", "bv2"):
        vecs[nm] = nc.alloc_sbuf_tensor("v_" + nm, [128, NDB], F32)
    ident = nc.alloc_sbuf_tensor("ident", [128, 128], F32)
    qtl = nc.alloc_sbuf_tensor("qtl", [128, 6 * 512], BF16)
    cmbT = nc.alloc_sbuf_tensor("cmbT", [8, 2 * 512], F32)
    router_sb = nc.alloc_sbuf_tensor("router_sb", [128, 16 * 8], BF16)
    sigr = nc.alloc_sbuf_tensor("sigr", [1, 1024], F32)
    metap = sigr
    sigb = nc.alloc_sbuf_tensor("sigb", [128, 512], F32)
    ptile = nc.alloc_sbuf_tensor("ptile", [128, 2 * 512], BF16)
    i32t = nc.alloc_sbuf_tensor("i32t", [128, 256], mybir.dt.int32)

    tapb = nc.alloc_sbuf_tensor("tapb", [128, 512], F32) if debug else None

    pb = [nc.alloc_psum_tensor(f"pb{i}", [128, 512], F32) for i in range(8)]

    rem = nc.sbuf_bytes_remaining
    assert rem >= 0, f"SBUF over budget by {-rem} bytes"

    with nc.Block() as block:
        P = Prog(nc, block)
        banks = [Slot(pb[i]) for i in range(8)]
        wring = Ring([Slot(wr_sb[:, i * 512:(i + 1) * 512]) for i in range(7)])
        for i, s in enumerate(wring.slots):
            s.qsem = f"dW{i}"
        aslot = [Slot(arena[:, i * 512:(i + 1) * 512]) for i in range(44)]
        bounce = Ring([aslot[40], aslot[41], aslot[42], aslot[43]])
        fsl2 = Slot(fs2)
        fsl2.qsem = "dF0"
        fsl3 = Slot(fs3)
        fsl3.qsem = "dF1"
        tap_sl = Slot(None)
        bbase_slot = Slot(None)
        biasall_slot = Slot(None)
        st8_sl = Slot(None)
        st16_sl = Slot(None)
        brow_sl = Slot(None)
        sigr_sl = Slot(None)

        def stc(i, w=512):
            return st8f[0:1, i * 512:i * 512 + w]

        def xt(db, u):
            return x_sb[:, db * 2 * CH + u * CH: db * 2 * CH + (u + 1) * CH]

        def ht(db, u):
            return h_sb[:, db * 2 * CH + u * CH: db * 2 * CH + (u + 1) * CH]

        for nm in ("n1w", "n1b", "n2w", "n2b", "lnf1", "lnf2", "bq2", "bv2"):
            P.dma("A", vecs[nm][:, :], dt[nm][:, :], sem="dV")
        P.dma("A", vecs["bkc2"][:, :], dt["bk2"][:, :], sem="dV")
        P.dma("A", metap[:, :], dt["pos"][:, :], sem="dV")
        P.dma("A", metars[:, :], dt["rsmeta"][:, :], sem="dV")
        P.dma("A", metapad[:, :], dt["padmeta"][:, :], sem="dV")
        tok_vecs = P.full("dV")
        tok_pos = tok_rs = tok_pad = tok_vecs
        vtok = {nm: tok_vecs for nm in
                ("n1w", "n1b", "n2w", "n2b", "lnf1", "lnf2", "bq2", "bv2")}
        vtok["bkc2"] = P.op("V", lambda e: e.tensor_single_scalar(
            vecs["bkc2"][:, :], vecs["bkc2"][:, :], CINV, ALU.mult),
            waits=[tok_vecs])
        xtok = [[None, None] for _ in range(NDB)]
        for db in range(NDB):
            P.dma("A", x_sb[:, db * 2 * CH:(db + 1) * 2 * CH], dt["x_fm"][db],
                  sem="dX")
        tok_x = P.full("dX")
        for db in range(NDB):
            xtok[db][0] = xtok[db][1] = tok_x
        for kb in range(NDB):
            P.dma("G", router_sb[:, kb * 8:(kb + 1) * 8],
                  dt["f_router"][kb * 128:(kb + 1) * 128, :], sem="dM")
        P.dma("G", blerow[:, :], dt["blerow"][:, :], sem="dM")
        tok_misc = P.full("dM")
        tok_ble = tok_misc
        rtok = [tok_misc] * NDB

        tok_iota = P.op("G", lambda e: e.iota(
            iotac[:, :], [[0, 1]], channel_multiplier=1,
            allow_small_or_imprecise_dtypes=True))
        t_ = P.op("G", lambda e: e.iota(
            lng_sb[:, :], [[0, 1]], base=5, channel_multiplier=1,
            allow_small_or_imprecise_dtypes=True))
        t_ = P.op("A", lambda e: e.activation(
            lng_sb[:, :], lng_sb[:, :], AF.Exp, bias=0.0, scale=-LN2),
            waits=[t_])
        t_ = P.op("V", lambda e: e.tensor_scalar(
            lng_sb[:, :], lng_sb[:, :], -1.0, 1.0, ALU.mult, ALU.add), waits=[t_])
        tok_lng = P.op("A", lambda e: e.activation(
            lng_sb[:, :], lng_sb[:, :], AF.Ln, bias=0.0, scale=1.0), waits=[t_])

        t_ = P.op("V", lambda e: e.tensor_single_scalar(
            invf[:, :], iotac[:, :], 64.0, ALU.is_ge), waits=[tok_iota])
        t_ = P.op("V", lambda e: e.scalar_tensor_tensor(
            invf[:, :], invf[:, :], -64.0, iotac[:, :], ALU.mult, ALU.add),
            waits=[t_])
        tok_invf = P.op("A", lambda e: e.activation(
            invf[:, :], invf[:, :], AF.Exp, bias=0.0, scale=-LN1E6 / 64.0), waits=[t_])

        tok_e2 = P.op("G", lambda e: e.memset(ones1r[:, :], 1.0))

        t_ = P.op("G", lambda e: e.memset(e16[:, :], 1.0))
        t_ = P.op("G", lambda e: e.affine_select(
            e16[:, :], e16[:, :], pattern=[[1, 16 * 128]], compare_op=ALU.is_ge,
            fill=0.0, base=0, channel_multiplier=-128), waits=[t_])
        tok_e16 = P.op("G", lambda e: e.affine_select(
            e16[:, :], e16[:, :], pattern=[[-1, 16 * 128]], compare_op=ALU.is_ge,
            fill=0.0, base=127, channel_multiplier=128), waits=[t_])

        tok_ones = P.op("G", lambda e: e.memset(onesst[:, :], 1.0))

        t_ = P.op("G", lambda e: e.memset(lind[:, :], 0.0))
        tok_lind = t_
        for h in range(NH):
            tok_lind = P.op("G", lambda e, h=h: e.memset(
                lind[:, 16 * h + h:16 * h + h + 1], 1.0), waits=[tok_lind])

        t_ = P.op("G", lambda e: e.memset(ident[:, :], 1.0))
        tok_id = P.op("G", lambda e: e.affine_select(
            ident[:, :], ident[:, :], pattern=[[1, 128]], compare_op=ALU.is_equal,
            fill=0.0, base=0, channel_multiplier=-1), waits=[t_])

        t_ = P.op("G", lambda e: e.memset(tri[:, :], 1.0))
        _tt = P.op("G", lambda e: e.affine_select(
            tri[:, :], tri[:, :], pattern=[[1, 896]], compare_op=ALU.is_ge,
            fill=0.0, base=-384, channel_multiplier=-1), waits=[t_])
        tok_tri = [_tt] * 4

        def trim(m):
            return tri[:, 384 - 128 * m:896 - 128 * m]

        t_ = P.op("G", lambda e: e.iota(
            gcols[:, :], [[1, 512]], channel_multiplier=0,
            allow_small_or_imprecise_dtypes=True))
        tok_gcols = P.op("A", lambda e: e.activation(
            gcols[:, :], gcols[:, :], AF.Exp, bias=0.0, scale=lng_sb[:, :]),
            waits=[t_, tok_lng])

        tok_kstat0 = P.op("G", lambda e: e.memset(kstatr[:, :], 0.0))
        tok_sig0 = None
        tok_brow0 = P.op("G", lambda e: e.memset(brow[:, :], 0.0))

        bkD, bkV = banks[6], banks[7]
        tD = P.op("T", lambda e: e.matmul(
            bkD.ap[:, 0:NMC], ones1r[:, :], metars[0:1, 0:NMC],
            start=True, stop=True), waits=[tok_e2, tok_rs] + bkD.fill_waits())
        tV = P.op("T", lambda e: e.matmul(
            bkV.ap[:, 0:NMC], ones1r[:, :], metars[0:1, NMC:2 * NMC],
            start=True, stop=True), waits=[tok_e2, tok_rs] + bkV.fill_waits())
        tDm = P.op("V", lambda e: e.tensor_scalar(
            fs1[:, 0:NMC], bkD.ap[:, 0:NMC], iotac[:, :], None, ALU.subtract),
            waits=[tD, tok_iota])
        bkD.used(tDm)
        grow_tok = [None] * NH
        for h in range(NH):
            te = P.op("A", lambda e, h=h: e.activation(
                grow[:, NMC * h:NMC * (h + 1)], fs1[:, 0:NMC], AF.Exp, bias=0.0,
                scale=LNG[h]), waits=[tDm])
            tm = P.op("V", lambda e, h=h: e.tensor_tensor(
                grow[:, NMC * h:NMC * (h + 1)], grow[:, NMC * h:NMC * (h + 1)],
                bkV.ap[:, 0:NMC], ALU.mult), waits=[te, tV])
            bkV.used(tm)
            grow_tok[h] = tm

        def growcol(h, qv, s, kt):
            c = NMC * h + ((qv * 8 + s) * 4 + kt if s >= 0 else 64 + kt)
            return grow[:, c:c + 1]

        tok_cs = []
        for u in range(2):
            bA = banks[6]
            tb = P.op("T", lambda e, u=u: e.matmul(
                bA.ap[:, :], ones1r[:, :], metap[0:1, u * 512:(u + 1) * 512],
                start=True, stop=True),
                waits=[tok_e2, tok_pos] + bA.fill_waits())
            t1 = P.op("V", lambda e: e.tensor_scalar(
                fs2[:, :], bA.ap[:, :], invf[:, :], None, ALU.mult),
                waits=[tb, tok_invf])
            bA.used(t1)
            prev_act = None
            for trig, dst in (("cos", cosq[u]), ("sin", sinq[u])):
                shift = 0.0 if trig == "sin" else math.pi / 2.0
                t2 = P.op("V", lambda e, shift=shift: e.tensor_scalar(
                    fs3[:, :], fs2[:, :], shift, 1.0 / TWO_PI, ALU.add, ALU.mult),
                    waits=[t1, prev_act] + tok_cs)
                t3 = t2
                for hf in range(2):
                    t3a = P.op("V", lambda e, hf=hf: e.tensor_copy(
                        i32t[:, :], fs3[:, hf * 256:(hf + 1) * 256]), waits=[t3])
                    t3 = P.op("V", lambda e, hf=hf: e.tensor_copy(
                        fs1[:, hf * 256:(hf + 1) * 256], i32t[:, :]), waits=[t3a])
                t4a = P.op("V", lambda e, shift=shift: e.tensor_scalar(
                    fs3[:, :], fs2[:, :], shift, None, ALU.add), waits=[t2])
                t4 = P.op("V", lambda e: e.scalar_tensor_tensor(
                    fs3[:, :], fs1[:, :], -TWO_PI, fs3[:, :], ALU.mult, ALU.add),
                    waits=[t3, t4a])
                if trig == "cos":
                    t5 = P.op("A", lambda e, dst=dst: e.activation(
                        dst[:, :], fs3[:, :], AF.Sin, bias=0.0, scale=1.0),
                        waits=[t4, t3, t4a])
                else:
                    t5a = P.op("A", lambda e: e.activation(
                        fs1[:, :], fs3[:, :], AF.Sin, bias=0.0, scale=1.0),
                        waits=[t4, t3, t4a])
                    t5b = P.op("V", lambda e, dst=dst: e.tensor_copy(
                        dst[0:64, :], fs1[64:128, :]), waits=[t5a])
                    t5 = P.op("V", lambda e, dst=dst: e.tensor_copy(
                        dst[64:128, :], fs1[0:64, :]), waits=[t5b])
                prev_act = t5
                tok_cs.append(t5)
        if "rope" in taps:
            for i, tile in enumerate([cosq[0], sinq[0], cosq[1], sinq[1]]):
                tcp = P.op("V", lambda e, tile=tile: e.tensor_copy(
                    tapb[:, :], tile[:, :]), waits=tok_cs + tap_sl.fill_waits())
                td = P.dma("S", taps["rope"][i], tapb[:, :], waits=[tcp],
                           sem="dT")
                tap_sl.used(td)
        tok_sig0 = P.op("G", lambda e: e.memset(sigr[:, :], 0.0),
                        waits=[tok_cs[-1]] + tok_cs)

        def ln_stats(src, src_toks, rms, n, eps):
            bkS, bkQ = banks[5], banks[4]
            firstS = True
            firstQ = True
            prep = []
            for db in range(NDB):
                xb, sq = aslot[db], aslot[NDB + db]
                if not rms:
                    tc = P.op("V", lambda e, db=db, xb=xb: e.tensor_copy(
                        xb.ap, src(db)), waits=[src_toks(db)] + xb.fill_waits())
                    xb.ready = tc
                ts = P.op("A", lambda e, db=db, sq=sq: e.activation(
                    sq.ap, src(db), AF.Square, bias=0.0, scale=1.0),
                    waits=[src_toks(db)] + sq.fill_waits())
                sq.ready = ts
                prep.append((xb, sq))
            tS = None
            tQ = None
            for db in range(NDB):
                xb, sq = prep[db]
                last = db == NDB - 1
                if not rms:
                    w = [xb.ready, tok_ones] + (bkS.fill_waits() if firstS else [])
                    tS = P.op("T", lambda e, xb=xb, f=firstS, l=last: e.matmul(
                        bkS.ap[0:1, :], onesst[:, 0:1], xb.ap, start=f, stop=l),
                        waits=w, inc=True)
                    xb.used(tS)
                    firstS = False
                w = [sq.ready, tok_ones] + (bkQ.fill_waits() if firstQ else [])
                tQ = P.op("T", lambda e, sq=sq, f=firstQ, l=last: e.matmul(
                    bkQ.ap[0:1, :], onesst[:, 0:1], sq.ap, start=f, stop=l),
                    waits=w, inc=True)
                sq.used(tQ)
                firstQ = False
            stw = st8_sl.fill_waits()
            if rms:
                t1 = P.op("V", lambda e: e.memset(stc(0), 0.0), waits=stw)
            else:
                t1 = P.op("V", lambda e: e.tensor_single_scalar(
                    stc(0), bkS.ap[0:1, :], -1.0 / n, ALU.mult),
                    waits=[tS] + stw)
                bkS.used(t1)
            t2 = P.op("V", lambda e, eps=eps: e.tensor_scalar(
                stc(2), bkQ.ap[0:1, :], 1.0 / n, eps, ALU.mult, ALU.add),
                waits=[tQ] + stw)
            bkQ.used(t2)
            if not rms:
                t3 = P.op("V", lambda e: e.tensor_tensor(
                    stc(3), stc(0), stc(0), ALU.mult), waits=[t1])
                t2 = P.op("V", lambda e: e.tensor_tensor(
                    stc(2), stc(2), stc(3), ALU.subtract), waits=[t2, t3])
            t4 = P.op("A", lambda e: e.activation(
                stc(2), stc(2), AF.Sqrt, bias=0.0, scale=1.0), waits=[t2])
            t5 = P.op("V", lambda e: e.reciprocal(stc(1), stc(2)), waits=[t4])
            return (t1, t5)

        def ln_bcast(rows_toks):
            bM, bI = banks[5], banks[4]
            tM = P.op("T", lambda e: e.matmul(
                bM.ap[:, :], ones1r[:, :], stc(0), start=True, stop=True),
                waits=list(rows_toks) + [tok_e2] + bM.fill_waits())
            tI = P.op("T", lambda e: e.matmul(
                bI.ap[:, :], ones1r[:, :], stc(1), start=True, stop=True),
                waits=list(rows_toks) + [tok_e2] + bI.fill_waits())
            st8_sl.used(tM)
            st8_sl.used(tI)
            return tM, tI

        def ln_apply(dst, src, src_toks, wv, bv, tM, tI, extra=()):
            bM, bI = banks[5], banks[4]
            toks = []
            for db in range(NDB):
                t1 = P.op("V", lambda e, db=db: e.tensor_tensor(
                    fs1[:, :], src(db), bM.ap[:, :], ALU.add),
                    waits=[src_toks(db), tM] + list(extra))
                bM.used(t1)
                if wv is not None:
                    t2 = P.op("V", lambda e, db=db: e.scalar_tensor_tensor(
                        fs2[:, :], fs1[:, :], wv[:, db:db + 1], bI.ap[:, :],
                        ALU.mult, ALU.mult), waits=[t1, tI])
                    bI.used(t2)
                    t2 = P.op("V", lambda e, db=db: e.tensor_scalar(
                        dst(db), fs2[:, :], bv[:, db:db + 1], None, ALU.add),
                        waits=[t2])
                else:
                    t2 = P.op("V", lambda e, db=db: e.tensor_tensor(
                        dst(db), fs1[:, :], bI.ap[:, :], ALU.mult), waits=[t1, tI])
                    bI.used(t2)
                toks.append(t2)
            return toks

        def rms_apply(dst, src, src_toks, wv, tI):
            bI = banks[4]
            toks = []
            for db in range(NDB):
                t2 = P.op("V", lambda e, db=db: e.scalar_tensor_tensor(
                    dst(db), src(db), wv[:, db:db + 1], bI.ap[:, :],
                    ALU.mult, ALU.mult), waits=[src_toks(db), tI])
                bI.used(t2)
                toks.append(t2)
            return toks

        def proj_pass(W, Ktot, O, movings, evac, g_oc=2):
            NKB = ceil_div(Ktot, 128)
            NOC = ceil_div(O, 128)
            nl = len(movings)
            assert g_oc * nl <= 4
            ocs = list(range(NOC))
            ogs = [ocs[i:i + g_oc] for i in range(0, NOC, g_oc)]
            for ogi, og in enumerate(ogs):
                base = 0 if ogi % 2 == 0 else 4
                c0 = og[0] * 128
                c1 = min(og[-1] * 128 + 128, O)
                width = c1 - c0
                mm_tok = {}
                for kb in range(NKB):
                    r0 = kb * 128
                    rows = min(128, Ktot - r0)
                    slot = wring.next()
                    wt = P.dma("G", slot.ap[0:rows, 0:width],
                               W[r0:r0 + rows, c0:c1], waits=slot.fill_waits(),
                               sem=slot.qsem)
                    slot.ready = wt
                    for oi, oc in enumerate(og):
                        w0 = oc * 128 - c0
                        w1 = min(w0 + 128, width)
                        for li, (tiles, ttok) in enumerate(movings):
                            bk = banks[base + oi * nl + li]
                            first = kb == 0
                            last = kb == NKB - 1
                            waits = [wt, ttok(kb)]
                            if first:
                                waits += bk.fill_waits()
                            tok = P.op("T", lambda e, bk=bk, slot=slot, rows=rows,
                                       w0=w0, w1=w1, tiles=tiles, kb=kb,
                                       first=first, last=last: e.matmul(
                                           bk.ap[0:(w1 - w0), :],
                                           slot.ap[0:rows, w0:w1],
                                           tiles(kb)[0:rows, :],
                                           start=first, stop=last),
                                       waits=waits, inc=True)
                            slot.used(tok)
                            if last:
                                mm_tok[(oi, li)] = tok
                for oi, oc in enumerate(og):
                    ow = min(oc * 128 + 128, O) - oc * 128
                    for li in range(nl):
                        bk = banks[base + oi * nl + li]
                        evac(li, oc, bk, mm_tok[(oi, li)], ow)

        def vproj_pass(W, movings, evac):
            nl = len(movings)
            assert nl == 1
            tiles, ttok = movings[0]
            for og in range(4):
                base = 0 if og % 2 == 0 else 4
                mm_tok = {}
                for kb in range(NDB):
                    slot = wring.next()
                    wt = P.dma("G", slot.ap[:, :],
                               W[kb * 128:(kb + 1) * 128, og * 512:(og + 1) * 512],
                               waits=slot.fill_waits(), sem=slot.qsem)
                    slot.ready = wt
                    for tt in range(4):
                        bk = banks[base + tt]
                        first = kb == 0
                        last = kb == NDB - 1
                        waits = [wt, ttok(kb)]
                        if first:
                            waits += bk.fill_waits()
                        tok = P.op("T", lambda e, bk=bk, slot=slot, tiles=tiles,
                                   kb=kb, tt=tt, first=first, last=last: e.matmul(
                                       bk.ap[:, :],
                                       tiles(kb)[:, tt * 128:(tt + 1) * 128],
                                       slot.ap[:, :], start=first, stop=last),
                                   waits=waits, inc=True)
                        slot.used(tok)
                        if last:
                            mm_tok[tt] = tok
                for tt in range(4):
                    evac(0, tt, og, banks[base + tt], mm_tok[tt])

        def tap_fm(name, db, u, src_ap, wtok, srcslot=None):
            if name in taps:
                t = P.op("V", lambda e: e.tensor_copy(tapb[:, :], src_ap),
                         waits=[wtok] + tap_sl.fill_waits())
                td = P.dma("S", taps[name][db, :, u * CH:(u + 1) * CH],
                           tapb[:, :], waits=[t], sem="dT")
                tap_sl.used(td)
                if srcslot is not None:
                    srcslot.used(t)

        htok = [[None, None] for _ in range(NDB)]
        for u in range(2):
            rt = ln_stats(lambda db, u=u: xt(db, u), lambda db, u=u: xtok[db][u],
                          rms=False, n=D, eps=EPS1)
            tM, tI = ln_bcast(rt)
            toks = ln_apply(lambda db, u=u: ht(db, u), lambda db, u=u: xt(db, u),
                            lambda db, u=u: xtok[db][u], vecs["n1w"], vecs["n1b"],
                            tM, tI, extra=[vtok["n1w"], vtok["n1b"]])
            for db in range(NDB):
                htok[db][u] = toks[db]
                tap_fm("h1", db, u, ht(db, u), toks[db])

        cc_tok = [None] * 4

        def kv_wave(u, blk):
            gi = (0 if blk == 1 else 2) + u
            gsem = f"dG{gi}"
            statrow_toks = []
            tz = P.op("V", lambda e: e.memset(aslot[32].ap, 0.0),
                      waits=aslot[32].fill_waits())
            aslot[32].ready = tz

            def kevac(li, oc, bk, mmtok, ow=128):
                s = aslot[oc]
                if blk == 1:
                    t = P.op("V", lambda e, bk=bk, s=s: e.tensor_single_scalar(
                        s.ap, bk.ap[:, :], CINV, ALU.mult),
                        waits=[mmtok] + s.fill_waits())
                else:
                    t0 = P.op("V", lambda e, bk=bk, oc=oc: e.tensor_scalar(
                        fs1[:, :], bk.ap[:, :], CINV, vecs["bkc2"][:, oc:oc + 1],
                        ALU.mult, ALU.add), waits=[mmtok, vtok["bkc2"]])
                    t = rope_apply(u, fs1, s, [t0] + s.fill_waits())
                bk.used(t)
                s.ready = t
                if blk == 2:
                    statrow_toks.append(k2_stats(oc, s))
                tap_fm("k1" if blk == 1 else "k2", oc, u, s.ap, t, srcslot=s)

            def k2_stats(oc, s):
                sq = bounce.next()
                ts = P.op("A", lambda e, s=s, sq=sq: e.activation(
                    sq.ap, s.ap, AF.Square, bias=0.0, scale=1.0),
                    waits=[s.ready] + sq.fill_waits())
                sq.ready = ts
                s.used(ts)
                bk = banks[3]
                tm = P.op("T", lambda e, sq=sq, bk=bk: e.matmul(
                    bk.ap[0:1, :], onesst[:, 0:1], sq.ap, start=True, stop=True),
                    waits=[ts, tok_ones] + bk.fill_waits())
                sq.used(tm)
                t1 = P.op("V", lambda e, bk=bk: e.reduce_max(
                    st8f[0:1, 2080:2081], bk.ap[0:1, :], axis=AX.X), waits=[tm])
                bk.used(t1)
                t2 = P.op("A", lambda e: e.activation(
                    st8f[0:1, 2080:2081], st8f[0:1, 2080:2081], AF.Sqrt,
                    bias=0.0, scale=1.0), waits=[t1])
                t3 = P.op("V", lambda e, oc=oc: e.tensor_single_scalar(
                    st8f[0:1, 2048 + oc:2049 + oc], st8f[0:1, 2080:2081],
                    1.01, ALU.mult), waits=[t2])
                return t3

            wname = "a_wk" if blk == 1 else "f_wk"
            mov = [(lambda kb, u=u: ht(kb, u), lambda kb, u=u: htok[kb][u])]
            proj_pass(dt[wname], D, D, mov, kevac, g_oc=3)
            if blk == 2:
                tb = P.op("V", lambda e: e.tensor_copy(
                    aslot[32].ap[0:1, 0:16], st8f[0:1, 2048:2064]),
                    waits=statrow_toks + [tz])
                aslot[32].ready = tb

            def vevac(li, tt, og, bk, mmtok):
                s = aslot[16 + tt * 4 + og]
                t = P.op("V", lambda e, bk=bk, s=s: e.tensor_copy(
                    s.ap, bk.ap[:, :]), waits=[mmtok] + s.fill_waits())
                bk.used(t)
                s.ready = t
                if ("v1" if blk == 1 else "v2") in taps:
                    nm = "v1" if blk == 1 else "v2"
                    tcp = P.op("V", lambda e, s=s: e.tensor_copy(tapb[:, :], s.ap),
                               waits=[t] + tap_sl.fill_waits())
                    s.used(tcp)
                    tdd = P.dma("S", taps[nm][u, tt, :, og * 512:(og + 1) * 512],
                                tapb[:, :], waits=[tcp], sem="dT")
                    tap_sl.used(tdd)

            vproj_pass(dt["a_wv" if blk == 1 else "f_wv"], mov, vevac)
            sready = [aslot[i].ready for i in range(33)]
            tst = P.dma("S", gin[gi].ap(), arena[:, 0:GW], waits=sready, sem=gsem)
            for i in range(33):
                aslot[i].used(tst)
            cc_tok[gi] = P.emit("G", lambda e, gi=gi: e.collective_compute(
                "AllGather", ALU.bypass,
                replica_groups=[[0, 1, 2, 3], [4, 5, 6, 7]],
                ins=[gin[gi].ap().opt()], outs=[gout[gi].ap().opt()]),
                waits=[tst], inc="cc")

        def rope_apply(u, src_f32, dstslot, waits):
            w = list(waits) + [tok_cs[-1]]
            ta = P.op("V", lambda e: e.tensor_tensor(
                fs2[:, :], src_f32[:, :], cosq[u][:, :], ALU.mult), waits=w)
            tb0 = P.op("V", lambda e: e.tensor_tensor(
                fs3[0:64, :], src_f32[64:128, :], sinq[u][64:128, :], ALU.mult),
                waits=w)
            tb1 = P.op("V", lambda e: e.tensor_tensor(
                fs3[64:128, :], src_f32[0:64, :], sinq[u][0:64, :], ALU.mult),
                waits=w)
            t0 = P.op("V", lambda e, d=dstslot: e.tensor_tensor(
                d.ap[0:64, :], fs2[0:64, :], fs3[0:64, :], ALU.subtract),
                waits=[ta, tb0] + dstslot.fill_waits())
            t1 = P.op("V", lambda e, d=dstslot: e.tensor_tensor(
                d.ap[64:128, :], fs2[64:128, :], fs3[64:128, :], ALU.add),
                waits=[ta, tb1])
            return t1

        kv_wave(0, 1)
        kv_wave(1, 1)

        def qevac1(li, oc, bk, mmtok, ow=128):
            s = aslot[li * 16 + oc]
            t = P.op("V", lambda e, bk=bk, s=s: e.tensor_copy(s.ap, bk.ap[:, :]),
                     waits=[mmtok] + s.fill_waits())
            bk.used(t)
            s.ready = t
            tap_fm("q1", oc, li, s.ap, t, srcslot=s)

        proj_pass(dt["a_wq"], D, D,
                  [(lambda kb: ht(kb, 0), lambda kb: htok[kb][0]),
                   (lambda kb: ht(kb, 1), lambda kb: htok[kb][1])],
                  qevac1, g_oc=2)
        for li in range(2):
            rd = [aslot[li * 16 + i].ready for i in range(16)]
            tq = P.dma("S", qspill[li], arena[:, li * 16 * 512:(li + 1) * 16 * 512],
                       waits=rd, sem="dQS")
            for i in range(16):
                aslot[li * 16 + i].used(tq)
        tok_qs1 = P.full("dQS")

        def gevac(li, oc, bk, mmtok, ow=128):
            s = aslot[li * 16 + oc]
            t = P.op("A", lambda e, bk=bk, s=s: e.activation(
                s.ap, bk.ap[:, :], AF.Silu, bias=0.0, scale=1.0),
                waits=[mmtok] + s.fill_waits())
            bk.used(t)
            s.ready = t
            tap_fm("hg", oc, li, s.ap, t, srcslot=s)

        proj_pass(dt["a_wg"], D, D,
                  [(lambda kb: ht(kb, 0), lambda kb: htok[kb][0]),
                   (lambda kb: ht(kb, 1), lambda kb: htok[kb][1])],
                  gevac, g_oc=2)
        for li in range(2):
            rd = [aslot[li * 16 + i].ready for i in range(16)]
            thg = P.dma("S", hgspill[li],
                        arena[:, li * 16 * 512:(li + 1) * 16 * 512],
                        waits=rd, sem="dHS")
            for i in range(16):
                aslot[li * 16 + i].used(thg)
        tok_hs = P.full("dHS")

        if stage <= 1:
            P.join()
            finish_out_dummy(P, y_out, x_sb, xtok)
            return nc, dict(taps=taps)

        kring = Ring([Slot(arena[:, 0:16 * 512]), Slot(arena[:, 16 * 512:32 * 512])])
        vring = Ring([Slot(arena[:, 32 * 512:36 * 512]),
                      Slot(arena[:, 36 * 512:40 * 512])])
        qring = Ring([Slot(qtl[:, i * 512:(i + 1) * 512]) for i in range(6)])
        for i, s in enumerate(kring.slots):
            s.qsem = f"dKR{i}"
        for i, s in enumerate(vring.slots):
            s.qsem = f"dVR{i}"
        for i, s in enumerate(qring.slots):
            s.qsem = f"dQR{i}"
        for i, s in enumerate(bounce.slots):
            s.qsem = f"dBN{i}"
        ptr = Ring([Slot(ptile[:, 0:512]), Slot(ptile[:, 512:1024])])

        def attn_mixer(blk, ztok):
            gbase = 0 if blk == 1 else 2
            kst_tok = {}
            for qv in range(2):
                iters = [("diag", qv)] + [("slot", s) for s in
                                          range(4 if qv == 0 else 8)]
                if blk == 2:
                    for s in (range(4) if qv == 0 else range(4, 8)):
                        gg, row = (gbase + (0 if s < 4 else 1)), s % 4
                        P.dma("S", kstatr[0:1, s * 16:(s + 1) * 16],
                              gout[gg][row * 128:row * 128 + 1,
                                       KW + VW:KW + VW + 16],
                              waits=[cc_tok[gg], tok_kstat0],
                              sem=f"dKS{qv}")
                    kst_tok[qv] = [P.full(f"dKS{qv}")]
                    tb1 = P.op("V", lambda e, qv=qv: e.tensor_tensor(
                        mrow[0:1, 0:128], kstatr[0:1, :],
                        blerow[0:1, 128 * qv:128 * (qv + 1)], ALU.mult),
                        waits=kst_tok.get(0, []) + kst_tok.get(1, []) + [tok_ble])
                    tmx = P.op("V", lambda e: e.tensor_copy(
                        mrow[0:1, 272:288], mrow[0:1, 0:16]), waits=[tb1])
                    for s in range(1, 8):
                        tmx = P.op("V", lambda e, s=s: e.tensor_tensor(
                            mrow[0:1, 272:288], mrow[0:1, 272:288],
                            mrow[0:1, s * 16:(s + 1) * 16], ALU.max), waits=[tmx])
                    maxk_tok = tmx
                    bkP = banks[6]
                    tP = P.op("T", lambda e, qv=qv: e.matmul(
                        bkP.ap[:, 0:8], ones1r[:, :],
                        metapad[0:1, 8 * qv:8 * (qv + 1)], start=True, stop=True),
                        waits=[tok_e2, tok_pad] + bkP.fill_waits())
                    tPc = P.op("V", lambda e: e.tensor_copy(
                        sigb[:, 0:8], bkP.ap[:, 0:8]), waits=[tP])
                    bkP.used(tPc)
                hgroups = ([list(range(0, 6)), list(range(6, 12)),
                            list(range(12, 16))] if blk == 1 else
                           [list(range(0, 5)), list(range(5, 10)),
                            list(range(10, 15)), [15]])
                for hg_i, hgrp in enumerate(hgroups):
                    ctx_banks = {h: banks[i] for i, h in enumerate(hgrp)}
                    lbank = banks[5] if blk == 2 else None
                    qslots = {}
                    bias_sl = {}
                    if blk == 2:
                        btoks = {}
                    for h in hgrp:
                        qs = qring.next()
                        tq = P.dma("S", qs.ap,
                                   qspill[qv][:, h * 512:(h + 1) * 512],
                                   waits=qs.fill_waits(), sem=qs.qsem)
                        qs.ready = tq
                        qslots[h] = qs
                        if blk == 2:
                            sq = bounce.next()
                            ts = P.op("A", lambda e, qs=qs, sq=sq: e.activation(
                                sq.ap, qs.ap, AF.Square, bias=0.0, scale=1.0),
                                waits=[tq] + sq.fill_waits())
                            sq.ready = ts
                            qs.used(ts)
                            bkq = banks[6]
                            tm = P.op("T", lambda e, sq=sq, bkq=bkq: e.matmul(
                                bkq.ap[0:1, :], onesst[:, 0:1], sq.ap,
                                start=True, stop=True),
                                waits=[ts, tok_ones] + bkq.fill_waits())
                            sq.used(tm)
                            t1 = P.op("V", lambda e, bkq=bkq: e.reduce_max(
                                st8f[0:1, 2081:2082], bkq.ap[0:1, :], axis=AX.X),
                                waits=[tm])
                            bkq.used(t1)
                            t2 = P.op("A", lambda e: e.activation(
                                st8f[0:1, 2081:2082], st8f[0:1, 2081:2082], AF.Sqrt, bias=0.0,
                                scale=1.0), waits=[t1])
                            t3 = P.op("V", lambda e: e.tensor_single_scalar(
                                st8f[0:1, 2081:2082], st8f[0:1, 2081:2082], -1.01, ALU.mult),
                                waits=[t2])
                            t4 = P.op("V", lambda e, h=h: e.tensor_tensor(
                                brow[0:1, h:h + 1], st8f[0:1, 2081:2082],
                                mrow[0:1, 272 + h:273 + h], ALU.mult),
                                waits=[t3, maxk_tok, tok_brow0] +
                                brow_sl.fill_waits())
                            btoks[h] = t4
                    if blk == 2:
                        bkB = banks[6]
                        bbs = bbase_slot
                        tB = P.op("T", lambda e: e.matmul(
                            bkB.ap[:, 0:16], ones1r[:, :], brow[0:1, :],
                            start=True, stop=True),
                            waits=list(btoks.values()) + [tok_e2] +
                            bkB.fill_waits())
                        brow_sl.used(tB)
                        tBc = P.op("V", lambda e: e.tensor_copy(
                            bbase[:, :], bkB.ap[:, 0:16]),
                            waits=[tB] + bbs.fill_waits())
                        bkB.used(tBc)
                        bbs.ready = tBc
                        tsh = P.op("V", lambda e: e.tensor_single_scalar(
                            brow[0:1, :], brow[0:1, :], 14.0, ALU.add),
                            waits=list(btoks.values()) + brow_sl.fill_waits())
                        btoks["shift"] = tsh
                        if "att2b" in taps and qv == 0 and hg_i == 0:
                            tb0 = P.op("V", lambda e: e.tensor_copy(
                                tapb[:, 0:16], bbase[:, :]),
                                waits=[tBc] + tap_sl.fill_waits())
                            tbd = P.dma("S", taps["att2b"][:, :], tapb[:, 0:16],
                                        waits=[tb0], sem="dT")
                            tap_sl.used(tbd)
                        if "att2k" in taps and qv == 0 and hg_i == 0:
                            tk0 = P.op("V", lambda e: e.tensor_copy(
                                tapb[0:1, 0:128], kstatr[0:1, :]),
                                waits=[maxk_tok] + tap_sl.fill_waits())
                            tkd = P.dma("S", taps["att2k"][:, :], tapb[0:1, 0:128],
                                        waits=[tk0], sem="dT")
                            tap_sl.used(tkd)
                        bas = biasall_slot
                        btk = None
                        for hi, h in enumerate(hgrp):
                            btk = P.op("V", lambda e, hi=hi, h=h: e.tensor_scalar(
                                basb[:, hi * 8:(hi + 1) * 8], sigb[:, 0:8], -1.0,
                                bbase[:, h:h + 1], ALU.mult, ALU.add),
                                waits=[tPc, tBc] +
                                (bas.fill_waits() if hi == 0 else []))
                        bas.ready = btk
                    first_mm = {h: True for h in hgrp}
                    lfirst = [True]
                    ltok = None
                    for it in iters:
                        if it[0] == "diag":
                            ksrc = gin[gbase + qv]
                            kwait = [cc_tok[gbase + qv]]
                        else:
                            s = it[1]
                            gg = gbase + (0 if s < 4 else 1)
                            row = s % 4
                            ksrc = gout[gg][row * 128:(row + 1) * 128, :]
                            kwait = [cc_tok[gg]]
                        ks = kring.next()
                        tk = P.dma("S", ks.ap, ksrc[:, 0:KW],
                                   waits=kwait + ks.fill_waits(), sem=ks.qsem)
                        ks.ready = tk
                        for kt in range(4):
                            v = vring.next()
                            tv = P.dma(
                                "S", v.ap,
                                ksrc[:, KW + kt * D:KW + (kt + 1) * D],
                                waits=kwait + v.fill_waits(), sem=v.qsem)
                            v.ready = tv
                            for h in hgrp:
                                hi = hgrp.index(h)
                                qs = qslots[h]
                                sbk = banks[6 + (kt % 2)]
                                tmm = P.op("T", lambda e, sbk=sbk, ks=ks, h=h,
                                           kt=kt, qs=qs: e.matmul(
                                               sbk.ap[:, :],
                                               ks.ap[:, h * 512 + kt * 128:
                                                     h * 512 + (kt + 1) * 128],
                                               qs.ap, start=True, stop=True),
                                           waits=[ks.ready, qs.ready] +
                                           sbk.fill_waits(), inc=True)
                                ks.used(tmm)
                                qs.used(tmm)
                                pt = ptr.next()
                                if blk == 1:
                                    rs = growcol(h, qv,
                                                 it[1] if it[0] == "slot" else -1,
                                                 kt)
                                    tev = P.op("A", lambda e, sbk=sbk, pt=pt,
                                               rs=rs: e.activation(
                                                   pt.ap, sbk.ap[:, :], AF.Copy,
                                                   bias=0.0, scale=rs),
                                               waits=[tmm, grow_tok[h]] +
                                               pt.fill_waits())
                                else:
                                    if it[0] == "diag":
                                        bias = bbase[:, h:h + 1]
                                        bwait = [bbase_slot.ready]
                                        bsl = bbase_slot
                                    else:
                                        bias = basb[:, hi * 8 + it[1]:
                                                    hi * 8 + it[1] + 1]
                                        bwait = [biasall_slot.ready]
                                        bsl = biasall_slot
                                    tev = P.op("A", lambda e, sbk=sbk, pt=pt,
                                               bias=bias: e.activation(
                                                   pt.ap, sbk.ap[:, :], AF.Exp,
                                                   bias=bias, scale=1.0),
                                               waits=[tmm] + bwait +
                                               pt.fill_waits())
                                    bsl.used(tev)
                                sbk.used(tev)
                                pt.ready = tev
                                if it[0] == "diag":
                                    tev = P.op("V", lambda e, pt=pt, kt=kt:
                                               e.tensor_tensor(
                                                   pt.ap, pt.ap, trim(kt),
                                                   ALU.mult),
                                               waits=[tev, tok_tri[kt]])
                                    pt.ready = tev
                                if blk == 2:
                                    llast = (it is iters[-1]) and kt == 3 and \
                                        h == hgrp[-1]
                                    tl = P.op("T", lambda e, pt=pt, h=h,
                                              lf=lfirst[0], ll=llast: e.matmul(
                                                  lbank.ap[0:16, :],
                                                  lind[:, 16 * h:16 * (h + 1)],
                                                  pt.ap, start=lf, stop=ll),
                                              waits=[pt.ready, tok_lind] +
                                              (lbank.fill_waits() if lfirst[0]
                                               else []), inc=True)
                                    lfirst[0] = False
                                    pt.used(tl)
                                    if llast:
                                        ltok = tl
                                cbk = ctx_banks[h]
                                clast = (it is iters[-1]) and kt == 3
                                tc = P.op("T", lambda e, cbk=cbk, pt=pt, h=h,
                                          kt=kt, v=v, cf=first_mm[h],
                                          cl=clast: e.matmul(
                                              cbk.ap[:, :],
                                              v.ap[:, h * 128:(h + 1) * 128],
                                              pt.ap, start=cf, stop=cl),
                                          waits=[pt.ready, v.ready] +
                                          (cbk.fill_waits() if first_mm[h]
                                           else []), inc=True)
                                first_mm[h] = False
                                pt.used(tc)
                                v.used(tc)
                                if clast:
                                    ctx_banks[h] = (cbk, tc)
 
                    if blk == 2:
                        if "att2l" in taps and qv == 0 and hg_i == 0:
                            tl0 = P.op("V", lambda e: e.tensor_copy(
                                tapb[0:16, :], lbank.ap[0:16, :]),
                                waits=[ltok] + tap_sl.fill_waits())
                            tld = P.dma("S", taps["att2l"][:, :], tapb[0:16, :],
                                        waits=[tl0], sem="dT")
                            tap_sl.used(tld)
                            lbank.used(tl0)
                        trc = P.op("V", lambda e: e.tensor_single_scalar(
                            st16[:, :], lbank.ap[0:16, :], 1e-11, ALU.add),
                            waits=[ltok] + st16_sl.fill_waits())
                        lbank.used(trc)
                        trc = P.op("V", lambda e: e.reciprocal(
                            st16[:, :], st16[:, :]), waits=[trc])
                    else:
                        trc = None
                    for h in hgrp:
                        cbk, ctok = ctx_banks[h]
                        if blk == 1:
                            fin_retention(h, qv, cbk, ctok, ztok)
                        else:
                            fin_softmax(h, qv, cbk, ctok, trc, ztok)

        def fin_retention(h, qv, cbk, ctok, ztok):
            t0 = P.op("V", lambda e, cbk=cbk: e.tensor_copy(fs1[:, :], cbk.ap[:, :]),
                      waits=[ctok])
            cbk.used(t0)
            tg = P.op("T", lambda e, h=h, cbk=cbk: e.matmul(
                cbk.ap[:, :], e16[:, 128 * h:128 * (h + 1)], gcols[0:16, :],
                start=True, stop=True), waits=[t0, tok_e16, tok_gcols])
            t1 = P.op("V", lambda e, cbk=cbk: e.tensor_tensor(
                fs1[:, :], fs1[:, :], cbk.ap[:, :], ALU.mult), waits=[tg])
            cbk.used(t1)
            b1, b2 = bounce.next(), bounce.next()
            tb = P.op("V", lambda e, b1=b1: e.tensor_copy(b1.ap, fs1[:, :]),
                      waits=[t1] + b1.fill_waits())
            b1.ready = tb
            tsq = P.op("A", lambda e, b2=b2: e.activation(
                b2.ap, fs1[:, :], AF.Square, bias=0.0, scale=1.0),
                waits=[t1] + b2.fill_waits())
            b2.ready = tsq
            tm1 = P.op("T", lambda e, b1=b1, cbk=cbk: e.matmul(
                cbk.ap[0:1, :], onesst[:, 0:1], b1.ap, start=True, stop=True),
                waits=[tb, tok_ones, t1], inc=True)
            b1.used(tm1)
            t2 = P.op("V", lambda e, cbk=cbk: e.tensor_single_scalar(
                stc(0), cbk.ap[0:1, :], -1.0 / DHD, ALU.mult),
                waits=[tm1] + st8_sl.fill_waits())
            cbk.used(t2)
            tm2 = P.op("T", lambda e, b2=b2, cbk=cbk: e.matmul(
                cbk.ap[0:1, :], onesst[:, 0:1], b2.ap, start=True, stop=True),
                waits=[tsq, t2], inc=True)
            b2.used(tm2)
            t3 = P.op("V", lambda e, cbk=cbk: e.tensor_scalar(
                stc(2), cbk.ap[0:1, :], 1.0 / DHD, EPS1, ALU.mult, ALU.add),
                waits=[tm2])
            cbk.used(t3)
            t4 = P.op("V", lambda e: e.tensor_tensor(
                stc(3), stc(0), stc(0), ALU.mult), waits=[t2])
            t5 = P.op("V", lambda e: e.tensor_tensor(
                stc(2), stc(2), stc(3), ALU.subtract), waits=[t3, t4])
            t6 = P.op("A", lambda e: e.activation(
                stc(2), stc(2), AF.Sqrt, bias=0.0, scale=1.0), waits=[t5])
            t7 = P.op("V", lambda e: e.reciprocal(stc(1), stc(2)), waits=[t6])
            tM = P.op("T", lambda e, cbk=cbk: e.matmul(
                cbk.ap[:, :], ones1r[:, :], stc(0), start=True, stop=True),
                waits=[t2, t7, tok_e2])
            st8_sl.used(tM)
            t8 = P.op("V", lambda e, cbk=cbk: e.tensor_tensor(
                fs1[:, :], fs1[:, :], cbk.ap[:, :], ALU.add), waits=[tM])
            cbk.used(t8)
            tI = P.op("T", lambda e, cbk=cbk: e.matmul(
                cbk.ap[:, :], ones1r[:, :], stc(1), start=True, stop=True),
                waits=[t8, tok_e2])
            st8_sl.used(tI)
            hgs = bounce.next()
            thg = P.dma("S", hgs.ap, hgspill[qv][:, h * 512:(h + 1) * 512],
                        waits=hgs.fill_waits() + [tok_hs], sem=hgs.qsem)
            hgs.ready = thg
            t9 = P.op("V", lambda e, cbk=cbk: e.tensor_tensor(
                fs1[:, :], fs1[:, :], cbk.ap[:, :], ALU.mult), waits=[tI])
            cbk.used(t9)
            t10 = P.op("V", lambda e, h=h, qv=qv, hgs=hgs: e.tensor_tensor(
                ht(h, qv), fs1[:, :], hgs.ap, ALU.mult), waits=[t9, thg])
            hgs.used(t10)
            ztok[h][qv] = t10
            tap_fm("retz", h, qv, ht(h, qv), t10)

        def fin_softmax(h, qv, cbk, ctok, trc, ztok):
            rb = banks[7] if cbk is not banks[7] else banks[6]
            tR = P.op("T", lambda e, h=h, rb=rb: e.matmul(
                rb.ap[:, :], e16[:, 128 * h:128 * (h + 1)], st16[0:16, :],
                start=True, stop=True),
                waits=[trc, tok_e16] + rb.fill_waits())
            st16_sl.used(tR)
            t0 = P.op("V", lambda e, cbk=cbk: e.tensor_copy(fs1[:, :], cbk.ap[:, :]),
                      waits=[ctok])
            cbk.used(t0)
            t1 = P.op("V", lambda e, rb=rb: e.tensor_tensor(
                fs1[:, :], fs1[:, :], rb.ap[:, :], ALU.mult), waits=[t0, tR])
            rb.used(t1)
            t2 = P.op("V", lambda e, h=h, qv=qv: e.tensor_scalar(
                ht(h, qv), fs1[:, :], vecs["bv2"][:, h:h + 1], None, ALU.add),
                waits=[t1, tok_vecs])
            ztok[h][qv] = t2
            tap_fm("ctx2", h, qv, ht(h, qv), t2)

        P.join()
        ztok = [[None, None] for _ in range(NH)]
        attn_mixer(1, ztok)

        def woevac1(li, oc, bk, mmtok, ow=128):
            t = P.op("V", lambda e, oc=oc, li=li, bk=bk: e.tensor_tensor(
                xt(oc, li), xt(oc, li), bk.ap[:, :], ALU.add),
                waits=[mmtok, xtok[oc][li]])
            bk.used(t)
            xtok[oc][li] = t
            tap_fm("xwo", oc, li, xt(oc, li), t)

        proj_pass(dt["a_wo"], D, D,
                  [(lambda kb: ht(kb, 0), lambda kb: ztok[kb][0]),
                   (lambda kb: ht(kb, 1), lambda kb: ztok[kb][1])],
                  woevac1, g_oc=2)

        if stage <= 2:
            P.join()
            finish_out_dummy(P, y_out, x_sb, xtok)
            return nc, dict(taps=taps)

        P.join()
        h2tok = [[None, None] for _ in range(NDB)]
        xmid_tok = [[None, None] for _ in range(NDB)]
        for u in range(2):
            rt = ln_stats(lambda db, u=u: xt(db, u), lambda db, u=u: xtok[db][u],
                          rms=False, n=D, eps=EPS1)
            tM, tI = ln_bcast(rt)
            toks = ln_apply(lambda db, u=u: ht(db, u), lambda db, u=u: xt(db, u),
                            lambda db, u=u: xtok[db][u], vecs["n2w"], vecs["n2b"],
                            tM, tI, extra=[tok_vecs])
            for db in range(NDB):
                h2tok[db][u] = toks[db]
                tap_fm("h2", db, u, ht(db, u), toks[db])
            utok = [None] * NKB1
            usl = [aslot[i] for i in range(NKB1)]

            def w1evac(li, oc, bk, mmtok, ow=128):
                s = usl[oc]
                t = P.op("A", lambda e, bk=bk, s=s, ow=ow: e.activation(
                    s.ap[0:ow, :], bk.ap[0:ow, :], AF.Silu, bias=0.0, scale=1.0),
                    waits=[mmtok] + s.fill_waits())
                bk.used(t)
                s.ready = t

            mv = [(lambda kb, u=u: ht(kb, u), lambda kb, u=u: h2tok[kb][u])]
            proj_pass(dt["a_w1"], D, HID1, mv, w1evac, g_oc=4)

            def w2evac(li, oc, bk, mmtok, ow=128):
                s = usl[oc]
                t = P.op("V", lambda e, bk=bk, s=s, ow=ow: e.tensor_tensor(
                    s.ap[0:ow, :], s.ap[0:ow, :], bk.ap[0:ow, :], ALU.mult),
                    waits=[mmtok, s.ready])
                bk.used(t)
                s.ready = t
                utok[oc] = t

            proj_pass(dt["a_w2"], D, HID1, mv, w2evac, g_oc=4)

            def w3evac(li, oc, bk, mmtok, ow=128):
                t = P.op("V", lambda e, oc=oc, u=u, bk=bk: e.tensor_tensor(
                    xt(oc, u), xt(oc, u), bk.ap[:, :], ALU.add),
                    waits=[mmtok, xtok[oc][u]])
                bk.used(t)
                xtok[oc][u] = t

            proj_pass(dt["a_w3"], HID1, D,
                      [(lambda kb: usl[kb].ap, lambda kb: utok[kb])],
                      w3evac, g_oc=4)
            for kb in range(NKB1):
                usl[kb].used(utok[kb])

            for db in range(NDB):
                sl = fsl2 if db % 2 == 0 else fsl3
                tld = P.dma("A", sl.ap[:, :],
                            dt["x_fm"][db, :, u * CH:(u + 1) * CH],
                            waits=sl.fill_waits() + [P.full("cV"), P.full("cA")],
                            sem=sl.qsem)
                t = P.op("V", lambda e, db=db, u=u, sl=sl: e.tensor_tensor(
                    xt(db, u), xt(db, u), sl.ap[:, :], ALU.add),
                    waits=[tld, xtok[db][u]])
                sl.used(t)
                xtok[db][u] = t
            rt = ln_stats(lambda db, u=u: xt(db, u), lambda db, u=u: xtok[db][u],
                          rms=False, n=D, eps=EPS1)
            tM, tI = ln_bcast(rt)
            toks = ln_apply(lambda db, u=u: xt(db, u), lambda db, u=u: xt(db, u),
                            lambda db, u=u: xtok[db][u], None, None, tM, tI)
            for db in range(NDB):
                xtok[db][u] = toks[db]
                tap_fm("xmid", db, u, xt(db, u), toks[db])
                xmid_tok[db][u] = P.dma(
                    "S", xmidspill[db, :, u * CH:(u + 1) * CH], xt(db, u),
                    waits=[toks[db]], sem=f"dXS{u}")
                xtok[db][u] = xmid_tok[db][u]

            rt = ln_stats(lambda db, u=u: xt(db, u), lambda db, u=u: xtok[db][u],
                          rms=True, n=D, eps=EPS2)
            tM, tI = ln_bcast(rt)
            toks = rms_apply(lambda db, u=u: ht(db, u), lambda db, u=u: xt(db, u),
                             lambda db, u=u: xtok[db][u], vecs["lnf1"], tI)
            for db in range(NDB):
                htok[db][u] = toks[db]

            def qevac2(li, oc, bk, mmtok, ow=128, u=u):
                s = aslot[oc]
                t0 = P.op("V", lambda e, bk=bk, oc=oc: e.tensor_scalar(
                    fs1[:, :], bk.ap[:, :], 1.0, vecs["bq2"][:, oc:oc + 1],
                    ALU.mult, ALU.add), waits=[mmtok, tok_vecs])
                t = rope_apply(u, fs1, s, [t0] + s.fill_waits())
                bk.used(t)
                s.ready = t
                tap_fm("q2", oc, u, s.ap, t, srcslot=s)

            proj_pass(dt["f_wq"], D, D,
                      [(lambda kb, u=u: ht(kb, u), lambda kb, u=u: htok[kb][u])],
                      qevac2, g_oc=3)
            rd = [aslot[i].ready for i in range(16)]
            tq2 = P.dma("S", qspill[u], arena[:, 0:16 * 512], waits=rd,
                        sem="dQS2")
            for i in range(16):
                aslot[i].used(tq2)
            kv_wave(u, 2)

        if stage <= 3:
            P.join()
            finish_out_dummy(P, y_out, x_sb, xtok)
            return nc, dict(taps=taps)

        P.join()
        ztok2 = [[None, None] for _ in range(NH)]
        attn_mixer(2, ztok2)

        def woevac2(li, oc, bk, mmtok, ow=128):
            t = P.op("V", lambda e, oc=oc, li=li, bk=bk: e.tensor_tensor(
                xt(oc, li), xt(oc, li), bk.ap[:, :], ALU.add),
                waits=[mmtok, xtok[oc][li]])
            bk.used(t)
            xtok[oc][li] = t
            tap_fm("xwo2", oc, li, xt(oc, li), t)

        proj_pass(dt["f_wo"], D, D,
                  [(lambda kb: ht(kb, 0), lambda kb: ztok2[kb][0]),
                   (lambda kb: ht(kb, 1), lambda kb: ztok2[kb][1])],
                  woevac2, g_oc=2)

        P.join()
        for u in range(2):
            rt = ln_stats(lambda db, u=u: xt(db, u), lambda db, u=u: xtok[db][u],
                          rms=True, n=D, eps=EPS2)
            tM, tI = ln_bcast(rt)
            toks = rms_apply(lambda db, u=u: ht(db, u), lambda db, u=u: xt(db, u),
                             lambda db, u=u: xtok[db][u], vecs["lnf2"], tI)
            for db in range(NDB):
                h2tok[db][u] = toks[db]
                tap_fm("h2r", db, u, ht(db, u), toks[db])

        cmb_tok = [None, None]
        for u in range(2):
            ttoks = []
            for tt in range(4):
                bk = banks[tt % 2]
                first = True
                for kb in range(NDB):
                    last = kb == NDB - 1
                    tmm = P.op("T", lambda e, bk=bk, kb=kb, tt=tt, u=u,
                               first=first, last=last: e.matmul(
                                   bk.ap[0:128, 0:8],
                                   h_sb[:, kb * 2 * CH + u * CH + tt * 128:
                                        kb * 2 * CH + u * CH + (tt + 1) * 128],
                                   router_sb[:, kb * 8:(kb + 1) * 8],
                                   start=first, stop=last),
                               waits=[h2tok[kb][u], rtok[kb]] +
                               (bk.fill_waits() if first else []), inc=True)
                    first = False
                t1 = P.op("V", lambda e, bk=bk: e.reduce_max(
                    fs1[:, 0:1], bk.ap[:, 0:8], axis=AX.X,
                    negate=True), waits=[tmm])
                bk.used(t1)
                t2 = P.op("A", lambda e, bk=bk: e.activation(
                    fs2[:, 0:8], bk.ap[:, 0:8], AF.Exp, bias=fs1[:, 0:1],
                    scale=1.0, accum_out=fs1[:, 1:2]), waits=[t1])
                bk.used(t2)
                t3 = P.op("V", lambda e: e.reciprocal(
                    fs1[:, 2:3], fs1[:, 1:2]), waits=[t2])
                t4 = P.op("V", lambda e: e.tensor_scalar(
                    fs2[:, 0:8], fs2[:, 0:8], fs1[:, 2:3], None, ALU.mult),
                    waits=[t3])
                t5 = P.op("V", lambda e: e.reduce_max(
                    fs1[:, 3:4], fs2[:, 0:8], axis=AX.X, negate=True), waits=[t4])
                t6 = P.op("V", lambda e: e.tensor_scalar(
                    fs3[:, 0:8], fs2[:, 0:8], fs1[:, 3:4], None, ALU.add),
                    waits=[t5])
                t7 = P.op("V", lambda e: e.tensor_single_scalar(
                    fs3[:, 0:8], fs3[:, 0:8], 0.0, ALU.is_equal), waits=[t6])
                t8 = P.op("V", lambda e: e.scalar_tensor_tensor(
                    fs3[:, 0:8], fs3[:, 0:8], -2.0, fs2[:, 0:8],
                    ALU.mult, ALU.add), waits=[t7])
                t9 = P.op("V", lambda e: e.reduce_max(
                    fs1[:, 4:5], fs3[:, 0:8], axis=AX.X, negate=True), waits=[t8])
                t10 = P.op("V", lambda e: e.tensor_scalar(
                    fs3[:, 0:8], fs2[:, 0:8], fs1[:, 4:5], None, ALU.add),
                    waits=[t9])
                t11 = P.op("V", lambda e: e.tensor_single_scalar(
                    fs3[:, 0:8], fs3[:, 0:8], 0.0, ALU.is_ge), waits=[t10])
                t12 = P.op("V", lambda e: e.tensor_tensor(
                    fs1[:, 8:16], fs2[:, 0:8], fs3[:, 0:8], ALU.mult),
                    waits=[t11])
                bk6 = banks[6]
                tt_ = P.op("T", lambda e, tt=tt, bk6=bk6: e.transpose(
                    bk6.ap[0:8, tt * 128:(tt + 1) * 128], fs1[:, 8:16],
                    ident[:, :]),
                    waits=[t12, tok_id] + (bk6.fill_waits() if tt == 0 else []))
                ttoks.append(tt_)
            tc = P.op("V", lambda e, u=u: e.tensor_copy(
                cmbT[0:8, u * 512:(u + 1) * 512], banks[6].ap[0:8, :]),
                waits=ttoks)
            banks[6].used(tc)
            cmb_tok[u] = tc
            if "comb" in taps:
                P.dma("S", taps["comb"][0:8, u * 512:(u + 1) * 512],
                      cmbT[0:8, u * 512:(u + 1) * 512], waits=[tc], sem="dT")

        for ei in range(NE):
            ztoks = [[None, None] for _ in range(MI // 128)]
            zsl = [[aslot[oc], aslot[11 + oc]] for oc in range(MI // 128)]
            cb = [None, None]
            for li in range(2):
                cbb = banks[3] if li == 0 else banks[7]
                tcb = P.op("T", lambda e, ei=ei, li=li, cbb=cbb: e.matmul(
                    cbb.ap[:, :], e16[0:8, 128 * ei:128 * ei + 128],
                    cmbT[0:8, li * 512:(li + 1) * 512], start=True, stop=True),
                    waits=[cmb_tok[li], tok_e16] + cbb.fill_waits())
                cb[li] = (cbb, tcb)

            def gevac_e(li, oc, bk, mmtok, ow=128, ei=ei):
                s = zsl[oc][li]
                t = P.op("A", lambda e, bk=bk, s=s: e.activation(
                    s.ap, bk.ap[:, :], AF.Silu, bias=0.0, scale=1.0),
                    waits=[mmtok] + s.fill_waits())
                bk.used(t)
                s.ready = t

            mv2 = [(lambda kb: ht(kb, 0), lambda kb: h2tok[kb][0]),
                   (lambda kb: ht(kb, 1), lambda kb: h2tok[kb][1])]
            proj_pass(dt["f_wge"][ei], D, MI, mv2, gevac_e, g_oc=1)

            def uevac_e(li, oc, bk, mmtok, ow=128, ei=ei):
                s = zsl[oc][li]
                cbb, ctk = cb[li]
                t0 = P.op("V", lambda e, bk=bk, s=s: e.tensor_tensor(
                    fs1[:, :], s.ap, bk.ap[:, :], ALU.mult),
                    waits=[mmtok, s.ready])
                bk.used(t0)
                t1 = P.op("V", lambda e, s=s, cbb=cbb: e.tensor_tensor(
                    s.ap, fs1[:, :], cbb.ap[:, :], ALU.mult), waits=[t0, ctk])
                cbb.used(t1)
                s.ready = t1
                ztoks[oc][li] = t1

            proj_pass(dt["f_wue"][ei], D, MI, mv2, uevac_e, g_oc=1)

            def devac_e(li, oc, bk, mmtok, ow=128):
                t = P.op("V", lambda e, oc=oc, li=li, bk=bk: e.tensor_tensor(
                    xt(oc, li), xt(oc, li), bk.ap[:, :], ALU.add),
                    waits=[mmtok, xtok[oc][li]])
                bk.used(t)
                xtok[oc][li] = t

            proj_pass(dt["f_wde"][ei], MI, D,
                      [(lambda kb: zsl[kb][0].ap, lambda kb: ztoks[kb][0]),
                       (lambda kb: zsl[kb][1].ap, lambda kb: ztoks[kb][1])],
                      devac_e, g_oc=2)
            for oc in range(MI // 128):
                for li in range(2):
                    zsl[oc][li].used(ztoks[oc][li])

        for u in range(2):
            ssl = [aslot[i] for i in range(44)]
            stoks = [None] * 44

            def shg_evac(li, oc, bk, mmtok, ow=128):
                s = ssl[oc]
                t = P.op("A", lambda e, bk=bk, s=s: e.activation(
                    s.ap, bk.ap[:, :], AF.Silu, bias=0.0, scale=1.0),
                    waits=[mmtok] + s.fill_waits())
                bk.used(t)
                s.ready = t

            mv = [(lambda kb, u=u: ht(kb, u), lambda kb, u=u: h2tok[kb][u])]
            proj_pass(dt["f_shg"], D, SI, mv, shg_evac, g_oc=4)

            def shu_evac(li, oc, bk, mmtok, ow=128):
                s = ssl[oc]
                t = P.op("V", lambda e, bk=bk, s=s: e.tensor_tensor(
                    s.ap, s.ap, bk.ap[:, :], ALU.mult), waits=[mmtok, s.ready])
                bk.used(t)
                s.ready = t
                stoks[oc] = t

            proj_pass(dt["f_shu"], D, SI, mv, shu_evac, g_oc=4)

            bk3 = banks[3]
            first = True
            for kb in range(NDB):
                slot = wring.next()
                wt = P.dma("G", slot.ap[0:128, 0:1],
                           dt["f_sheg"][kb * 128:(kb + 1) * 128, :],
                           waits=slot.fill_waits(), sem=slot.qsem)
                slot.ready = wt
                last = kb == NDB - 1
                tmm = P.op("T", lambda e, slot=slot, kb=kb, u=u, first=first,
                           last=last, bk3=bk3: e.matmul(
                               bk3.ap[0:1, :], slot.ap[0:128, 0:1],
                               ht(kb, u), start=first, stop=last),
                           waits=[wt, h2tok[kb][u]] +
                           (bk3.fill_waits() if first else []), inc=True)
                slot.used(tmm)
                first = False
            tsg = P.op("A", lambda e, u=u, bk3=bk3: e.activation(
                sigr[0:1, u * 512:(u + 1) * 512], bk3.ap[0:1, :], AF.Sigmoid,
                bias=0.0, scale=1.0),
                waits=[tmm, tok_sig0] + sigr_sl.fill_waits())
            bk3.used(tsg)
            bkS = banks[3]
            tsb = P.op("T", lambda e, u=u, bkS=bkS: e.matmul(
                bkS.ap[:, :], ones1r[:, :], sigr[0:1, u * 512:(u + 1) * 512],
                start=True, stop=True),
                waits=[tsg, tok_e2] + bkS.fill_waits())
            sigr_sl.used(tsb)
            tsb2 = P.op("V", lambda e, bkS=bkS: e.tensor_copy(
                sigb[:, :], bkS.ap[:, :]), waits=[tsb])
            bkS.used(tsb2)

            def shd_evac(li, oc, bk, mmtok, ow=128, u=u, tsb2=tsb2):
                t0 = P.op("V", lambda e, bk=bk: e.tensor_tensor(
                    fs1[:, :], sigb[:, :], bk.ap[:, :], ALU.mult),
                    waits=[mmtok, tsb2])
                bk.used(t0)
                t1 = P.op("V", lambda e, oc=oc, u=u: e.tensor_tensor(
                    xt(oc, u), xt(oc, u), fs1[:, :], ALU.add),
                    waits=[t0, xtok[oc][u]])
                xtok[oc][u] = t1
                tap_fm("moe", oc, u, xt(oc, u), t1)

            proj_pass(dt["f_shd"], SI, D,
                      [(lambda kb: ssl[kb].ap, lambda kb: stoks[kb])],
                      shd_evac, g_oc=4)
            for kb in range(44):
                ssl[kb].used(stoks[kb])

        P.join()
        out_dma = []
        for u in range(2):
            for db in range(NDB):
                sl = fsl2 if db % 2 == 0 else fsl3
                tld = P.dma("A", sl.ap[:, :],
                            xmidspill[db, :, u * CH:(u + 1) * CH],
                            waits=[xmid_tok[db][u]] + sl.fill_waits(),
                            sem=sl.qsem)
                t = P.op("V", lambda e, db=db, u=u, sl=sl: e.tensor_tensor(
                    xt(db, u), xt(db, u), sl.ap[:, :], ALU.add),
                    waits=[tld, xtok[db][u]])
                sl.used(t)
                xtok[db][u] = t
            rt = ln_stats(lambda db, u=u: xt(db, u), lambda db, u=u: xtok[db][u],
                          rms=False, n=D, eps=EPS1)
            tM, tI = ln_bcast(rt)
            toks = ln_apply(lambda db, u=u: xt(db, u), lambda db, u=u: xt(db, u),
                            lambda db, u=u: xtok[db][u], None, None, tM, tI)
            for db in range(NDB):
                xtok[db][u] = toks[db]
                out_dma.append(P.dma("A", y_out[db, :, u * CH:(u + 1) * CH],
                                     xt(db, u), waits=[toks[db]], sem="dY"))

        P.emit("S", lambda e: e.nop(),
               waits=[P.full(nm) for nm in list(P.cnt)])

    return nc, dict(taps=taps)


def finish_out_dummy(P, y_out, x_sb, xtok):
    for u in range(2):
        for db in range(16):
            P.dma("A", y_out[db, :, u * CH:(u + 1) * CH],
                  x_sb[:, db * 2 * CH + u * CH: db * 2 * CH + (u + 1) * CH],
                  waits=[xtok[db][u]], sem="dY")
    P.emit("S", lambda e: e.nop(), waits=[P.full(nm) for nm in list(P.cnt)])


def _core_meta(r):
    cA, cB = r, 7 - r
    chunks = [cA, cB]
    pos = np.zeros((1, 2 * CH), np.float32)
    for qv in range(2):
        pos[0, qv * CH:(qv + 1) * CH] = (np.arange(CH, dtype=np.float32)
                                         + 512 * chunks[qv])

    def chunk_at_slot(s):
        rho, w = s % 4, s // 4
        return rho if w == 0 else 7 - rho

    rs = np.zeros((1, 2 * NMC), np.float32)
    for qv in range(2):
        qc = chunks[qv]
        for s in range(8):
            kc = chunk_at_slot(s)
            for kt in range(4):
                c = (qv * 8 + s) * 4 + kt
                if kc < qc:
                    rs[0, c] = 512 * qc - (512 * kc + 128 * kt)
                    rs[0, NMC + c] = 1.0
    for kt in range(4):
        rs[0, 64 + kt] = -128.0 * kt
        rs[0, NMC + 64 + kt] = 1.0

    pad = np.full((1, 16), 1000.0, np.float32)
    ble = np.zeros((1, 256), np.float32)
    for qv in range(2):
        qc = chunks[qv]
        for s in range(8):
            kc = chunk_at_slot(s)
            if kc < qc:
                pad[0, qv * 8 + s] = 0.0
            for h in range(NH):
                ble[0, qv * 128 + s * 16 + h] = 1.0 if kc <= qc else 0.0
    return pos, rs, pad, ble


def _vec(a):
    return np.ascontiguousarray(np.asarray(a, np.float32).reshape(NDB, 128).T)


def kernel(**inputs):
    nc, meta = build_program()
    wmap = {
        "a_wq": "L0_attn_wq", "a_wk": "L0_attn_wk", "a_wv": "L0_attn_wv",
        "a_wg": "L0_attn_wg", "a_wo": "L0_attn_wo", "a_w1": "L0_attn_w1",
        "a_w2": "L0_attn_w2", "a_w3": "L0_attn_w3",
        "f_wq": "L0_ffn_wq", "f_wk": "L0_ffn_wk", "f_wv": "L0_ffn_wv",
        "f_wo": "L0_ffn_wo", "f_router": "L0_ffn_w_router",
        "f_wge": "L0_ffn_w_gate_e", "f_wue": "L0_ffn_w_up_e",
        "f_wde": "L0_ffn_w_down_e", "f_shg": "L0_ffn_w_sh_gate",
        "f_shu": "L0_ffn_w_sh_up", "f_shd": "L0_ffn_w_sh_down",
    }
    vmap = {
        "n1w": "L0_attn_norm1_w", "n1b": "L0_attn_norm1_b",
        "n2w": "L0_attn_norm2_w", "n2b": "L0_attn_norm2_b",
        "lnf1": "L0_ffn_w_ln1", "lnf2": "L0_ffn_w_ln2",
        "bq2": "L0_ffn_bq", "bk2": "L0_ffn_bk", "bv2": "L0_ffn_bv",
    }
    shared = {k: np.ascontiguousarray(np.asarray(inputs[v], np.float32))
              for k, v in wmap.items()}
    shared["f_sheg"] = np.ascontiguousarray(
        np.asarray(inputs["L0_ffn_w_sh_eg"], np.float32).reshape(D, 1))
    for k, v in vmap.items():
        shared[k] = _vec(inputs[v])
    X = np.asarray(inputs["L0_attn_x"], np.float32)

    in_maps = []
    for core in range(8):
        b, r = core // 4, core % 4
        cA, cB = r, 7 - r
        xa = X[b, 512 * cA:512 * cA + 512]
        xb = X[b, 512 * cB:512 * cB + 512]
        xf = np.concatenate([xa, xb], 0).T.reshape(NDB, 128, 2 * CH)
        pos, rs, pad, ble = _core_meta(r)
        m = dict(shared)
        m["x_fm"] = np.ascontiguousarray(xf)
        m["pos"] = pos
        m["rsmeta"] = rs
        m["padmeta"] = pad
        m["blerow"] = ble
        in_maps.append(m)

    res = run_bass_kernel_spmd(nc, in_maps, core_ids=list(range(8)))

    out = np.empty((2, 4096, D), np.float32)
    for core in range(8):
        b, r = core // 4, core % 4
        cA, cB = r, 7 - r
        y = np.asarray(res.results[core]["y"])
        yt = y.reshape(D, 2 * CH).T
        out[b, 512 * cA:512 * cA + 512] = yt[0:CH]
        out[b, 512 * cB:512 * cB + 512] = yt[CH:2 * CH]
    return out


# version 03
# speedup vs baseline: 1.2036x; 1.2036x over previous
import math
import numpy as np
import concourse.bass as bass
import concourse.mybir as mybir
from concourse.bass_utils import run_bass_kernel_spmd

F32 = mybir.dt.float32
BF16 = mybir.dt.bfloat16
AF = mybir.ActivationFunctionType
ALU = mybir.AluOpType
AX = mybir.AxisListType

D = 2048
NDB = 16
CH = 512
NH = 16
DHD = 128
HID1 = 5461
NKB1 = 43
NE = 8
MI = 1408
SI = 5632
EPS1 = 1e-5
EPS2 = 1e-6
CINV = DHD ** -0.5
LN2 = math.log(2.0)
LN1E6 = math.log(1.0e6)
TWO_PI = 2.0 * math.pi
GAMMA = [1.0 - 2.0 ** (-5.0 - h) for h in range(NH)]
LNG = [math.log(g) for g in GAMMA]

KW = NDB * CH
VW = 4 * D
SW = CH
GW = KW + VW + SW
NMC = 68


def ceil_div(a, b):
    return (a + b - 1) // b


class Prog:
    def __init__(self, nc, block):
        self.nc = nc
        self.block = block
        self.sec = {"T": block.tensor, "V": block.vector, "A": block.scalar,
                    "G": block.gpsimd, "S": block.sync}
        self.sem = {}
        self.cnt = {}
        for nm in ("cT", "cV", "cA", "cG", "dS", "dA", "dG", "cc"):
            self.sem[nm] = nc.alloc_semaphore(nm)
            self.cnt[nm] = 0
        self.seen = {e: {} for e in self.sec}

    def emit(self, eng, fn, waits=(), inc=None, inc_by=1):
        need = []
        seen = self.seen[eng]
        for t in waits:
            if t is None:
                continue
            nm, val = t
            if seen.get(nm, 0) >= val:
                continue
            seen[nm] = val
            need.append((nm, val))
        tok = None
        if inc is not None:
            self.cnt[inc] += inc_by
            tok = (inc, self.cnt[inc])
        sems = self.sem

        def run(e):
            for nm, val in need:
                e.wait_ge(sems[nm], val)
            inst = fn(e)
            if inc is not None:
                inst.then_inc(sems[inc], inc_by)
        self.sec[eng](run)
        return tok

    ESEM = {"T": "cT", "V": "cV", "A": "cA", "G": "cG"}
    DSEM = {"S": "dS", "A": "dA", "G": "dG"}

    def op(self, eng, fn, waits=(), inc=True):
        return self.emit(eng, fn, waits, inc=self.ESEM[eng] if inc else None)

    def getsem(self, nm):
        if nm not in self.sem:
            self.sem[nm] = self.nc.alloc_semaphore(nm)
            self.cnt[nm] = 0
        return nm

    def dma(self, eng, out, in_, waits=(), sem=None):
        if sem is not None:
            sem = self.getsem(sem)
        return self.emit(eng, lambda e: e.dma_start(out=out, in_=in_),
                         waits, inc=sem or self.DSEM[eng], inc_by=16)

    def full(self, nm):
        return (nm, self.cnt.get(nm, 0))

    def join(self):
        toks = [(nm, self.cnt[nm]) for nm in self.cnt]
        for eng in ("T", "V", "A", "G", "S"):
            self.emit(eng, lambda e: e.nop(), waits=toks)


class Slot:
    def __init__(self, ap):
        self.ap = ap
        self.ready = None
        self.users = []
        self.qsem = None

    def fill_waits(self):
        w = list(self.users)
        self.users = []
        return w

    def used(self, tok):
        if tok is not None:
            self.users.append(tok)


class Ring:
    def __init__(self, slots):
        self.slots = slots
        self.i = 0

    def next(self):
        s = self.slots[self.i % len(self.slots)]
        self.i += 1
        return s


def build_program(stage=99, debug=()):
    nc = bass.Bass("TRN2", target_bir_lowering=False, debug=False)

    dt = {}
    dt["x_fm"] = nc.dram_tensor("x_fm", [NDB, 128, 2 * CH], F32, kind="ExternalInput")
    for nm, sh in [
        ("a_wq", [D, D]), ("a_wk", [D, D]), ("a_wv", [D, D]), ("a_wg", [D, D]),
        ("a_wo", [D, D]), ("a_w1", [D, HID1]), ("a_w2", [D, HID1]), ("a_w3", [HID1, D]),
        ("f_wq", [D, D]), ("f_wk", [D, D]), ("f_wv", [D, D]), ("f_wo", [D, D]),
        ("f_router", [D, NE]),
        ("f_wge", [NE, D, MI]), ("f_wue", [NE, D, MI]), ("f_wde", [NE, MI, D]),
        ("f_shg", [D, SI]), ("f_shu", [D, SI]), ("f_shd", [SI, D]),
        ("f_sheg", [D, 1]),
    ]:
        dt[nm] = nc.dram_tensor(nm, sh, F32, kind="ExternalInput")
    for nm in ("n1w", "n1b", "n2w", "n2b", "lnf1", "lnf2", "bq2", "bk2", "bv2"):
        dt[nm] = nc.dram_tensor(nm, [128, NDB], F32, kind="ExternalInput")
    dt["pos"] = nc.dram_tensor("pos", [1, 2 * CH], F32, kind="ExternalInput")
    dt["rsmeta"] = nc.dram_tensor("rsmeta", [1, 2 * NMC], F32, kind="ExternalInput")
    dt["padmeta"] = nc.dram_tensor("padmeta", [1, 16], F32, kind="ExternalInput")
    dt["blerow"] = nc.dram_tensor("blerow", [2, 128], F32, kind="ExternalInput")

    y_out = nc.dram_tensor("y", [NDB, 128, 2 * CH], F32, kind="ExternalOutput")

    gin = [nc.dram_tensor(f"gin{i}", [128, GW], BF16) for i in range(4)]
    gout = [nc.dram_tensor(f"gout{i}", [4 * 128, GW], BF16) for i in range(4)]
    qspill = nc.dram_tensor("qspill", [2, 128, NDB * CH], BF16)
    hgspill = nc.dram_tensor("hgspill", [2, 128, NDB * CH], BF16)
    xmidspill = nc.dram_tensor("xmidspill", [NDB, 128, 2 * CH], F32)

    TAPSHAPES = {
        "h1": [NDB, 128, 2 * CH], "k1": [NDB, 128, 2 * CH], "v1": [2, 4, 128, D],
        "q1": [NDB, 128, 2 * CH], "hg": [NDB, 128, 2 * CH],
        "retz": [NDB, 128, 2 * CH], "xwo": [NDB, 128, 2 * CH],
        "h2": [NDB, 128, 2 * CH], "xmid": [NDB, 128, 2 * CH],
        "q2": [NDB, 128, 2 * CH], "k2": [NDB, 128, 2 * CH], "v2": [2, 4, 128, D],
        "ctx2": [NDB, 128, 2 * CH], "xwo2": [NDB, 128, 2 * CH],
        "h2r": [NDB, 128, 2 * CH], "comb": [8, 2 * CH],
        "gin1": [2, 128, GW], "gin2": [2, 128, GW],
        "rope": [4, 128, 512], "att2l": [16, 512], "att2b": [128, 16],
        "att2k": [1, 128],
        "moe": [NDB, 128, 2 * CH],
    }
    taps = {t: nc.dram_tensor("tap_" + t, TAPSHAPES[t], F32, kind="ExternalOutput")
            for t in debug}

    x_sb = nc.alloc_sbuf_tensor("x_sb", [128, NDB * 2 * CH], F32)
    h_sb = nc.alloc_sbuf_tensor("h_sb", [128, NDB * 2 * CH], BF16)
    arena = nc.alloc_sbuf_tensor("arena", [128, 44 * 512], BF16)
    wr_sb = nc.alloc_sbuf_tensor("wr_sb", [128, 7 * 512], BF16)
    fs1 = nc.alloc_sbuf_tensor("fs1", [128, 512], F32)
    fs2 = nc.alloc_sbuf_tensor("fs2", [128, 512], F32)
    fs3 = nc.alloc_sbuf_tensor("fs3", [128, 512], F32)
    cosq = [nc.alloc_sbuf_tensor(f"cos{u}", [128, 512], BF16) for u in range(2)]
    sinq = [nc.alloc_sbuf_tensor(f"sin{u}", [128, 512], BF16) for u in range(2)]
    tri = nc.alloc_sbuf_tensor("tri", [128, 896], BF16)
    grow = nc.alloc_sbuf_tensor("grow", [128, NH * NMC], F32)
    gcols = nc.alloc_sbuf_tensor("gcols", [NH, 512], F32)
    st16 = gcols
    st8f = nc.alloc_sbuf_tensor("st8f", [1, 2112], F32)
    ones1r = nc.alloc_sbuf_tensor("ones1r", [1, 128], F32)
    e16 = nc.alloc_sbuf_tensor("e16", [16, 16 * 128], F32)
    onesst = nc.alloc_sbuf_tensor("onesst", [128, 1], BF16)
    lind = nc.alloc_sbuf_tensor("lind", [128, 16 * 16], BF16)
    iotac = nc.alloc_sbuf_tensor("iotac", [128, 1], F32)
    lng_sb = nc.alloc_sbuf_tensor("lng_sb", [16, 1], F32)
    invf = nc.alloc_sbuf_tensor("invf", [128, 1], F32)
    metars = nc.alloc_sbuf_tensor("metars", [1, 2 * NMC], F32)
    metapad = nc.alloc_sbuf_tensor("metapad", [1, 16], F32)
    blerow = nc.alloc_sbuf_tensor("ble_sb", [1, 256], BF16)
    kstatr = nc.alloc_sbuf_tensor("kstatr", [1, 128], BF16)
    mrow = nc.alloc_sbuf_tensor("mrow", [1, 288], F32)
    brow = nc.alloc_sbuf_tensor("brow", [1, 16], F32)
    bbase = nc.alloc_sbuf_tensor("bbase", [128, 16], F32)
    basb = nc.alloc_sbuf_tensor("basb", [128, 48], F32)
    vecs = {}
    for nm in ("n1w", "n1b", "n2w", "n2b", "lnf1", "lnf2", "bq2", "bkc2", "bv2"):
        vecs[nm] = nc.alloc_sbuf_tensor("v_" + nm, [128, NDB], F32)
    ident = nc.alloc_sbuf_tensor("ident", [128, 128], F32)
    qtl = nc.alloc_sbuf_tensor("qtl", [128, 6 * 512], BF16)
    cmbT = nc.alloc_sbuf_tensor("cmbT", [8, 2 * 512], F32)
    router_sb = nc.alloc_sbuf_tensor("router_sb", [128, 16 * 8], BF16)
    sigr = nc.alloc_sbuf_tensor("sigr", [1, 1024], F32)
    metap = sigr
    sigb = nc.alloc_sbuf_tensor("sigb", [128, 512], F32)
    ptile = nc.alloc_sbuf_tensor("ptile", [128, 2 * 512], BF16)
    i32t = nc.alloc_sbuf_tensor("i32t", [128, 256], mybir.dt.int32)

    tapb = nc.alloc_sbuf_tensor("tapb", [128, 512], F32) if debug else None

    pb = [nc.alloc_psum_tensor(f"pb{i}", [128, 512], F32) for i in range(8)]

    rem = nc.sbuf_bytes_remaining
    assert rem >= 0, f"SBUF over budget by {-rem} bytes"

    with nc.Block() as block:
        P = Prog(nc, block)
        banks = [Slot(pb[i]) for i in range(8)]
        wring = Ring([Slot(wr_sb[:, i * 512:(i + 1) * 512]) for i in range(7)])
        for i, s in enumerate(wring.slots):
            s.qsem = f"dW{i}"
        aslot = [Slot(arena[:, i * 512:(i + 1) * 512]) for i in range(44)]
        bounce = Ring([aslot[40], aslot[41], aslot[42], aslot[43]])
        fsl2 = Slot(fs2)
        fsl2.qsem = "dF0"
        fsl3 = Slot(fs3)
        fsl3.qsem = "dF1"
        tap_sl = Slot(None)
        bbase_slot = Slot(None)
        biasall_slot = Slot(None)
        st8_sl = Slot(None)
        st16_sl = Slot(None)
        brow_sl = Slot(None)
        sigr_sl = Slot(None)

        def stc(i, w=512):
            return st8f[0:1, i * 512:i * 512 + w]

        def xt(db, u):
            return x_sb[:, db * 2 * CH + u * CH: db * 2 * CH + (u + 1) * CH]

        def ht(db, u):
            return h_sb[:, db * 2 * CH + u * CH: db * 2 * CH + (u + 1) * CH]

        for nm in ("n1w", "n1b", "n2w", "n2b", "lnf1", "lnf2", "bq2", "bv2"):
            P.dma("A", vecs[nm][:, :], dt[nm][:, :], sem="dV")
        P.dma("A", vecs["bkc2"][:, :], dt["bk2"][:, :], sem="dV")
        P.dma("A", metap[:, :], dt["pos"][:, :], sem="dV")
        P.dma("A", metars[:, :], dt["rsmeta"][:, :], sem="dV")
        P.dma("A", metapad[:, :], dt["padmeta"][:, :], sem="dV")
        tok_vecs = P.full("dV")
        tok_pos = tok_rs = tok_pad = tok_vecs
        vtok = {nm: tok_vecs for nm in
                ("n1w", "n1b", "n2w", "n2b", "lnf1", "lnf2", "bq2", "bv2")}
        vtok["bkc2"] = P.op("V", lambda e: e.tensor_single_scalar(
            vecs["bkc2"][:, :], vecs["bkc2"][:, :], CINV, ALU.mult),
            waits=[tok_vecs])
        xtok = [[None, None] for _ in range(NDB)]
        for db in range(NDB):
            P.dma("A", x_sb[:, db * 2 * CH:(db + 1) * 2 * CH], dt["x_fm"][db],
                  sem="dX")
        tok_x = P.full("dX")
        for db in range(NDB):
            xtok[db][0] = xtok[db][1] = tok_x
        for kb in range(NDB):
            P.dma("G", router_sb[:, kb * 8:(kb + 1) * 8],
                  dt["f_router"][kb * 128:(kb + 1) * 128, :], sem="dM")
        P.dma("G", blerow[:, :], dt["blerow"][:, :], sem="dM")
        tok_misc = P.full("dM")
        tok_ble = tok_misc
        rtok = [tok_misc] * NDB

        tok_iota = P.op("G", lambda e: e.iota(
            iotac[:, :], [[0, 1]], channel_multiplier=1,
            allow_small_or_imprecise_dtypes=True))
        t_ = P.op("G", lambda e: e.iota(
            lng_sb[:, :], [[0, 1]], base=5, channel_multiplier=1,
            allow_small_or_imprecise_dtypes=True))
        t_ = P.op("A", lambda e: e.activation(
            lng_sb[:, :], lng_sb[:, :], AF.Exp, bias=0.0, scale=-LN2),
            waits=[t_])
        t_ = P.op("V", lambda e: e.tensor_scalar(
            lng_sb[:, :], lng_sb[:, :], -1.0, 1.0, ALU.mult, ALU.add), waits=[t_])
        tok_lng = P.op("A", lambda e: e.activation(
            lng_sb[:, :], lng_sb[:, :], AF.Ln, bias=0.0, scale=1.0), waits=[t_])

        t_ = P.op("V", lambda e: e.tensor_single_scalar(
            invf[:, :], iotac[:, :], 64.0, ALU.is_ge), waits=[tok_iota])
        t_ = P.op("V", lambda e: e.scalar_tensor_tensor(
            invf[:, :], invf[:, :], -64.0, iotac[:, :], ALU.mult, ALU.add),
            waits=[t_])
        tok_invf = P.op("A", lambda e: e.activation(
            invf[:, :], invf[:, :], AF.Exp, bias=0.0, scale=-LN1E6 / 64.0), waits=[t_])

        tok_e2 = P.op("G", lambda e: e.memset(ones1r[:, :], 1.0))

        t_ = P.op("G", lambda e: e.memset(e16[:, :], 1.0))
        t_ = P.op("G", lambda e: e.affine_select(
            e16[:, :], e16[:, :], pattern=[[1, 16 * 128]], compare_op=ALU.is_ge,
            fill=0.0, base=0, channel_multiplier=-128), waits=[t_])
        tok_e16 = P.op("G", lambda e: e.affine_select(
            e16[:, :], e16[:, :], pattern=[[-1, 16 * 128]], compare_op=ALU.is_ge,
            fill=0.0, base=127, channel_multiplier=128), waits=[t_])

        tok_ones = P.op("G", lambda e: e.memset(onesst[:, :], 1.0))

        t_ = P.op("G", lambda e: e.memset(lind[:, :], 0.0))
        tok_lind = t_
        for h in range(NH):
            tok_lind = P.op("G", lambda e, h=h: e.memset(
                lind[:, 16 * h + h:16 * h + h + 1], 1.0), waits=[tok_lind])

        t_ = P.op("G", lambda e: e.memset(ident[:, :], 1.0))
        tok_id = P.op("G", lambda e: e.affine_select(
            ident[:, :], ident[:, :], pattern=[[1, 128]], compare_op=ALU.is_equal,
            fill=0.0, base=0, channel_multiplier=-1), waits=[t_])

        t_ = P.op("G", lambda e: e.memset(tri[:, :], 1.0))
        _tt = P.op("G", lambda e: e.affine_select(
            tri[:, :], tri[:, :], pattern=[[1, 896]], compare_op=ALU.is_ge,
            fill=0.0, base=-384, channel_multiplier=-1), waits=[t_])
        tok_tri = [_tt] * 4

        def trim(m):
            return tri[:, 384 - 128 * m:896 - 128 * m]

        t_ = P.op("G", lambda e: e.iota(
            gcols[:, :], [[1, 512]], channel_multiplier=0,
            allow_small_or_imprecise_dtypes=True))
        tok_gcols = P.op("A", lambda e: e.activation(
            gcols[:, :], gcols[:, :], AF.Exp, bias=0.0, scale=lng_sb[:, :]),
            waits=[t_, tok_lng])

        tok_kstat0 = P.op("G", lambda e: e.memset(kstatr[:, :], 0.0))
        tok_sig0 = None
        tok_brow0 = P.op("G", lambda e: e.memset(brow[:, :], 0.0))

        bkD, bkV = banks[6], banks[7]
        tD = P.op("T", lambda e: e.matmul(
            bkD.ap[:, 0:NMC], ones1r[:, :], metars[0:1, 0:NMC],
            start=True, stop=True), waits=[tok_e2, tok_rs] + bkD.fill_waits())
        tV = P.op("T", lambda e: e.matmul(
            bkV.ap[:, 0:NMC], ones1r[:, :], metars[0:1, NMC:2 * NMC],
            start=True, stop=True), waits=[tok_e2, tok_rs] + bkV.fill_waits())
        tDm = P.op("V", lambda e: e.tensor_scalar(
            fs1[:, 0:NMC], bkD.ap[:, 0:NMC], iotac[:, :], None, ALU.subtract),
            waits=[tD, tok_iota])
        bkD.used(tDm)
        grow_tok = [None] * NH
        for h in range(NH):
            te = P.op("A", lambda e, h=h: e.activation(
                grow[:, NMC * h:NMC * (h + 1)], fs1[:, 0:NMC], AF.Exp, bias=0.0,
                scale=LNG[h]), waits=[tDm])
            tm = P.op("V", lambda e, h=h: e.tensor_tensor(
                grow[:, NMC * h:NMC * (h + 1)], grow[:, NMC * h:NMC * (h + 1)],
                bkV.ap[:, 0:NMC], ALU.mult), waits=[te, tV])
            bkV.used(tm)
            grow_tok[h] = tm

        def growcol(h, qv, s, kt):
            c = NMC * h + ((qv * 8 + s) * 4 + kt if s >= 0 else 64 + kt)
            return grow[:, c:c + 1]

        tok_cs = []
        for u in range(2):
            bA = banks[6]
            tb = P.op("T", lambda e, u=u: e.matmul(
                bA.ap[:, :], ones1r[:, :], metap[0:1, u * 512:(u + 1) * 512],
                start=True, stop=True),
                waits=[tok_e2, tok_pos] + bA.fill_waits())
            t1 = P.op("V", lambda e: e.tensor_scalar(
                fs2[:, :], bA.ap[:, :], invf[:, :], None, ALU.mult),
                waits=[tb, tok_invf])
            bA.used(t1)
            prev_act = None
            for trig, dst in (("cos", cosq[u]), ("sin", sinq[u])):
                shift = 0.0 if trig == "sin" else math.pi / 2.0
                t2 = P.op("V", lambda e, shift=shift: e.tensor_scalar(
                    fs3[:, :], fs2[:, :], shift, 1.0 / TWO_PI, ALU.add, ALU.mult),
                    waits=[t1, prev_act] + tok_cs)
                t3 = t2
                for hf in range(2):
                    t3a = P.op("V", lambda e, hf=hf: e.tensor_copy(
                        i32t[:, :], fs3[:, hf * 256:(hf + 1) * 256]), waits=[t3])
                    t3 = P.op("V", lambda e, hf=hf: e.tensor_copy(
                        fs1[:, hf * 256:(hf + 1) * 256], i32t[:, :]), waits=[t3a])
                t4a = P.op("V", lambda e, shift=shift: e.tensor_scalar(
                    fs3[:, :], fs2[:, :], shift, None, ALU.add), waits=[t2])
                t4 = P.op("V", lambda e: e.scalar_tensor_tensor(
                    fs3[:, :], fs1[:, :], -TWO_PI, fs3[:, :], ALU.mult, ALU.add),
                    waits=[t3, t4a])
                if trig == "cos":
                    t5 = P.op("A", lambda e, dst=dst: e.activation(
                        dst[:, :], fs3[:, :], AF.Sin, bias=0.0, scale=1.0),
                        waits=[t4, t3, t4a])
                else:
                    t5a = P.op("A", lambda e: e.activation(
                        fs1[:, :], fs3[:, :], AF.Sin, bias=0.0, scale=1.0),
                        waits=[t4, t3, t4a])
                    t5b = P.op("V", lambda e, dst=dst: e.tensor_copy(
                        dst[0:64, :], fs1[64:128, :]), waits=[t5a])
                    t5 = P.op("V", lambda e, dst=dst: e.tensor_copy(
                        dst[64:128, :], fs1[0:64, :]), waits=[t5b])
                prev_act = t5
                tok_cs.append(t5)
        if "rope" in taps:
            for i, tile in enumerate([cosq[0], sinq[0], cosq[1], sinq[1]]):
                tcp = P.op("V", lambda e, tile=tile: e.tensor_copy(
                    tapb[:, :], tile[:, :]), waits=tok_cs + tap_sl.fill_waits())
                td = P.dma("S", taps["rope"][i], tapb[:, :], waits=[tcp],
                           sem="dT")
                tap_sl.used(td)
        tok_sig0 = P.op("G", lambda e: e.memset(sigr[:, :], 0.0),
                        waits=[tok_cs[-1]] + tok_cs)

        def ln_stats(src, src_toks, rms, n, eps):
            bkS, bkQ = banks[5], banks[4]
            firstS = True
            firstQ = True
            prep = []
            for db in range(NDB):
                xb, sq = aslot[db], aslot[NDB + db]
                if not rms:
                    tc = P.op("V", lambda e, db=db, xb=xb: e.tensor_copy(
                        xb.ap, src(db)), waits=[src_toks(db)] + xb.fill_waits())
                    xb.ready = tc
                ts = P.op("A", lambda e, db=db, sq=sq: e.activation(
                    sq.ap, src(db), AF.Square, bias=0.0, scale=1.0),
                    waits=[src_toks(db)] + sq.fill_waits())
                sq.ready = ts
                prep.append((xb, sq))
            tS = None
            tQ = None
            for db in range(NDB):
                xb, sq = prep[db]
                last = db == NDB - 1
                if not rms:
                    w = [xb.ready, tok_ones] + (bkS.fill_waits() if firstS else [])
                    tS = P.op("T", lambda e, xb=xb, f=firstS, l=last: e.matmul(
                        bkS.ap[0:1, :], onesst[:, 0:1], xb.ap, start=f, stop=l),
                        waits=w, inc=True)
                    xb.used(tS)
                    firstS = False
                w = [sq.ready, tok_ones] + (bkQ.fill_waits() if firstQ else [])
                tQ = P.op("T", lambda e, sq=sq, f=firstQ, l=last: e.matmul(
                    bkQ.ap[0:1, :], onesst[:, 0:1], sq.ap, start=f, stop=l),
                    waits=w, inc=True)
                sq.used(tQ)
                firstQ = False
            stw = st8_sl.fill_waits()
            if rms:
                t1 = P.op("V", lambda e: e.memset(stc(0), 0.0), waits=stw)
            else:
                t1 = P.op("V", lambda e: e.tensor_single_scalar(
                    stc(0), bkS.ap[0:1, :], -1.0 / n, ALU.mult),
                    waits=[tS] + stw)
                bkS.used(t1)
            t2 = P.op("V", lambda e, eps=eps: e.tensor_scalar(
                stc(2), bkQ.ap[0:1, :], 1.0 / n, eps, ALU.mult, ALU.add),
                waits=[tQ] + stw)
            bkQ.used(t2)
            if not rms:
                t3 = P.op("V", lambda e: e.tensor_tensor(
                    stc(3), stc(0), stc(0), ALU.mult), waits=[t1])
                t2 = P.op("V", lambda e: e.tensor_tensor(
                    stc(2), stc(2), stc(3), ALU.subtract), waits=[t2, t3])
            t4 = P.op("A", lambda e: e.activation(
                stc(2), stc(2), AF.Sqrt, bias=0.0, scale=1.0), waits=[t2])
            t5 = P.op("V", lambda e: e.reciprocal(stc(1), stc(2)), waits=[t4])
            return (t1, t5)

        def ln_bcast(rows_toks):
            bM, bI = banks[5], banks[4]
            tM = P.op("T", lambda e: e.matmul(
                bM.ap[:, :], ones1r[:, :], stc(0), start=True, stop=True),
                waits=list(rows_toks) + [tok_e2] + bM.fill_waits())
            tI = P.op("T", lambda e: e.matmul(
                bI.ap[:, :], ones1r[:, :], stc(1), start=True, stop=True),
                waits=list(rows_toks) + [tok_e2] + bI.fill_waits())
            st8_sl.used(tM)
            st8_sl.used(tI)
            return tM, tI

        def ln_apply(dst, src, src_toks, wv, bv, tM, tI, extra=()):
            bM, bI = banks[5], banks[4]
            toks = []
            for db in range(NDB):
                t1 = P.op("V", lambda e, db=db: e.tensor_tensor(
                    fs1[:, :], src(db), bM.ap[:, :], ALU.add),
                    waits=[src_toks(db), tM] + list(extra))
                bM.used(t1)
                if wv is not None:
                    t2 = P.op("V", lambda e, db=db: e.scalar_tensor_tensor(
                        fs2[:, :], fs1[:, :], wv[:, db:db + 1], bI.ap[:, :],
                        ALU.mult, ALU.mult), waits=[t1, tI])
                    bI.used(t2)
                    t2 = P.op("V", lambda e, db=db: e.tensor_scalar(
                        dst(db), fs2[:, :], bv[:, db:db + 1], None, ALU.add),
                        waits=[t2])
                else:
                    t2 = P.op("V", lambda e, db=db: e.tensor_tensor(
                        dst(db), fs1[:, :], bI.ap[:, :], ALU.mult), waits=[t1, tI])
                    bI.used(t2)
                toks.append(t2)
            return toks

        def rms_apply(dst, src, src_toks, wv, tI):
            bI = banks[4]
            toks = []
            for db in range(NDB):
                t2 = P.op("V", lambda e, db=db: e.scalar_tensor_tensor(
                    dst(db), src(db), wv[:, db:db + 1], bI.ap[:, :],
                    ALU.mult, ALU.mult), waits=[src_toks(db), tI])
                bI.used(t2)
                toks.append(t2)
            return toks

        def proj_pass(W, Ktot, O, movings, evac, g_oc=2):
            NKB = ceil_div(Ktot, 128)
            NOC = ceil_div(O, 128)
            nl = len(movings)
            assert g_oc * nl <= 4
            ocs = list(range(NOC))
            ogs = [ocs[i:i + g_oc] for i in range(0, NOC, g_oc)]
            for ogi, og in enumerate(ogs):
                base = 0 if ogi % 2 == 0 else 4
                c0 = og[0] * 128
                c1 = min(og[-1] * 128 + 128, O)
                width = c1 - c0
                mm_tok = {}
                for kb in range(NKB):
                    r0 = kb * 128
                    rows = min(128, Ktot - r0)
                    slot = wring.next()
                    wt = P.dma("G", slot.ap[0:rows, 0:width],
                               W[r0:r0 + rows, c0:c1], waits=slot.fill_waits(),
                               sem=slot.qsem)
                    slot.ready = wt
                    for oi, oc in enumerate(og):
                        w0 = oc * 128 - c0
                        w1 = min(w0 + 128, width)
                        for li, (tiles, ttok) in enumerate(movings):
                            bk = banks[base + oi * nl + li]
                            first = kb == 0
                            last = kb == NKB - 1
                            kb_final = (oi == len(og) - 1) and (li == nl - 1)
                            waits = [wt, ttok(kb)]
                            if first:
                                waits += bk.fill_waits()
                            tok = P.op("T", lambda e, bk=bk, slot=slot, rows=rows,
                                       w0=w0, w1=w1, tiles=tiles, kb=kb,
                                       first=first, last=last: e.matmul(
                                           bk.ap[0:(w1 - w0), :],
                                           slot.ap[0:rows, w0:w1],
                                           tiles(kb)[0:rows, :],
                                           start=first, stop=last),
                                       waits=waits, inc=(last or kb_final))
                            if kb_final:
                                slot.used(tok)
                            if last:
                                mm_tok[(oi, li)] = tok
                for oi, oc in enumerate(og):
                    ow = min(oc * 128 + 128, O) - oc * 128
                    for li in range(nl):
                        bk = banks[base + oi * nl + li]
                        evac(li, oc, bk, mm_tok[(oi, li)], ow)

        def vproj_pass(W, movings, evac):
            nl = len(movings)
            assert nl == 1
            tiles, ttok = movings[0]
            for og in range(4):
                base = 0 if og % 2 == 0 else 4
                mm_tok = {}
                for kb in range(NDB):
                    slot = wring.next()
                    wt = P.dma("G", slot.ap[:, :],
                               W[kb * 128:(kb + 1) * 128, og * 512:(og + 1) * 512],
                               waits=slot.fill_waits(), sem=slot.qsem)
                    slot.ready = wt
                    for tt in range(4):
                        bk = banks[base + tt]
                        first = kb == 0
                        last = kb == NDB - 1
                        waits = [wt, ttok(kb)]
                        if first:
                            waits += bk.fill_waits()
                        tok = P.op("T", lambda e, bk=bk, slot=slot, tiles=tiles,
                                   kb=kb, tt=tt, first=first, last=last: e.matmul(
                                       bk.ap[:, :],
                                       tiles(kb)[:, tt * 128:(tt + 1) * 128],
                                       slot.ap[:, :], start=first, stop=last),
                                   waits=waits, inc=(last or tt == 3))
                        if tt == 3:
                            slot.used(tok)
                        if last:
                            mm_tok[tt] = tok
                for tt in range(4):
                    evac(0, tt, og, banks[base + tt], mm_tok[tt])

        def tap_fm(name, db, u, src_ap, wtok, srcslot=None):
            if name in taps:
                t = P.op("V", lambda e: e.tensor_copy(tapb[:, :], src_ap),
                         waits=[wtok] + tap_sl.fill_waits())
                td = P.dma("S", taps[name][db, :, u * CH:(u + 1) * CH],
                           tapb[:, :], waits=[t], sem="dT")
                tap_sl.used(td)
                if srcslot is not None:
                    srcslot.used(t)

        htok = [[None, None] for _ in range(NDB)]
        for u in range(2):
            rt = ln_stats(lambda db, u=u: xt(db, u), lambda db, u=u: xtok[db][u],
                          rms=False, n=D, eps=EPS1)
            tM, tI = ln_bcast(rt)
            toks = ln_apply(lambda db, u=u: ht(db, u), lambda db, u=u: xt(db, u),
                            lambda db, u=u: xtok[db][u], vecs["n1w"], vecs["n1b"],
                            tM, tI, extra=[vtok["n1w"], vtok["n1b"]])
            for db in range(NDB):
                htok[db][u] = toks[db]
                tap_fm("h1", db, u, ht(db, u), toks[db])

        cc_tok = [None] * 4

        def kv_wave(u, blk):
            gi = (0 if blk == 1 else 2) + u
            gsem = f"dG{gi}"
            statrow_toks = []
            tz = P.op("V", lambda e: e.memset(aslot[32].ap, 0.0),
                      waits=aslot[32].fill_waits())
            aslot[32].ready = tz

            def kevac(li, oc, bk, mmtok, ow=128):
                s = aslot[oc]
                if blk == 1:
                    t = P.op("V", lambda e, bk=bk, s=s: e.tensor_single_scalar(
                        s.ap, bk.ap[:, :], CINV, ALU.mult),
                        waits=[mmtok] + s.fill_waits())
                else:
                    t0 = P.op("V", lambda e, bk=bk, oc=oc: e.tensor_scalar(
                        fs1[:, :], bk.ap[:, :], CINV, vecs["bkc2"][:, oc:oc + 1],
                        ALU.mult, ALU.add), waits=[mmtok, vtok["bkc2"]])
                    t = rope_apply(u, fs1, s, [t0] + s.fill_waits())
                bk.used(t)
                s.ready = t
                if blk == 2:
                    statrow_toks.append(k2_stats(oc, s))
                tap_fm("k1" if blk == 1 else "k2", oc, u, s.ap, t, srcslot=s)

            def k2_stats(oc, s):
                sq = bounce.next()
                ts = P.op("A", lambda e, s=s, sq=sq: e.activation(
                    sq.ap, s.ap, AF.Square, bias=0.0, scale=1.0),
                    waits=[s.ready] + sq.fill_waits())
                sq.ready = ts
                s.used(ts)
                bk = banks[3]
                tm = P.op("T", lambda e, sq=sq, bk=bk: e.matmul(
                    bk.ap[0:1, :], onesst[:, 0:1], sq.ap, start=True, stop=True),
                    waits=[ts, tok_ones] + bk.fill_waits())
                sq.used(tm)
                t1 = P.op("V", lambda e, bk=bk: e.reduce_max(
                    st8f[0:1, 2080:2081], bk.ap[0:1, :], axis=AX.X), waits=[tm])
                bk.used(t1)
                t2 = P.op("A", lambda e: e.activation(
                    st8f[0:1, 2080:2081], st8f[0:1, 2080:2081], AF.Sqrt,
                    bias=0.0, scale=1.0), waits=[t1])
                t3 = P.op("V", lambda e, oc=oc: e.tensor_single_scalar(
                    st8f[0:1, 2048 + oc:2049 + oc], st8f[0:1, 2080:2081],
                    1.01, ALU.mult), waits=[t2])
                return t3

            wname = "a_wk" if blk == 1 else "f_wk"
            mov = [(lambda kb, u=u: ht(kb, u), lambda kb, u=u: htok[kb][u])]
            proj_pass(dt[wname], D, D, mov, kevac, g_oc=3)
            if blk == 2:
                tb = P.op("V", lambda e: e.tensor_copy(
                    aslot[32].ap[0:1, 0:16], st8f[0:1, 2048:2064]),
                    waits=statrow_toks + [tz])
                aslot[32].ready = tb

            def vevac(li, tt, og, bk, mmtok):
                s = aslot[16 + tt * 4 + og]
                t = P.op("V", lambda e, bk=bk, s=s: e.tensor_copy(
                    s.ap, bk.ap[:, :]), waits=[mmtok] + s.fill_waits())
                bk.used(t)
                s.ready = t
                if ("v1" if blk == 1 else "v2") in taps:
                    nm = "v1" if blk == 1 else "v2"
                    tcp = P.op("V", lambda e, s=s: e.tensor_copy(tapb[:, :], s.ap),
                               waits=[t] + tap_sl.fill_waits())
                    s.used(tcp)
                    tdd = P.dma("S", taps[nm][u, tt, :, og * 512:(og + 1) * 512],
                                tapb[:, :], waits=[tcp], sem="dT")
                    tap_sl.used(tdd)

            vproj_pass(dt["a_wv" if blk == 1 else "f_wv"], mov, vevac)
            sready = [aslot[i].ready for i in range(33)]
            tst = P.dma("S", gin[gi].ap(), arena[:, 0:GW], waits=sready, sem=gsem)
            for i in range(33):
                aslot[i].used(tst)
            cc_tok[gi] = P.emit("G", lambda e, gi=gi: e.collective_compute(
                "AllGather", ALU.bypass,
                replica_groups=[[0, 1, 2, 3], [4, 5, 6, 7]],
                ins=[gin[gi].ap().opt()], outs=[gout[gi].ap().opt()]),
                waits=[tst], inc="cc")

        def rope_apply(u, src_f32, dstslot, waits):
            w = list(waits) + [tok_cs[-1]]
            ta = P.op("V", lambda e: e.tensor_tensor(
                fs2[:, :], src_f32[:, :], cosq[u][:, :], ALU.mult), waits=w)
            tb0 = P.op("V", lambda e: e.tensor_tensor(
                fs3[0:64, :], src_f32[64:128, :], sinq[u][64:128, :], ALU.mult),
                waits=w)
            tb1 = P.op("V", lambda e: e.tensor_tensor(
                fs3[64:128, :], src_f32[0:64, :], sinq[u][0:64, :], ALU.mult),
                waits=w)
            t0 = P.op("V", lambda e, d=dstslot: e.tensor_tensor(
                d.ap[0:64, :], fs2[0:64, :], fs3[0:64, :], ALU.subtract),
                waits=[ta, tb0] + dstslot.fill_waits())
            t1 = P.op("V", lambda e, d=dstslot: e.tensor_tensor(
                d.ap[64:128, :], fs2[64:128, :], fs3[64:128, :], ALU.add),
                waits=[ta, tb1])
            return t1

        kv_wave(0, 1)
        kv_wave(1, 1)

        def qevac1(li, oc, bk, mmtok, ow=128):
            s = aslot[li * 16 + oc]
            t = P.op("V", lambda e, bk=bk, s=s: e.tensor_copy(s.ap, bk.ap[:, :]),
                     waits=[mmtok] + s.fill_waits())
            bk.used(t)
            s.ready = t
            tap_fm("q1", oc, li, s.ap, t, srcslot=s)

        proj_pass(dt["a_wq"], D, D,
                  [(lambda kb: ht(kb, 0), lambda kb: htok[kb][0]),
                   (lambda kb: ht(kb, 1), lambda kb: htok[kb][1])],
                  qevac1, g_oc=2)
        for li in range(2):
            rd = [aslot[li * 16 + i].ready for i in range(16)]
            tq = P.dma("S", qspill[li], arena[:, li * 16 * 512:(li + 1) * 16 * 512],
                       waits=rd, sem="dQS")
            for i in range(16):
                aslot[li * 16 + i].used(tq)
        tok_qs1 = P.full("dQS")

        def gevac(li, oc, bk, mmtok, ow=128):
            s = aslot[li * 16 + oc]
            t = P.op("A", lambda e, bk=bk, s=s: e.activation(
                s.ap, bk.ap[:, :], AF.Silu, bias=0.0, scale=1.0),
                waits=[mmtok] + s.fill_waits())
            bk.used(t)
            s.ready = t
            tap_fm("hg", oc, li, s.ap, t, srcslot=s)

        proj_pass(dt["a_wg"], D, D,
                  [(lambda kb: ht(kb, 0), lambda kb: htok[kb][0]),
                   (lambda kb: ht(kb, 1), lambda kb: htok[kb][1])],
                  gevac, g_oc=2)
        for li in range(2):
            rd = [aslot[li * 16 + i].ready for i in range(16)]
            thg = P.dma("S", hgspill[li],
                        arena[:, li * 16 * 512:(li + 1) * 16 * 512],
                        waits=rd, sem="dHS")
            for i in range(16):
                aslot[li * 16 + i].used(thg)
        tok_hs = P.full("dHS")

        if stage <= 1:
            P.join()
            finish_out_dummy(P, y_out, x_sb, xtok)
            return nc, dict(taps=taps)

        kring = Ring([Slot(arena[:, 0:16 * 512]), Slot(arena[:, 16 * 512:32 * 512])])
        vring = Ring([Slot(arena[:, 32 * 512:36 * 512]),
                      Slot(arena[:, 36 * 512:40 * 512])])
        qring = Ring([Slot(qtl[:, i * 512:(i + 1) * 512]) for i in range(6)])
        for i, s in enumerate(kring.slots):
            s.qsem = f"dKR{i}"
        for i, s in enumerate(vring.slots):
            s.qsem = f"dVR{i}"
        for i, s in enumerate(qring.slots):
            s.qsem = f"dQR{i}"
        for i, s in enumerate(bounce.slots):
            s.qsem = f"dBN{i}"
        ptr = Ring([Slot(ptile[:, 0:512]), Slot(ptile[:, 512:1024])])

        def attn_mixer(blk, ztok):
            gbase = 0 if blk == 1 else 2
            kst_tok = {}
            for qv in range(2):
                iters = [("diag", qv)] + [("slot", s) for s in
                                          range(4 if qv == 0 else 8)]
                if blk == 2:
                    for s in (range(4) if qv == 0 else range(4, 8)):
                        gg, row = (gbase + (0 if s < 4 else 1)), s % 4
                        P.dma("S", kstatr[0:1, s * 16:(s + 1) * 16],
                              gout[gg][row * 128:row * 128 + 1,
                                       KW + VW:KW + VW + 16],
                              waits=[cc_tok[gg], tok_kstat0],
                              sem=f"dKS{qv}")
                    kst_tok[qv] = [P.full(f"dKS{qv}")]
                    tb1 = P.op("V", lambda e, qv=qv: e.tensor_tensor(
                        mrow[0:1, 0:128], kstatr[0:1, :],
                        blerow[0:1, 128 * qv:128 * (qv + 1)], ALU.mult),
                        waits=kst_tok.get(0, []) + kst_tok.get(1, []) + [tok_ble])
                    tmx = P.op("V", lambda e: e.tensor_copy(
                        mrow[0:1, 272:288], mrow[0:1, 0:16]), waits=[tb1])
                    for s in range(1, 8):
                        tmx = P.op("V", lambda e, s=s: e.tensor_tensor(
                            mrow[0:1, 272:288], mrow[0:1, 272:288],
                            mrow[0:1, s * 16:(s + 1) * 16], ALU.max), waits=[tmx])
                    maxk_tok = tmx
                    bkP = banks[6]
                    tP = P.op("T", lambda e, qv=qv: e.matmul(
                        bkP.ap[:, 0:8], ones1r[:, :],
                        metapad[0:1, 8 * qv:8 * (qv + 1)], start=True, stop=True),
                        waits=[tok_e2, tok_pad] + bkP.fill_waits())
                    tPc = P.op("V", lambda e: e.tensor_copy(
                        sigb[:, 0:8], bkP.ap[:, 0:8]), waits=[tP])
                    bkP.used(tPc)
                hgroups = ([list(range(0, 6)), list(range(6, 12)),
                            list(range(12, 16))] if blk == 1 else
                           [list(range(0, 5)), list(range(5, 10)),
                            list(range(10, 15)), [15]])
                for hg_i, hgrp in enumerate(hgroups):
                    ctx_banks = {h: banks[i] for i, h in enumerate(hgrp)}
                    lbank = banks[5] if blk == 2 else None
                    qslots = {}
                    bias_sl = {}
                    if blk == 2:
                        btoks = {}
                    for h in hgrp:
                        qs = qring.next()
                        tq = P.dma("S", qs.ap,
                                   qspill[qv][:, h * 512:(h + 1) * 512],
                                   waits=qs.fill_waits(), sem=qs.qsem)
                        qs.ready = tq
                        qslots[h] = qs
                        if blk == 2:
                            sq = bounce.next()
                            ts = P.op("A", lambda e, qs=qs, sq=sq: e.activation(
                                sq.ap, qs.ap, AF.Square, bias=0.0, scale=1.0),
                                waits=[tq] + sq.fill_waits())
                            sq.ready = ts
                            qs.used(ts)
                            bkq = banks[6]
                            tm = P.op("T", lambda e, sq=sq, bkq=bkq: e.matmul(
                                bkq.ap[0:1, :], onesst[:, 0:1], sq.ap,
                                start=True, stop=True),
                                waits=[ts, tok_ones] + bkq.fill_waits())
                            sq.used(tm)
                            t1 = P.op("V", lambda e, bkq=bkq: e.reduce_max(
                                st8f[0:1, 2081:2082], bkq.ap[0:1, :], axis=AX.X),
                                waits=[tm])
                            bkq.used(t1)
                            t2 = P.op("A", lambda e: e.activation(
                                st8f[0:1, 2081:2082], st8f[0:1, 2081:2082], AF.Sqrt, bias=0.0,
                                scale=1.0), waits=[t1])
                            t3 = P.op("V", lambda e: e.tensor_single_scalar(
                                st8f[0:1, 2081:2082], st8f[0:1, 2081:2082], -1.01, ALU.mult),
                                waits=[t2])
                            t4 = P.op("V", lambda e, h=h: e.tensor_tensor(
                                brow[0:1, h:h + 1], st8f[0:1, 2081:2082],
                                mrow[0:1, 272 + h:273 + h], ALU.mult),
                                waits=[t3, maxk_tok, tok_brow0] +
                                brow_sl.fill_waits())
                            btoks[h] = t4
                    if blk == 2:
                        bkB = banks[6]
                        bbs = bbase_slot
                        tB = P.op("T", lambda e: e.matmul(
                            bkB.ap[:, 0:16], ones1r[:, :], brow[0:1, :],
                            start=True, stop=True),
                            waits=list(btoks.values()) + [tok_e2] +
                            bkB.fill_waits())
                        brow_sl.used(tB)
                        tBc = P.op("V", lambda e: e.tensor_copy(
                            bbase[:, :], bkB.ap[:, 0:16]),
                            waits=[tB] + bbs.fill_waits())
                        bkB.used(tBc)
                        bbs.ready = tBc
                        tsh = P.op("V", lambda e: e.tensor_single_scalar(
                            brow[0:1, :], brow[0:1, :], 14.0, ALU.add),
                            waits=list(btoks.values()) + brow_sl.fill_waits())
                        btoks["shift"] = tsh
                        if "att2b" in taps and qv == 0 and hg_i == 0:
                            tb0 = P.op("V", lambda e: e.tensor_copy(
                                tapb[:, 0:16], bbase[:, :]),
                                waits=[tBc] + tap_sl.fill_waits())
                            tbd = P.dma("S", taps["att2b"][:, :], tapb[:, 0:16],
                                        waits=[tb0], sem="dT")
                            tap_sl.used(tbd)
                        if "att2k" in taps and qv == 0 and hg_i == 0:
                            tk0 = P.op("V", lambda e: e.tensor_copy(
                                tapb[0:1, 0:128], kstatr[0:1, :]),
                                waits=[maxk_tok] + tap_sl.fill_waits())
                            tkd = P.dma("S", taps["att2k"][:, :], tapb[0:1, 0:128],
                                        waits=[tk0], sem="dT")
                            tap_sl.used(tkd)
                        bas = biasall_slot
                        btk = None
                        for hi, h in enumerate(hgrp):
                            btk = P.op("V", lambda e, hi=hi, h=h: e.tensor_scalar(
                                basb[:, hi * 8:(hi + 1) * 8], sigb[:, 0:8], -1.0,
                                bbase[:, h:h + 1], ALU.mult, ALU.add),
                                waits=[tPc, tBc] +
                                (bas.fill_waits() if hi == 0 else []))
                        bas.ready = btk
                    first_mm = {h: True for h in hgrp}
                    lfirst = [True]
                    ltok = None
                    tiles_sched = []
                    for it in iters:
                        for kt in range(4):
                            for h in hgrp:
                                tiles_sched.append((it, kt, h))
                    n_t = len(tiles_sched)
                    cur = {"it": None, "ks": None, "ktkey": None, "v": None}
                    pend = None
                    ltok = None

                    def stage_a(j):
                        it, kt, h = tiles_sched[j]
                        if it != cur["it"]:
                            if it[0] == "diag":
                                ksrc = gin[gbase + qv]
                                kwait = [cc_tok[gbase + qv]]
                            else:
                                s = it[1]
                                gg = gbase + (0 if s < 4 else 1)
                                row = s % 4
                                ksrc = gout[gg][row * 128:(row + 1) * 128, :]
                                kwait = [cc_tok[gg]]
                            ks = kring.next()
                            tk = P.dma("S", ks.ap, ksrc[:, 0:KW],
                                       waits=kwait + ks.fill_waits(), sem=ks.qsem)
                            ks.ready = tk
                            cur["it"] = it
                            cur["ks"] = ks
                            cur["ksrc"] = ksrc
                            cur["kwait"] = kwait
                            cur["ktkey"] = None
                        if (id(it), kt) != cur["ktkey"]:
                            v = vring.next()
                            tv = P.dma("S", v.ap,
                                       cur["ksrc"][:, KW + kt * D:KW + (kt + 1) * D],
                                       waits=cur["kwait"] + v.fill_waits(),
                                       sem=v.qsem)
                            v.ready = tv
                            cur["ktkey"] = (id(it), kt)
                            cur["v"] = v
                        ks = cur["ks"]
                        v = cur["v"]
                        qs = qslots[h]
                        sbk = banks[6 + (j % 2)]
                        tmm = P.op("T", lambda e, sbk=sbk, ks=ks, h=h, kt=kt,
                                   qs=qs: e.matmul(
                                       sbk.ap[:, :],
                                       ks.ap[:, h * 512 + kt * 128:
                                             h * 512 + (kt + 1) * 128],
                                       qs.ap, start=True, stop=True),
                                   waits=[ks.ready, qs.ready] + sbk.fill_waits(),
                                   inc=True)
                        ks.used(tmm)
                        qs.used(tmm)
                        return (it, kt, h, sbk, tmm, v)

                    def stage_b(state, jb):
                        nonlocal ltok
                        it, kt, h, sbk, tmm, v = state
                        hi = hgrp.index(h)
                        pt = ptr.next()
                        if blk == 1:
                            rs = growcol(h, qv, it[1] if it[0] == "slot" else -1,
                                         kt)
                            tev = P.op("A", lambda e, sbk=sbk, pt=pt,
                                       rs=rs: e.activation(
                                           pt.ap, sbk.ap[:, :], AF.Copy,
                                           bias=0.0, scale=rs),
                                       waits=[tmm, grow_tok[h]] + pt.fill_waits())
                        else:
                            if it[0] == "diag":
                                bias = bbase[:, h:h + 1]
                                bwait = [bbase_slot.ready]
                                bsl = bbase_slot
                            else:
                                bias = basb[:, hi * 8 + it[1]:hi * 8 + it[1] + 1]
                                bwait = [biasall_slot.ready]
                                bsl = biasall_slot
                            tev = P.op("A", lambda e, sbk=sbk, pt=pt,
                                       bias=bias: e.activation(
                                           pt.ap, sbk.ap[:, :], AF.Exp,
                                           bias=bias, scale=1.0),
                                       waits=[tmm] + bwait + pt.fill_waits())
                            bsl.used(tev)
                        sbk.used(tev)
                        pt.ready = tev
                        if it[0] == "diag":
                            tev = P.op("V", lambda e, pt=pt, kt=kt: e.tensor_tensor(
                                pt.ap, pt.ap, trim(kt), ALU.mult),
                                waits=[tev, tok_tri[kt]])
                            pt.ready = tev
                        if blk == 2:
                            llast = jb == n_t - 1
                            tl = P.op("T", lambda e, pt=pt, h=h, lf=lfirst[0],
                                      ll=llast: e.matmul(
                                          lbank.ap[0:16, :],
                                          lind[:, 16 * h:16 * (h + 1)],
                                          pt.ap, start=lf, stop=ll),
                                      waits=[pt.ready, tok_lind] +
                                      (lbank.fill_waits() if lfirst[0] else []),
                                      inc=True)
                            lfirst[0] = False
                            pt.used(tl)
                            if llast:
                                ltok = tl
                        cbk = ctx_banks[h]
                        clast = (it is iters[-1]) and kt == 3
                        tc = P.op("T", lambda e, cbk=cbk, pt=pt, h=h, kt=kt, v=v,
                                  cf=first_mm[h], cl=clast: e.matmul(
                                      cbk.ap[:, :],
                                      v.ap[:, h * 128:(h + 1) * 128],
                                      pt.ap, start=cf, stop=cl),
                                  waits=[pt.ready, v.ready] +
                                  (cbk.fill_waits() if first_mm[h] else []),
                                  inc=True)
                        first_mm[h] = False
                        pt.used(tc)
                        v.used(tc)
                        if clast:
                            ctx_banks[h] = (cbk, tc)

                    for j in range(n_t + 1):
                        if j < n_t:
                            st = stage_a(j)
                        if j > 0:
                            stage_b(pend, j - 1)
                        if j < n_t:
                            pend = st
                    if blk == 2:
                        if "att2l" in taps and qv == 0 and hg_i == 0:
                            tl0 = P.op("V", lambda e: e.tensor_copy(
                                tapb[0:16, :], lbank.ap[0:16, :]),
                                waits=[ltok] + tap_sl.fill_waits())
                            tld = P.dma("S", taps["att2l"][:, :], tapb[0:16, :],
                                        waits=[tl0], sem="dT")
                            tap_sl.used(tld)
                            lbank.used(tl0)
                        trc = P.op("V", lambda e: e.tensor_single_scalar(
                            st16[:, :], lbank.ap[0:16, :], 1e-11, ALU.add),
                            waits=[ltok] + st16_sl.fill_waits())
                        lbank.used(trc)
                        trc = P.op("V", lambda e: e.reciprocal(
                            st16[:, :], st16[:, :]), waits=[trc])
                    else:
                        trc = None
                    for h in hgrp:
                        cbk, ctok = ctx_banks[h]
                        if blk == 1:
                            fin_retention(h, qv, cbk, ctok, ztok)
                        else:
                            fin_softmax(h, qv, cbk, ctok, trc, ztok)

        def fin_retention(h, qv, cbk, ctok, ztok):
            t0 = P.op("V", lambda e, cbk=cbk: e.tensor_copy(fs1[:, :], cbk.ap[:, :]),
                      waits=[ctok])
            cbk.used(t0)
            tg = P.op("T", lambda e, h=h, cbk=cbk: e.matmul(
                cbk.ap[:, :], e16[:, 128 * h:128 * (h + 1)], gcols[0:16, :],
                start=True, stop=True), waits=[t0, tok_e16, tok_gcols])
            t1 = P.op("V", lambda e, cbk=cbk: e.tensor_tensor(
                fs1[:, :], fs1[:, :], cbk.ap[:, :], ALU.mult), waits=[tg])
            cbk.used(t1)
            b1, b2 = bounce.next(), bounce.next()
            tb = P.op("V", lambda e, b1=b1: e.tensor_copy(b1.ap, fs1[:, :]),
                      waits=[t1] + b1.fill_waits())
            b1.ready = tb
            tsq = P.op("A", lambda e, b2=b2: e.activation(
                b2.ap, fs1[:, :], AF.Square, bias=0.0, scale=1.0),
                waits=[t1] + b2.fill_waits())
            b2.ready = tsq
            tm1 = P.op("T", lambda e, b1=b1, cbk=cbk: e.matmul(
                cbk.ap[0:1, :], onesst[:, 0:1], b1.ap, start=True, stop=True),
                waits=[tb, tok_ones, t1], inc=True)
            b1.used(tm1)
            t2 = P.op("V", lambda e, cbk=cbk: e.tensor_single_scalar(
                stc(0), cbk.ap[0:1, :], -1.0 / DHD, ALU.mult),
                waits=[tm1] + st8_sl.fill_waits())
            cbk.used(t2)
            tm2 = P.op("T", lambda e, b2=b2, cbk=cbk: e.matmul(
                cbk.ap[0:1, :], onesst[:, 0:1], b2.ap, start=True, stop=True),
                waits=[tsq, t2], inc=True)
            b2.used(tm2)
            t3 = P.op("V", lambda e, cbk=cbk: e.tensor_scalar(
                stc(2), cbk.ap[0:1, :], 1.0 / DHD, EPS1, ALU.mult, ALU.add),
                waits=[tm2])
            cbk.used(t3)
            t4 = P.op("V", lambda e: e.tensor_tensor(
                stc(3), stc(0), stc(0), ALU.mult), waits=[t2])
            t5 = P.op("V", lambda e: e.tensor_tensor(
                stc(2), stc(2), stc(3), ALU.subtract), waits=[t3, t4])
            t6 = P.op("A", lambda e: e.activation(
                stc(2), stc(2), AF.Sqrt, bias=0.0, scale=1.0), waits=[t5])
            t7 = P.op("V", lambda e: e.reciprocal(stc(1), stc(2)), waits=[t6])
            tM = P.op("T", lambda e, cbk=cbk: e.matmul(
                cbk.ap[:, :], ones1r[:, :], stc(0), start=True, stop=True),
                waits=[t2, t7, tok_e2])
            st8_sl.used(tM)
            t8 = P.op("V", lambda e, cbk=cbk: e.tensor_tensor(
                fs1[:, :], fs1[:, :], cbk.ap[:, :], ALU.add), waits=[tM])
            cbk.used(t8)
            tI = P.op("T", lambda e, cbk=cbk: e.matmul(
                cbk.ap[:, :], ones1r[:, :], stc(1), start=True, stop=True),
                waits=[t8, tok_e2])
            st8_sl.used(tI)
            hgs = bounce.next()
            thg = P.dma("S", hgs.ap, hgspill[qv][:, h * 512:(h + 1) * 512],
                        waits=hgs.fill_waits() + [tok_hs], sem=hgs.qsem)
            hgs.ready = thg
            t9 = P.op("V", lambda e, cbk=cbk: e.tensor_tensor(
                fs1[:, :], fs1[:, :], cbk.ap[:, :], ALU.mult), waits=[tI])
            cbk.used(t9)
            t10 = P.op("V", lambda e, h=h, qv=qv, hgs=hgs: e.tensor_tensor(
                ht(h, qv), fs1[:, :], hgs.ap, ALU.mult), waits=[t9, thg])
            hgs.used(t10)
            ztok[h][qv] = t10
            tap_fm("retz", h, qv, ht(h, qv), t10)

        def fin_softmax(h, qv, cbk, ctok, trc, ztok):
            rb = banks[7] if cbk is not banks[7] else banks[6]
            tR = P.op("T", lambda e, h=h, rb=rb: e.matmul(
                rb.ap[:, :], e16[:, 128 * h:128 * (h + 1)], st16[0:16, :],
                start=True, stop=True),
                waits=[trc, tok_e16] + rb.fill_waits())
            st16_sl.used(tR)
            t0 = P.op("V", lambda e, cbk=cbk: e.tensor_copy(fs1[:, :], cbk.ap[:, :]),
                      waits=[ctok])
            cbk.used(t0)
            t1 = P.op("V", lambda e, rb=rb: e.tensor_tensor(
                fs1[:, :], fs1[:, :], rb.ap[:, :], ALU.mult), waits=[t0, tR])
            rb.used(t1)
            t2 = P.op("V", lambda e, h=h, qv=qv: e.tensor_scalar(
                ht(h, qv), fs1[:, :], vecs["bv2"][:, h:h + 1], None, ALU.add),
                waits=[t1, tok_vecs])
            ztok[h][qv] = t2
            tap_fm("ctx2", h, qv, ht(h, qv), t2)

        P.join()
        ztok = [[None, None] for _ in range(NH)]
        attn_mixer(1, ztok)

        def woevac1(li, oc, bk, mmtok, ow=128):
            t = P.op("V", lambda e, oc=oc, li=li, bk=bk: e.tensor_tensor(
                xt(oc, li), xt(oc, li), bk.ap[:, :], ALU.add),
                waits=[mmtok, xtok[oc][li]])
            bk.used(t)
            xtok[oc][li] = t
            tap_fm("xwo", oc, li, xt(oc, li), t)

        proj_pass(dt["a_wo"], D, D,
                  [(lambda kb: ht(kb, 0), lambda kb: ztok[kb][0]),
                   (lambda kb: ht(kb, 1), lambda kb: ztok[kb][1])],
                  woevac1, g_oc=2)

        if stage <= 2:
            P.join()
            finish_out_dummy(P, y_out, x_sb, xtok)
            return nc, dict(taps=taps)

        P.join()
        h2tok = [[None, None] for _ in range(NDB)]
        xmid_tok = [[None, None] for _ in range(NDB)]
        for u in range(2):
            rt = ln_stats(lambda db, u=u: xt(db, u), lambda db, u=u: xtok[db][u],
                          rms=False, n=D, eps=EPS1)
            tM, tI = ln_bcast(rt)
            toks = ln_apply(lambda db, u=u: ht(db, u), lambda db, u=u: xt(db, u),
                            lambda db, u=u: xtok[db][u], vecs["n2w"], vecs["n2b"],
                            tM, tI, extra=[tok_vecs])
            for db in range(NDB):
                h2tok[db][u] = toks[db]
                tap_fm("h2", db, u, ht(db, u), toks[db])
            utok = [None] * NKB1
            usl = [aslot[i] for i in range(NKB1)]

            def w1evac(li, oc, bk, mmtok, ow=128):
                s = usl[oc]
                t = P.op("A", lambda e, bk=bk, s=s, ow=ow: e.activation(
                    s.ap[0:ow, :], bk.ap[0:ow, :], AF.Silu, bias=0.0, scale=1.0),
                    waits=[mmtok] + s.fill_waits())
                bk.used(t)
                s.ready = t

            mv = [(lambda kb, u=u: ht(kb, u), lambda kb, u=u: h2tok[kb][u])]
            proj_pass(dt["a_w1"], D, HID1, mv, w1evac, g_oc=4)

            def w2evac(li, oc, bk, mmtok, ow=128):
                s = usl[oc]
                t = P.op("V", lambda e, bk=bk, s=s, ow=ow: e.tensor_tensor(
                    s.ap[0:ow, :], s.ap[0:ow, :], bk.ap[0:ow, :], ALU.mult),
                    waits=[mmtok, s.ready])
                bk.used(t)
                s.ready = t
                utok[oc] = t

            proj_pass(dt["a_w2"], D, HID1, mv, w2evac, g_oc=4)

            def w3evac(li, oc, bk, mmtok, ow=128):
                t = P.op("V", lambda e, oc=oc, u=u, bk=bk: e.tensor_tensor(
                    xt(oc, u), xt(oc, u), bk.ap[:, :], ALU.add),
                    waits=[mmtok, xtok[oc][u]])
                bk.used(t)
                xtok[oc][u] = t

            proj_pass(dt["a_w3"], HID1, D,
                      [(lambda kb: usl[kb].ap, lambda kb: utok[kb])],
                      w3evac, g_oc=4)
            for kb in range(NKB1):
                usl[kb].used(utok[kb])

            for db in range(NDB):
                sl = fsl2 if db % 2 == 0 else fsl3
                tld = P.dma("A", sl.ap[:, :],
                            dt["x_fm"][db, :, u * CH:(u + 1) * CH],
                            waits=sl.fill_waits() + [P.full("cV"), P.full("cA")],
                            sem=sl.qsem)
                t = P.op("V", lambda e, db=db, u=u, sl=sl: e.tensor_tensor(
                    xt(db, u), xt(db, u), sl.ap[:, :], ALU.add),
                    waits=[tld, xtok[db][u]])
                sl.used(t)
                xtok[db][u] = t
            rt = ln_stats(lambda db, u=u: xt(db, u), lambda db, u=u: xtok[db][u],
                          rms=False, n=D, eps=EPS1)
            tM, tI = ln_bcast(rt)
            toks = ln_apply(lambda db, u=u: xt(db, u), lambda db, u=u: xt(db, u),
                            lambda db, u=u: xtok[db][u], None, None, tM, tI)
            for db in range(NDB):
                xtok[db][u] = toks[db]
                tap_fm("xmid", db, u, xt(db, u), toks[db])
                xmid_tok[db][u] = P.dma(
                    "S", xmidspill[db, :, u * CH:(u + 1) * CH], xt(db, u),
                    waits=[toks[db]], sem=f"dXS{u}")
                xtok[db][u] = xmid_tok[db][u]

            rt = ln_stats(lambda db, u=u: xt(db, u), lambda db, u=u: xtok[db][u],
                          rms=True, n=D, eps=EPS2)
            tM, tI = ln_bcast(rt)
            toks = rms_apply(lambda db, u=u: ht(db, u), lambda db, u=u: xt(db, u),
                             lambda db, u=u: xtok[db][u], vecs["lnf1"], tI)
            for db in range(NDB):
                htok[db][u] = toks[db]

            def qevac2(li, oc, bk, mmtok, ow=128, u=u):
                s = aslot[oc]
                t0 = P.op("V", lambda e, bk=bk, oc=oc: e.tensor_scalar(
                    fs1[:, :], bk.ap[:, :], 1.0, vecs["bq2"][:, oc:oc + 1],
                    ALU.mult, ALU.add), waits=[mmtok, tok_vecs])
                t = rope_apply(u, fs1, s, [t0] + s.fill_waits())
                bk.used(t)
                s.ready = t
                tap_fm("q2", oc, u, s.ap, t, srcslot=s)

            proj_pass(dt["f_wq"], D, D,
                      [(lambda kb, u=u: ht(kb, u), lambda kb, u=u: htok[kb][u])],
                      qevac2, g_oc=3)
            rd = [aslot[i].ready for i in range(16)]
            tq2 = P.dma("S", qspill[u], arena[:, 0:16 * 512], waits=rd,
                        sem="dQS2")
            for i in range(16):
                aslot[i].used(tq2)
            kv_wave(u, 2)

        if stage <= 3:
            P.join()
            finish_out_dummy(P, y_out, x_sb, xtok)
            return nc, dict(taps=taps)

        P.join()
        ztok2 = [[None, None] for _ in range(NH)]
        attn_mixer(2, ztok2)

        def woevac2(li, oc, bk, mmtok, ow=128):
            t = P.op("V", lambda e, oc=oc, li=li, bk=bk: e.tensor_tensor(
                xt(oc, li), xt(oc, li), bk.ap[:, :], ALU.add),
                waits=[mmtok, xtok[oc][li]])
            bk.used(t)
            xtok[oc][li] = t
            tap_fm("xwo2", oc, li, xt(oc, li), t)

        proj_pass(dt["f_wo"], D, D,
                  [(lambda kb: ht(kb, 0), lambda kb: ztok2[kb][0]),
                   (lambda kb: ht(kb, 1), lambda kb: ztok2[kb][1])],
                  woevac2, g_oc=2)

        P.join()
        for u in range(2):
            rt = ln_stats(lambda db, u=u: xt(db, u), lambda db, u=u: xtok[db][u],
                          rms=True, n=D, eps=EPS2)
            tM, tI = ln_bcast(rt)
            toks = rms_apply(lambda db, u=u: ht(db, u), lambda db, u=u: xt(db, u),
                             lambda db, u=u: xtok[db][u], vecs["lnf2"], tI)
            for db in range(NDB):
                h2tok[db][u] = toks[db]
                tap_fm("h2r", db, u, ht(db, u), toks[db])

        cmb_tok = [None, None]
        for u in range(2):
            ttoks = []
            for tt in range(4):
                bk = banks[tt % 2]
                first = True
                for kb in range(NDB):
                    last = kb == NDB - 1
                    tmm = P.op("T", lambda e, bk=bk, kb=kb, tt=tt, u=u,
                               first=first, last=last: e.matmul(
                                   bk.ap[0:128, 0:8],
                                   h_sb[:, kb * 2 * CH + u * CH + tt * 128:
                                        kb * 2 * CH + u * CH + (tt + 1) * 128],
                                   router_sb[:, kb * 8:(kb + 1) * 8],
                                   start=first, stop=last),
                               waits=[h2tok[kb][u], rtok[kb]] +
                               (bk.fill_waits() if first else []), inc=True)
                    first = False
                t1 = P.op("V", lambda e, bk=bk: e.reduce_max(
                    fs1[:, 0:1], bk.ap[:, 0:8], axis=AX.X,
                    negate=True), waits=[tmm])
                bk.used(t1)
                t2 = P.op("A", lambda e, bk=bk: e.activation(
                    fs2[:, 0:8], bk.ap[:, 0:8], AF.Exp, bias=fs1[:, 0:1],
                    scale=1.0, accum_out=fs1[:, 1:2]), waits=[t1])
                bk.used(t2)
                t3 = P.op("V", lambda e: e.reciprocal(
                    fs1[:, 2:3], fs1[:, 1:2]), waits=[t2])
                t4 = P.op("V", lambda e: e.tensor_scalar(
                    fs2[:, 0:8], fs2[:, 0:8], fs1[:, 2:3], None, ALU.mult),
                    waits=[t3])
                t5 = P.op("V", lambda e: e.reduce_max(
                    fs1[:, 3:4], fs2[:, 0:8], axis=AX.X, negate=True), waits=[t4])
                t6 = P.op("V", lambda e: e.tensor_scalar(
                    fs3[:, 0:8], fs2[:, 0:8], fs1[:, 3:4], None, ALU.add),
                    waits=[t5])
                t7 = P.op("V", lambda e: e.tensor_single_scalar(
                    fs3[:, 0:8], fs3[:, 0:8], 0.0, ALU.is_equal), waits=[t6])
                t8 = P.op("V", lambda e: e.scalar_tensor_tensor(
                    fs3[:, 0:8], fs3[:, 0:8], -2.0, fs2[:, 0:8],
                    ALU.mult, ALU.add), waits=[t7])
                t9 = P.op("V", lambda e: e.reduce_max(
                    fs1[:, 4:5], fs3[:, 0:8], axis=AX.X, negate=True), waits=[t8])
                t10 = P.op("V", lambda e: e.tensor_scalar(
                    fs3[:, 0:8], fs2[:, 0:8], fs1[:, 4:5], None, ALU.add),
                    waits=[t9])
                t11 = P.op("V", lambda e: e.tensor_single_scalar(
                    fs3[:, 0:8], fs3[:, 0:8], 0.0, ALU.is_ge), waits=[t10])
                t12 = P.op("V", lambda e: e.tensor_tensor(
                    fs1[:, 8:16], fs2[:, 0:8], fs3[:, 0:8], ALU.mult),
                    waits=[t11])
                bk6 = banks[6]
                tt_ = P.op("T", lambda e, tt=tt, bk6=bk6: e.transpose(
                    bk6.ap[0:8, tt * 128:(tt + 1) * 128], fs1[:, 8:16],
                    ident[:, :]),
                    waits=[t12, tok_id] + (bk6.fill_waits() if tt == 0 else []))
                ttoks.append(tt_)
            tc = P.op("V", lambda e, u=u: e.tensor_copy(
                cmbT[0:8, u * 512:(u + 1) * 512], banks[6].ap[0:8, :]),
                waits=ttoks)
            banks[6].used(tc)
            cmb_tok[u] = tc
            if "comb" in taps:
                P.dma("S", taps["comb"][0:8, u * 512:(u + 1) * 512],
                      cmbT[0:8, u * 512:(u + 1) * 512], waits=[tc], sem="dT")

        for ei in range(NE):
            ztoks = [[None, None] for _ in range(MI // 128)]
            zsl = [[aslot[oc], aslot[11 + oc]] for oc in range(MI // 128)]
            cb = [None, None]
            for li in range(2):
                cbb = banks[3] if li == 0 else banks[7]
                tcb = P.op("T", lambda e, ei=ei, li=li, cbb=cbb: e.matmul(
                    cbb.ap[:, :], e16[0:8, 128 * ei:128 * ei + 128],
                    cmbT[0:8, li * 512:(li + 1) * 512], start=True, stop=True),
                    waits=[cmb_tok[li], tok_e16] + cbb.fill_waits())
                cb[li] = (cbb, tcb)

            def gevac_e(li, oc, bk, mmtok, ow=128, ei=ei):
                s = zsl[oc][li]
                t = P.op("A", lambda e, bk=bk, s=s: e.activation(
                    s.ap, bk.ap[:, :], AF.Silu, bias=0.0, scale=1.0),
                    waits=[mmtok] + s.fill_waits())
                bk.used(t)
                s.ready = t

            mv2 = [(lambda kb: ht(kb, 0), lambda kb: h2tok[kb][0]),
                   (lambda kb: ht(kb, 1), lambda kb: h2tok[kb][1])]
            proj_pass(dt["f_wge"][ei], D, MI, mv2, gevac_e, g_oc=1)

            def uevac_e(li, oc, bk, mmtok, ow=128, ei=ei):
                s = zsl[oc][li]
                cbb, ctk = cb[li]
                t0 = P.op("V", lambda e, bk=bk, s=s: e.tensor_tensor(
                    fs1[:, :], s.ap, bk.ap[:, :], ALU.mult),
                    waits=[mmtok, s.ready])
                bk.used(t0)
                t1 = P.op("V", lambda e, s=s, cbb=cbb: e.tensor_tensor(
                    s.ap, fs1[:, :], cbb.ap[:, :], ALU.mult), waits=[t0, ctk])
                cbb.used(t1)
                s.ready = t1
                ztoks[oc][li] = t1

            proj_pass(dt["f_wue"][ei], D, MI, mv2, uevac_e, g_oc=1)

            def devac_e(li, oc, bk, mmtok, ow=128):
                t = P.op("V", lambda e, oc=oc, li=li, bk=bk: e.tensor_tensor(
                    xt(oc, li), xt(oc, li), bk.ap[:, :], ALU.add),
                    waits=[mmtok, xtok[oc][li]])
                bk.used(t)
                xtok[oc][li] = t

            proj_pass(dt["f_wde"][ei], MI, D,
                      [(lambda kb: zsl[kb][0].ap, lambda kb: ztoks[kb][0]),
                       (lambda kb: zsl[kb][1].ap, lambda kb: ztoks[kb][1])],
                      devac_e, g_oc=2)
            for oc in range(MI // 128):
                for li in range(2):
                    zsl[oc][li].used(ztoks[oc][li])

        for u in range(2):
            ssl = [aslot[i] for i in range(44)]
            stoks = [None] * 44

            def shg_evac(li, oc, bk, mmtok, ow=128):
                s = ssl[oc]
                t = P.op("A", lambda e, bk=bk, s=s: e.activation(
                    s.ap, bk.ap[:, :], AF.Silu, bias=0.0, scale=1.0),
                    waits=[mmtok] + s.fill_waits())
                bk.used(t)
                s.ready = t

            mv = [(lambda kb, u=u: ht(kb, u), lambda kb, u=u: h2tok[kb][u])]
            proj_pass(dt["f_shg"], D, SI, mv, shg_evac, g_oc=4)

            def shu_evac(li, oc, bk, mmtok, ow=128):
                s = ssl[oc]
                t = P.op("V", lambda e, bk=bk, s=s: e.tensor_tensor(
                    s.ap, s.ap, bk.ap[:, :], ALU.mult), waits=[mmtok, s.ready])
                bk.used(t)
                s.ready = t
                stoks[oc] = t

            proj_pass(dt["f_shu"], D, SI, mv, shu_evac, g_oc=4)

            bk3 = banks[3]
            first = True
            for kb in range(NDB):
                slot = wring.next()
                wt = P.dma("G", slot.ap[0:128, 0:1],
                           dt["f_sheg"][kb * 128:(kb + 1) * 128, :],
                           waits=slot.fill_waits(), sem=slot.qsem)
                slot.ready = wt
                last = kb == NDB - 1
                tmm = P.op("T", lambda e, slot=slot, kb=kb, u=u, first=first,
                           last=last, bk3=bk3: e.matmul(
                               bk3.ap[0:1, :], slot.ap[0:128, 0:1],
                               ht(kb, u), start=first, stop=last),
                           waits=[wt, h2tok[kb][u]] +
                           (bk3.fill_waits() if first else []), inc=True)
                slot.used(tmm)
                first = False
            tsg = P.op("A", lambda e, u=u, bk3=bk3: e.activation(
                sigr[0:1, u * 512:(u + 1) * 512], bk3.ap[0:1, :], AF.Sigmoid,
                bias=0.0, scale=1.0),
                waits=[tmm, tok_sig0] + sigr_sl.fill_waits())
            bk3.used(tsg)
            bkS = banks[3]
            tsb = P.op("T", lambda e, u=u, bkS=bkS: e.matmul(
                bkS.ap[:, :], ones1r[:, :], sigr[0:1, u * 512:(u + 1) * 512],
                start=True, stop=True),
                waits=[tsg, tok_e2] + bkS.fill_waits())
            sigr_sl.used(tsb)
            tsb2 = P.op("V", lambda e, bkS=bkS: e.tensor_copy(
                sigb[:, :], bkS.ap[:, :]), waits=[tsb])
            bkS.used(tsb2)

            def shd_evac(li, oc, bk, mmtok, ow=128, u=u, tsb2=tsb2):
                t0 = P.op("V", lambda e, bk=bk: e.tensor_tensor(
                    fs1[:, :], sigb[:, :], bk.ap[:, :], ALU.mult),
                    waits=[mmtok, tsb2])
                bk.used(t0)
                t1 = P.op("V", lambda e, oc=oc, u=u: e.tensor_tensor(
                    xt(oc, u), xt(oc, u), fs1[:, :], ALU.add),
                    waits=[t0, xtok[oc][u]])
                xtok[oc][u] = t1
                tap_fm("moe", oc, u, xt(oc, u), t1)

            proj_pass(dt["f_shd"], SI, D,
                      [(lambda kb: ssl[kb].ap, lambda kb: stoks[kb])],
                      shd_evac, g_oc=4)
            for kb in range(44):
                ssl[kb].used(stoks[kb])

        P.join()
        out_dma = []
        for u in range(2):
            for db in range(NDB):
                sl = fsl2 if db % 2 == 0 else fsl3
                tld = P.dma("A", sl.ap[:, :],
                            xmidspill[db, :, u * CH:(u + 1) * CH],
                            waits=[xmid_tok[db][u]] + sl.fill_waits(),
                            sem=sl.qsem)
                t = P.op("V", lambda e, db=db, u=u, sl=sl: e.tensor_tensor(
                    xt(db, u), xt(db, u), sl.ap[:, :], ALU.add),
                    waits=[tld, xtok[db][u]])
                sl.used(t)
                xtok[db][u] = t
            rt = ln_stats(lambda db, u=u: xt(db, u), lambda db, u=u: xtok[db][u],
                          rms=False, n=D, eps=EPS1)
            tM, tI = ln_bcast(rt)
            toks = ln_apply(lambda db, u=u: xt(db, u), lambda db, u=u: xt(db, u),
                            lambda db, u=u: xtok[db][u], None, None, tM, tI)
            for db in range(NDB):
                xtok[db][u] = toks[db]
                out_dma.append(P.dma("A", y_out[db, :, u * CH:(u + 1) * CH],
                                     xt(db, u), waits=[toks[db]], sem="dY"))

        P.emit("S", lambda e: e.nop(),
               waits=[P.full(nm) for nm in list(P.cnt)])

    return nc, dict(taps=taps)


def finish_out_dummy(P, y_out, x_sb, xtok):
    for u in range(2):
        for db in range(16):
            P.dma("A", y_out[db, :, u * CH:(u + 1) * CH],
                  x_sb[:, db * 2 * CH + u * CH: db * 2 * CH + (u + 1) * CH],
                  waits=[xtok[db][u]], sem="dY")
    P.emit("S", lambda e: e.nop(), waits=[P.full(nm) for nm in list(P.cnt)])


def _core_meta(r):
    cA, cB = r, 7 - r
    chunks = [cA, cB]
    pos = np.zeros((1, 2 * CH), np.float32)
    for qv in range(2):
        pos[0, qv * CH:(qv + 1) * CH] = (np.arange(CH, dtype=np.float32)
                                         + 512 * chunks[qv])

    def chunk_at_slot(s):
        rho, w = s % 4, s // 4
        return rho if w == 0 else 7 - rho

    rs = np.zeros((1, 2 * NMC), np.float32)
    for qv in range(2):
        qc = chunks[qv]
        for s in range(8):
            kc = chunk_at_slot(s)
            for kt in range(4):
                c = (qv * 8 + s) * 4 + kt
                if kc < qc:
                    rs[0, c] = 512 * qc - (512 * kc + 128 * kt)
                    rs[0, NMC + c] = 1.0
    for kt in range(4):
        rs[0, 64 + kt] = -128.0 * kt
        rs[0, NMC + 64 + kt] = 1.0

    pad = np.full((1, 16), 1000.0, np.float32)
    ble = np.zeros((1, 256), np.float32)
    for qv in range(2):
        qc = chunks[qv]
        for s in range(8):
            kc = chunk_at_slot(s)
            if kc < qc:
                pad[0, qv * 8 + s] = 0.0
            for h in range(NH):
                ble[0, qv * 128 + s * 16 + h] = 1.0 if kc <= qc else 0.0
    return pos, rs, pad, ble


def _vec(a):
    return np.ascontiguousarray(np.asarray(a, np.float32).reshape(NDB, 128).T)


def kernel(**inputs):
    nc, meta = build_program()
    wmap = {
        "a_wq": "L0_attn_wq", "a_wk": "L0_attn_wk", "a_wv": "L0_attn_wv",
        "a_wg": "L0_attn_wg", "a_wo": "L0_attn_wo", "a_w1": "L0_attn_w1",
        "a_w2": "L0_attn_w2", "a_w3": "L0_attn_w3",
        "f_wq": "L0_ffn_wq", "f_wk": "L0_ffn_wk", "f_wv": "L0_ffn_wv",
        "f_wo": "L0_ffn_wo", "f_router": "L0_ffn_w_router",
        "f_wge": "L0_ffn_w_gate_e", "f_wue": "L0_ffn_w_up_e",
        "f_wde": "L0_ffn_w_down_e", "f_shg": "L0_ffn_w_sh_gate",
        "f_shu": "L0_ffn_w_sh_up", "f_shd": "L0_ffn_w_sh_down",
    }
    vmap = {
        "n1w": "L0_attn_norm1_w", "n1b": "L0_attn_norm1_b",
        "n2w": "L0_attn_norm2_w", "n2b": "L0_attn_norm2_b",
        "lnf1": "L0_ffn_w_ln1", "lnf2": "L0_ffn_w_ln2",
        "bq2": "L0_ffn_bq", "bk2": "L0_ffn_bk", "bv2": "L0_ffn_bv",
    }
    shared = {k: np.ascontiguousarray(np.asarray(inputs[v], np.float32))
              for k, v in wmap.items()}
    shared["f_sheg"] = np.ascontiguousarray(
        np.asarray(inputs["L0_ffn_w_sh_eg"], np.float32).reshape(D, 1))
    for k, v in vmap.items():
        shared[k] = _vec(inputs[v])
    X = np.asarray(inputs["L0_attn_x"], np.float32)

    in_maps = []
    for core in range(8):
        b, r = core // 4, core % 4
        cA, cB = r, 7 - r
        xa = X[b, 512 * cA:512 * cA + 512]
        xb = X[b, 512 * cB:512 * cB + 512]
        xf = np.concatenate([xa, xb], 0).T.reshape(NDB, 128, 2 * CH)
        pos, rs, pad, ble = _core_meta(r)
        m = dict(shared)
        m["x_fm"] = np.ascontiguousarray(xf)
        m["pos"] = pos
        m["rsmeta"] = rs
        m["padmeta"] = pad
        m["blerow"] = ble
        in_maps.append(m)

    res = run_bass_kernel_spmd(nc, in_maps, core_ids=list(range(8)))

    out = np.empty((2, 4096, D), np.float32)
    for core in range(8):
        b, r = core // 4, core % 4
        cA, cB = r, 7 - r
        y = np.asarray(res.results[core]["y"])
        yt = y.reshape(D, 2 * CH).T
        out[b, 512 * cA:512 * cA + 512] = yt[0:CH]
        out[b, 512 * cB:512 * cB + 512] = yt[CH:2 * CH]
    return out


# version 05
# speedup vs baseline: 1.2945x; 1.0755x over previous
import math
import numpy as np
import concourse.bass as bass
import concourse.mybir as mybir
from concourse.bass_utils import run_bass_kernel_spmd

F32 = mybir.dt.float32
BF16 = mybir.dt.bfloat16
AF = mybir.ActivationFunctionType
ALU = mybir.AluOpType
AX = mybir.AxisListType

D = 2048
NDB = 16
CH = 512
NH = 16
DHD = 128
HID1 = 5461
NKB1 = 43
NE = 8
MI = 1408
SI = 5632
EPS1 = 1e-5
EPS2 = 1e-6
CINV = DHD ** -0.5
LN2 = math.log(2.0)
LN1E6 = math.log(1.0e6)
TWO_PI = 2.0 * math.pi
GAMMA = [1.0 - 2.0 ** (-5.0 - h) for h in range(NH)]
LNG = [math.log(g) for g in GAMMA]

KW = NDB * CH
VW = 4 * D
SW = CH
GW = KW + VW + SW
NMC = 68


def ceil_div(a, b):
    return (a + b - 1) // b


class Prog:
    def __init__(self, nc, block):
        self.nc = nc
        self.block = block
        self.sec = {"T": block.tensor, "V": block.vector, "A": block.scalar,
                    "G": block.gpsimd, "S": block.sync}
        self.sem = {}
        self.cnt = {}
        for nm in ("cT", "cV", "cA", "cG", "dS", "dA", "dG", "cc"):
            self.sem[nm] = nc.alloc_semaphore(nm)
            self.cnt[nm] = 0
        self.seen = {e: {} for e in self.sec}

    def emit(self, eng, fn, waits=(), inc=None, inc_by=1):
        need = []
        seen = self.seen[eng]
        for t in waits:
            if t is None:
                continue
            nm, val = t
            if seen.get(nm, 0) >= val:
                continue
            seen[nm] = val
            need.append((nm, val))
        tok = None
        if inc is not None:
            self.cnt[inc] += inc_by
            tok = (inc, self.cnt[inc])
        sems = self.sem

        def run(e):
            for nm, val in need:
                e.wait_ge(sems[nm], val)
            inst = fn(e)
            if inc is not None:
                inst.then_inc(sems[inc], inc_by)
        self.sec[eng](run)
        return tok

    ESEM = {"T": "cT", "V": "cV", "A": "cA", "G": "cG"}
    DSEM = {"S": "dS", "A": "dA", "G": "dG"}

    def op(self, eng, fn, waits=(), inc=True):
        return self.emit(eng, fn, waits, inc=self.ESEM[eng] if inc else None)

    def getsem(self, nm):
        if nm not in self.sem:
            self.sem[nm] = self.nc.alloc_semaphore(nm)
            self.cnt[nm] = 0
        return nm

    def dma(self, eng, out, in_, waits=(), sem=None):
        if sem is not None:
            sem = self.getsem(sem)
        return self.emit(eng, lambda e: e.dma_start(out=out, in_=in_),
                         waits, inc=sem or self.DSEM[eng], inc_by=16)

    def full(self, nm):
        return (nm, self.cnt.get(nm, 0))

    def join(self):
        toks = [(nm, self.cnt[nm]) for nm in self.cnt]
        for eng in ("T", "V", "A", "G", "S"):
            self.emit(eng, lambda e: e.nop(), waits=toks)


class Slot:
    def __init__(self, ap):
        self.ap = ap
        self.ready = None
        self.users = []
        self.qsem = None

    def fill_waits(self):
        w = list(self.users)
        self.users = []
        return w

    def used(self, tok):
        if tok is not None:
            self.users.append(tok)


class Ring:
    def __init__(self, slots):
        self.slots = slots
        self.i = 0

    def next(self):
        s = self.slots[self.i % len(self.slots)]
        self.i += 1
        return s


def build_program(stage=99, debug=()):
    nc = bass.Bass("TRN2", target_bir_lowering=False, debug=False)

    dt = {}
    dt["x_fm"] = nc.dram_tensor("x_fm", [NDB, 128, 2 * CH], F32, kind="ExternalInput")
    for nm, sh in [
        ("a_wq", [D, D]), ("a_wk", [D, D]), ("a_wv", [D, D]), ("a_wg", [D, D]),
        ("a_wo", [D, D]), ("a_w1", [D, HID1]), ("a_w2", [D, HID1]), ("a_w3", [HID1, D]),
        ("f_wq", [D, D]), ("f_wk", [D, D]), ("f_wv", [D, D]), ("f_wo", [D, D]),
        ("f_router", [D, NE]),
        ("f_wge", [NE, D, MI]), ("f_wue", [NE, D, MI]), ("f_wde", [NE, MI, D]),
        ("f_shg", [D, SI]), ("f_shu", [D, SI]), ("f_shd", [SI, D]),
        ("f_sheg", [D, 1]),
    ]:
        dt[nm] = nc.dram_tensor(nm, sh, F32, kind="ExternalInput")
    for nm in ("n1w", "n1b", "n2w", "n2b", "lnf1", "lnf2", "bq2", "bk2", "bv2"):
        dt[nm] = nc.dram_tensor(nm, [128, NDB], F32, kind="ExternalInput")
    dt["pos"] = nc.dram_tensor("pos", [1, 2 * CH], F32, kind="ExternalInput")
    dt["rsmeta"] = nc.dram_tensor("rsmeta", [1, 2 * NMC], F32, kind="ExternalInput")
    dt["padmeta"] = nc.dram_tensor("padmeta", [1, 16], F32, kind="ExternalInput")
    dt["blerow"] = nc.dram_tensor("blerow", [2, 128], F32, kind="ExternalInput")

    y_out = nc.dram_tensor("y", [NDB, 128, 2 * CH], F32, kind="ExternalOutput")

    gin = [nc.dram_tensor(f"gin{i}", [128, GW], BF16) for i in range(4)]
    gout = [nc.dram_tensor(f"gout{i}", [4 * 128, GW], BF16) for i in range(4)]
    qspill = nc.dram_tensor("qspill", [2, 128, NDB * CH], BF16)
    hgspill = nc.dram_tensor("hgspill", [2, 128, NDB * CH], BF16)
    xmidspill = nc.dram_tensor("xmidspill", [NDB, 128, 2 * CH], F32)

    TAPSHAPES = {
        "h1": [NDB, 128, 2 * CH], "k1": [NDB, 128, 2 * CH], "v1": [2, 4, 128, D],
        "q1": [NDB, 128, 2 * CH], "hg": [NDB, 128, 2 * CH],
        "retz": [NDB, 128, 2 * CH], "xwo": [NDB, 128, 2 * CH],
        "h2": [NDB, 128, 2 * CH], "xmid": [NDB, 128, 2 * CH],
        "q2": [NDB, 128, 2 * CH], "k2": [NDB, 128, 2 * CH], "v2": [2, 4, 128, D],
        "ctx2": [NDB, 128, 2 * CH], "xwo2": [NDB, 128, 2 * CH],
        "h2r": [NDB, 128, 2 * CH], "comb": [8, 2 * CH],
        "gin1": [2, 128, GW], "gin2": [2, 128, GW],
        "rope": [4, 128, 512], "att2l": [16, 512], "att2b": [128, 16],
        "att2k": [1, 128],
        "moe": [NDB, 128, 2 * CH],
    }
    taps = {t: nc.dram_tensor("tap_" + t, TAPSHAPES[t], F32, kind="ExternalOutput")
            for t in debug}

    x_sb = nc.alloc_sbuf_tensor("x_sb", [128, NDB * 2 * CH], F32)
    h_sb = nc.alloc_sbuf_tensor("h_sb", [128, NDB * 2 * CH], BF16)
    arena = nc.alloc_sbuf_tensor("arena", [128, 44 * 512], BF16)
    wr_sb = nc.alloc_sbuf_tensor("wr_sb", [128, 7 * 512], BF16)
    fs1 = nc.alloc_sbuf_tensor("fs1", [128, 512], F32)
    fs2 = nc.alloc_sbuf_tensor("fs2", [128, 512], F32)
    fs3 = nc.alloc_sbuf_tensor("fs3", [128, 512], F32)
    cosq = [nc.alloc_sbuf_tensor(f"cos{u}", [128, 512], BF16) for u in range(2)]
    sinq = [nc.alloc_sbuf_tensor(f"sin{u}", [128, 512], BF16) for u in range(2)]
    tri = nc.alloc_sbuf_tensor("tri", [128, 896], BF16)
    grow = nc.alloc_sbuf_tensor("grow", [128, NH * NMC], F32)
    gcols = nc.alloc_sbuf_tensor("gcols", [NH, 512], F32)
    st16 = gcols
    st8f = nc.alloc_sbuf_tensor("st8f", [1, 2112], F32)
    ones1r = nc.alloc_sbuf_tensor("ones1r", [1, 128], F32)
    e16 = nc.alloc_sbuf_tensor("e16", [16, 16 * 128], F32)
    onesst = nc.alloc_sbuf_tensor("onesst", [128, 1], BF16)
    lind = nc.alloc_sbuf_tensor("lind", [128, 16 * 16], BF16)
    iotac = nc.alloc_sbuf_tensor("iotac", [128, 1], F32)
    lng_sb = nc.alloc_sbuf_tensor("lng_sb", [16, 1], F32)
    invf = nc.alloc_sbuf_tensor("invf", [128, 1], F32)
    metars = nc.alloc_sbuf_tensor("metars", [1, 2 * NMC], F32)
    metapad = nc.alloc_sbuf_tensor("metapad", [1, 16], F32)
    blerow = nc.alloc_sbuf_tensor("ble_sb", [1, 256], BF16)
    kstatr = nc.alloc_sbuf_tensor("kstatr", [1, 128], BF16)
    mrow = nc.alloc_sbuf_tensor("mrow", [1, 288], F32)
    brow = nc.alloc_sbuf_tensor("brow", [1, 16], F32)
    bbase = nc.alloc_sbuf_tensor("bbase", [128, 16], F32)
    basb = nc.alloc_sbuf_tensor("basb", [128, 48], F32)
    vecs = {}
    for nm in ("n1w", "n1b", "n2w", "n2b", "lnf1", "lnf2", "bq2", "bkc2", "bv2"):
        vecs[nm] = nc.alloc_sbuf_tensor("v_" + nm, [128, NDB], F32)
    ident = nc.alloc_sbuf_tensor("ident", [128, 128], F32)
    qtl = nc.alloc_sbuf_tensor("qtl", [128, 6 * 512], BF16)
    cmbT = nc.alloc_sbuf_tensor("cmbT", [8, 2 * 512], F32)
    router_sb = nc.alloc_sbuf_tensor("router_sb", [128, 16 * 8], BF16)
    sigr = nc.alloc_sbuf_tensor("sigr", [1, 1024], F32)
    metap = sigr
    sigb = nc.alloc_sbuf_tensor("sigb", [128, 512], F32)
    ptile = nc.alloc_sbuf_tensor("ptile", [128, 2 * 512], BF16)
    i32t = nc.alloc_sbuf_tensor("i32t", [128, 256], mybir.dt.int32)

    tapb = nc.alloc_sbuf_tensor("tapb", [128, 512], F32) if debug else None

    pb = [nc.alloc_psum_tensor(f"pb{i}", [128, 512], F32) for i in range(8)]

    rem = nc.sbuf_bytes_remaining
    assert rem >= 0, f"SBUF over budget by {-rem} bytes"

    with nc.Block() as block:
        P = Prog(nc, block)
        banks = [Slot(pb[i]) for i in range(8)]
        wring = Ring([Slot(wr_sb[:, i * 512:(i + 1) * 512]) for i in range(7)])
        for i, s in enumerate(wring.slots):
            s.qsem = f"dW{i}"
        aslot = [Slot(arena[:, i * 512:(i + 1) * 512]) for i in range(44)]
        bounce = Ring([aslot[40], aslot[41], aslot[42], aslot[43]])
        fsl2 = Slot(fs2)
        fsl2.qsem = "dF0"
        fsl3 = Slot(fs3)
        fsl3.qsem = "dF1"
        tap_sl = Slot(None)
        bbase_slot = Slot(None)
        biasall_slot = Slot(None)
        st8_sl = Slot(None)
        st16_sl = Slot(None)
        brow_sl = Slot(None)
        sigr_sl = Slot(None)

        def stc(i, w=512):
            return st8f[0:1, i * 512:i * 512 + w]

        def xt(db, u):
            return x_sb[:, db * 2 * CH + u * CH: db * 2 * CH + (u + 1) * CH]

        def ht(db, u):
            return h_sb[:, db * 2 * CH + u * CH: db * 2 * CH + (u + 1) * CH]

        for nm in ("n1w", "n1b", "n2w", "n2b", "lnf1", "lnf2", "bq2", "bv2"):
            P.dma("A", vecs[nm][:, :], dt[nm][:, :], sem="dV")
        P.dma("A", vecs["bkc2"][:, :], dt["bk2"][:, :], sem="dV")
        P.dma("A", metap[:, :], dt["pos"][:, :], sem="dV")
        P.dma("A", metars[:, :], dt["rsmeta"][:, :], sem="dV")
        P.dma("A", metapad[:, :], dt["padmeta"][:, :], sem="dV")
        tok_vecs = P.full("dV")
        tok_pos = tok_rs = tok_pad = tok_vecs
        vtok = {nm: tok_vecs for nm in
                ("n1w", "n1b", "n2w", "n2b", "lnf1", "lnf2", "bq2", "bv2")}
        vtok["bkc2"] = P.op("V", lambda e: e.tensor_single_scalar(
            vecs["bkc2"][:, :], vecs["bkc2"][:, :], CINV, ALU.mult),
            waits=[tok_vecs])
        xtok = [[None, None] for _ in range(NDB)]
        for db in range(NDB):
            P.dma("A", x_sb[:, db * 2 * CH:(db + 1) * 2 * CH], dt["x_fm"][db],
                  sem="dX")
        tok_x = P.full("dX")
        for db in range(NDB):
            xtok[db][0] = xtok[db][1] = tok_x
        for kb in range(NDB):
            P.dma("G", router_sb[:, kb * 8:(kb + 1) * 8],
                  dt["f_router"][kb * 128:(kb + 1) * 128, :], sem="dM")
        P.dma("G", blerow[:, :], dt["blerow"][:, :], sem="dM")
        tok_misc = P.full("dM")
        tok_ble = tok_misc
        rtok = [tok_misc] * NDB

        tok_iota = P.op("G", lambda e: e.iota(
            iotac[:, :], [[0, 1]], channel_multiplier=1,
            allow_small_or_imprecise_dtypes=True))
        t_ = P.op("G", lambda e: e.iota(
            lng_sb[:, :], [[0, 1]], base=5, channel_multiplier=1,
            allow_small_or_imprecise_dtypes=True))
        t_ = P.op("A", lambda e: e.activation(
            lng_sb[:, :], lng_sb[:, :], AF.Exp, bias=0.0, scale=-LN2),
            waits=[t_])
        t_ = P.op("V", lambda e: e.tensor_scalar(
            lng_sb[:, :], lng_sb[:, :], -1.0, 1.0, ALU.mult, ALU.add), waits=[t_])
        tok_lng = P.op("A", lambda e: e.activation(
            lng_sb[:, :], lng_sb[:, :], AF.Ln, bias=0.0, scale=1.0), waits=[t_])

        t_ = P.op("V", lambda e: e.tensor_single_scalar(
            invf[:, :], iotac[:, :], 64.0, ALU.is_ge), waits=[tok_iota])
        t_ = P.op("V", lambda e: e.scalar_tensor_tensor(
            invf[:, :], invf[:, :], -64.0, iotac[:, :], ALU.mult, ALU.add),
            waits=[t_])
        tok_invf = P.op("A", lambda e: e.activation(
            invf[:, :], invf[:, :], AF.Exp, bias=0.0, scale=-LN1E6 / 64.0), waits=[t_])

        tok_e2 = P.op("G", lambda e: e.memset(ones1r[:, :], 1.0))

        t_ = P.op("G", lambda e: e.memset(e16[:, :], 1.0))
        t_ = P.op("G", lambda e: e.affine_select(
            e16[:, :], e16[:, :], pattern=[[1, 16 * 128]], compare_op=ALU.is_ge,
            fill=0.0, base=0, channel_multiplier=-128), waits=[t_])
        tok_e16 = P.op("G", lambda e: e.affine_select(
            e16[:, :], e16[:, :], pattern=[[-1, 16 * 128]], compare_op=ALU.is_ge,
            fill=0.0, base=127, channel_multiplier=128), waits=[t_])

        tok_ones = P.op("G", lambda e: e.memset(onesst[:, :], 1.0))

        t_ = P.op("G", lambda e: e.memset(lind[:, :], 0.0))
        tok_lind = t_
        for h in range(NH):
            tok_lind = P.op("G", lambda e, h=h: e.memset(
                lind[:, 16 * h + h:16 * h + h + 1], 1.0), waits=[tok_lind])

        t_ = P.op("G", lambda e: e.memset(ident[:, :], 1.0))
        tok_id = P.op("G", lambda e: e.affine_select(
            ident[:, :], ident[:, :], pattern=[[1, 128]], compare_op=ALU.is_equal,
            fill=0.0, base=0, channel_multiplier=-1), waits=[t_])

        t_ = P.op("G", lambda e: e.memset(tri[:, :], 1.0))
        _tt = P.op("G", lambda e: e.affine_select(
            tri[:, :], tri[:, :], pattern=[[1, 896]], compare_op=ALU.is_ge,
            fill=0.0, base=-384, channel_multiplier=-1), waits=[t_])
        tok_tri = [_tt] * 4

        def trim(m):
            return tri[:, 384 - 128 * m:896 - 128 * m]

        t_ = P.op("G", lambda e: e.iota(
            gcols[:, :], [[1, 512]], channel_multiplier=0,
            allow_small_or_imprecise_dtypes=True))
        tok_gcols = P.op("A", lambda e: e.activation(
            gcols[:, :], gcols[:, :], AF.Exp, bias=0.0, scale=lng_sb[:, :]),
            waits=[t_, tok_lng])

        tok_kstat0 = P.op("G", lambda e: e.memset(kstatr[:, :], 0.0))
        tok_sig0 = None
        tok_brow0 = P.op("G", lambda e: e.memset(brow[:, :], 0.0))

        bkD, bkV = banks[6], banks[7]
        tD = P.op("T", lambda e: e.matmul(
            bkD.ap[:, 0:NMC], ones1r[:, :], metars[0:1, 0:NMC],
            start=True, stop=True), waits=[tok_e2, tok_rs] + bkD.fill_waits())
        tV = P.op("T", lambda e: e.matmul(
            bkV.ap[:, 0:NMC], ones1r[:, :], metars[0:1, NMC:2 * NMC],
            start=True, stop=True), waits=[tok_e2, tok_rs] + bkV.fill_waits())
        tDm = P.op("V", lambda e: e.tensor_scalar(
            fs1[:, 0:NMC], bkD.ap[:, 0:NMC], iotac[:, :], None, ALU.subtract),
            waits=[tD, tok_iota])
        bkD.used(tDm)
        grow_tok = [None] * NH
        for h in range(NH):
            te = P.op("A", lambda e, h=h: e.activation(
                grow[:, NMC * h:NMC * (h + 1)], fs1[:, 0:NMC], AF.Exp, bias=0.0,
                scale=LNG[h]), waits=[tDm])
            tm = P.op("V", lambda e, h=h: e.tensor_tensor(
                grow[:, NMC * h:NMC * (h + 1)], grow[:, NMC * h:NMC * (h + 1)],
                bkV.ap[:, 0:NMC], ALU.mult), waits=[te, tV])
            bkV.used(tm)
            grow_tok[h] = tm

        def growcol(h, qv, s, kt):
            c = NMC * h + ((qv * 8 + s) * 4 + kt if s >= 0 else 64 + kt)
            return grow[:, c:c + 1]

        tok_cs = []
        for u in range(2):
            bA = banks[6]
            tb = P.op("T", lambda e, u=u: e.matmul(
                bA.ap[:, :], ones1r[:, :], metap[0:1, u * 512:(u + 1) * 512],
                start=True, stop=True),
                waits=[tok_e2, tok_pos] + bA.fill_waits())
            t1 = P.op("V", lambda e: e.tensor_scalar(
                fs2[:, :], bA.ap[:, :], invf[:, :], None, ALU.mult),
                waits=[tb, tok_invf])
            bA.used(t1)
            prev_act = None
            for trig, dst in (("cos", cosq[u]), ("sin", sinq[u])):
                shift = 0.0 if trig == "sin" else math.pi / 2.0
                t2 = P.op("V", lambda e, shift=shift: e.tensor_scalar(
                    fs3[:, :], fs2[:, :], shift, 1.0 / TWO_PI, ALU.add, ALU.mult),
                    waits=[t1, prev_act] + tok_cs)
                t3 = t2
                for hf in range(2):
                    t3a = P.op("V", lambda e, hf=hf: e.tensor_copy(
                        i32t[:, :], fs3[:, hf * 256:(hf + 1) * 256]), waits=[t3])
                    t3 = P.op("V", lambda e, hf=hf: e.tensor_copy(
                        fs1[:, hf * 256:(hf + 1) * 256], i32t[:, :]), waits=[t3a])
                t4a = P.op("V", lambda e, shift=shift: e.tensor_scalar(
                    fs3[:, :], fs2[:, :], shift, None, ALU.add), waits=[t2])
                t4 = P.op("V", lambda e: e.scalar_tensor_tensor(
                    fs3[:, :], fs1[:, :], -TWO_PI, fs3[:, :], ALU.mult, ALU.add),
                    waits=[t3, t4a])
                if trig == "cos":
                    t5 = P.op("A", lambda e, dst=dst: e.activation(
                        dst[:, :], fs3[:, :], AF.Sin, bias=0.0, scale=1.0),
                        waits=[t4, t3, t4a])
                else:
                    t5a = P.op("A", lambda e: e.activation(
                        fs1[:, :], fs3[:, :], AF.Sin, bias=0.0, scale=1.0),
                        waits=[t4, t3, t4a])
                    t5b = P.op("V", lambda e, dst=dst: e.tensor_copy(
                        dst[0:64, :], fs1[64:128, :]), waits=[t5a])
                    t5 = P.op("V", lambda e, dst=dst: e.tensor_copy(
                        dst[64:128, :], fs1[0:64, :]), waits=[t5b])
                prev_act = t5
                tok_cs.append(t5)
        if "rope" in taps:
            for i, tile in enumerate([cosq[0], sinq[0], cosq[1], sinq[1]]):
                tcp = P.op("V", lambda e, tile=tile: e.tensor_copy(
                    tapb[:, :], tile[:, :]), waits=tok_cs + tap_sl.fill_waits())
                td = P.dma("S", taps["rope"][i], tapb[:, :], waits=[tcp],
                           sem="dT")
                tap_sl.used(td)
        tok_sig0 = P.op("G", lambda e: e.memset(sigr[:, :], 0.0),
                        waits=[tok_cs[-1]] + tok_cs)

        def ln_stats(src, src_toks, rms, n, eps):
            bkS, bkQ = banks[5], banks[4]
            firstS = True
            firstQ = True
            prep = []
            for db in range(NDB):
                xb, sq = aslot[db], aslot[NDB + db]
                if not rms:
                    tc = P.op("V", lambda e, db=db, xb=xb: e.tensor_copy(
                        xb.ap, src(db)), waits=[src_toks(db)] + xb.fill_waits())
                    xb.ready = tc
                ts = P.op("A", lambda e, db=db, sq=sq: e.activation(
                    sq.ap, src(db), AF.Square, bias=0.0, scale=1.0),
                    waits=[src_toks(db)] + sq.fill_waits())
                sq.ready = ts
                prep.append((xb, sq))
            tS = None
            tQ = None
            for db in range(NDB):
                xb, sq = prep[db]
                last = db == NDB - 1
                if not rms:
                    w = [xb.ready, tok_ones] + (bkS.fill_waits() if firstS else [])
                    tS = P.op("T", lambda e, xb=xb, f=firstS, l=last: e.matmul(
                        bkS.ap[0:1, :], onesst[:, 0:1], xb.ap, start=f, stop=l),
                        waits=w, inc=True)
                    xb.used(tS)
                    firstS = False
                w = [sq.ready, tok_ones] + (bkQ.fill_waits() if firstQ else [])
                tQ = P.op("T", lambda e, sq=sq, f=firstQ, l=last: e.matmul(
                    bkQ.ap[0:1, :], onesst[:, 0:1], sq.ap, start=f, stop=l),
                    waits=w, inc=True)
                sq.used(tQ)
                firstQ = False
            stw = st8_sl.fill_waits()
            if rms:
                t1 = P.op("V", lambda e: e.memset(stc(0), 0.0), waits=stw)
            else:
                t1 = P.op("V", lambda e: e.tensor_single_scalar(
                    stc(0), bkS.ap[0:1, :], -1.0 / n, ALU.mult),
                    waits=[tS] + stw)
                bkS.used(t1)
            t2 = P.op("V", lambda e, eps=eps: e.tensor_scalar(
                stc(2), bkQ.ap[0:1, :], 1.0 / n, eps, ALU.mult, ALU.add),
                waits=[tQ] + stw)
            bkQ.used(t2)
            if not rms:
                t3 = P.op("V", lambda e: e.tensor_tensor(
                    stc(3), stc(0), stc(0), ALU.mult), waits=[t1])
                t2 = P.op("V", lambda e: e.tensor_tensor(
                    stc(2), stc(2), stc(3), ALU.subtract), waits=[t2, t3])
            t4 = P.op("A", lambda e: e.activation(
                stc(2), stc(2), AF.Sqrt, bias=0.0, scale=1.0), waits=[t2])
            t5 = P.op("V", lambda e: e.reciprocal(stc(1), stc(2)), waits=[t4])
            return (t1, t5)

        def ln_bcast(rows_toks):
            bM, bI = banks[5], banks[4]
            tM = P.op("T", lambda e: e.matmul(
                bM.ap[:, :], ones1r[:, :], stc(0), start=True, stop=True),
                waits=list(rows_toks) + [tok_e2] + bM.fill_waits())
            tI = P.op("T", lambda e: e.matmul(
                bI.ap[:, :], ones1r[:, :], stc(1), start=True, stop=True),
                waits=list(rows_toks) + [tok_e2] + bI.fill_waits())
            st8_sl.used(tM)
            st8_sl.used(tI)
            return tM, tI

        def ln_apply(dst, src, src_toks, wv, bv, tM, tI, extra=()):
            bM, bI = banks[5], banks[4]
            toks = []
            for db in range(NDB):
                t1 = P.op("V", lambda e, db=db: e.tensor_tensor(
                    fs1[:, :], src(db), bM.ap[:, :], ALU.add),
                    waits=[src_toks(db), tM] + list(extra))
                bM.used(t1)
                if wv is not None:
                    t2 = P.op("V", lambda e, db=db: e.scalar_tensor_tensor(
                        fs2[:, :], fs1[:, :], wv[:, db:db + 1], bI.ap[:, :],
                        ALU.mult, ALU.mult), waits=[t1, tI])
                    bI.used(t2)
                    t2 = P.op("V", lambda e, db=db: e.tensor_scalar(
                        dst(db), fs2[:, :], bv[:, db:db + 1], None, ALU.add),
                        waits=[t2])
                else:
                    t2 = P.op("V", lambda e, db=db: e.tensor_tensor(
                        dst(db), fs1[:, :], bI.ap[:, :], ALU.mult), waits=[t1, tI])
                    bI.used(t2)
                toks.append(t2)
            return toks

        def rms_apply(dst, src, src_toks, wv, tI):
            bI = banks[4]
            toks = []
            for db in range(NDB):
                t2 = P.op("V", lambda e, db=db: e.scalar_tensor_tensor(
                    dst(db), src(db), wv[:, db:db + 1], bI.ap[:, :],
                    ALU.mult, ALU.mult), waits=[src_toks(db), tI])
                bI.used(t2)
                toks.append(t2)
            return toks

        def proj_pass(W, Ktot, O, movings, evac, g_oc=2):
            NKB = ceil_div(Ktot, 128)
            NOC = ceil_div(O, 128)
            nl = len(movings)
            assert g_oc * nl <= 4
            ocs = list(range(NOC))
            ogs = [ocs[i:i + g_oc] for i in range(0, NOC, g_oc)]
            for ogi, og in enumerate(ogs):
                base = 0 if ogi % 2 == 0 else 4
                c0 = og[0] * 128
                c1 = min(og[-1] * 128 + 128, O)
                width = c1 - c0
                mm_tok = {}
                for kb in range(NKB):
                    r0 = kb * 128
                    rows = min(128, Ktot - r0)
                    slot = wring.next()
                    wt = P.dma("G", slot.ap[0:rows, 0:width],
                               W[r0:r0 + rows, c0:c1], waits=slot.fill_waits(),
                               sem=slot.qsem)
                    slot.ready = wt
                    for oi, oc in enumerate(og):
                        w0 = oc * 128 - c0
                        w1 = min(w0 + 128, width)
                        for li, (tiles, ttok) in enumerate(movings):
                            bk = banks[base + oi * nl + li]
                            first = kb == 0
                            last = kb == NKB - 1
                            kb_final = (oi == len(og) - 1) and (li == nl - 1)
                            waits = [wt, ttok(kb)]
                            if first:
                                waits += bk.fill_waits()
                            tok = P.op("T", lambda e, bk=bk, slot=slot, rows=rows,
                                       w0=w0, w1=w1, tiles=tiles, kb=kb,
                                       first=first, last=last: e.matmul(
                                           bk.ap[0:(w1 - w0), :],
                                           slot.ap[0:rows, w0:w1],
                                           tiles(kb)[0:rows, :],
                                           start=first, stop=last),
                                       waits=waits, inc=(last or kb_final))
                            if kb_final:
                                slot.used(tok)
                            if last:
                                mm_tok[(oi, li)] = tok
                for oi, oc in enumerate(og):
                    ow = min(oc * 128 + 128, O) - oc * 128
                    for li in range(nl):
                        bk = banks[base + oi * nl + li]
                        evac(li, oc, bk, mm_tok[(oi, li)], ow)

        def vproj_pass(W, movings, evac):
            nl = len(movings)
            assert nl == 1
            tiles, ttok = movings[0]
            for og in range(4):
                base = 0 if og % 2 == 0 else 4
                mm_tok = {}
                for kb in range(NDB):
                    slot = wring.next()
                    wt = P.dma("G", slot.ap[:, :],
                               W[kb * 128:(kb + 1) * 128, og * 512:(og + 1) * 512],
                               waits=slot.fill_waits(), sem=slot.qsem)
                    slot.ready = wt
                    for tt in range(4):
                        bk = banks[base + tt]
                        first = kb == 0
                        last = kb == NDB - 1
                        waits = [wt, ttok(kb)]
                        if first:
                            waits += bk.fill_waits()
                        tok = P.op("T", lambda e, bk=bk, slot=slot, tiles=tiles,
                                   kb=kb, tt=tt, first=first, last=last: e.matmul(
                                       bk.ap[:, :],
                                       tiles(kb)[:, tt * 128:(tt + 1) * 128],
                                       slot.ap[:, :], start=first, stop=last),
                                   waits=waits, inc=(last or tt == 3))
                        if tt == 3:
                            slot.used(tok)
                        if last:
                            mm_tok[tt] = tok
                for tt in range(4):
                    evac(0, tt, og, banks[base + tt], mm_tok[tt])

        def tap_fm(name, db, u, src_ap, wtok, srcslot=None):
            if name in taps:
                t = P.op("V", lambda e: e.tensor_copy(tapb[:, :], src_ap),
                         waits=[wtok] + tap_sl.fill_waits())
                td = P.dma("S", taps[name][db, :, u * CH:(u + 1) * CH],
                           tapb[:, :], waits=[t], sem="dT")
                tap_sl.used(td)
                if srcslot is not None:
                    srcslot.used(t)

        htok = [[None, None] for _ in range(NDB)]
        for u in range(2):
            rt = ln_stats(lambda db, u=u: xt(db, u), lambda db, u=u: xtok[db][u],
                          rms=False, n=D, eps=EPS1)
            tM, tI = ln_bcast(rt)
            toks = ln_apply(lambda db, u=u: ht(db, u), lambda db, u=u: xt(db, u),
                            lambda db, u=u: xtok[db][u], vecs["n1w"], vecs["n1b"],
                            tM, tI, extra=[vtok["n1w"], vtok["n1b"]])
            for db in range(NDB):
                htok[db][u] = toks[db]
                tap_fm("h1", db, u, ht(db, u), toks[db])

        cc_tok = [None] * 4

        def kv_wave(u, blk):
            gi = (0 if blk == 1 else 2) + u
            gsem = f"dG{gi}"
            statrow_toks = []
            tz = P.op("V", lambda e: e.memset(aslot[32].ap, 0.0),
                      waits=aslot[32].fill_waits())
            aslot[32].ready = tz

            def kevac(li, oc, bk, mmtok, ow=128):
                s = aslot[oc]
                if blk == 1:
                    t = P.op("V", lambda e, bk=bk, s=s: e.tensor_single_scalar(
                        s.ap, bk.ap[:, :], CINV, ALU.mult),
                        waits=[mmtok] + s.fill_waits())
                else:
                    t0 = P.op("V", lambda e, bk=bk, oc=oc: e.tensor_scalar(
                        fs1[:, :], bk.ap[:, :], CINV, vecs["bkc2"][:, oc:oc + 1],
                        ALU.mult, ALU.add), waits=[mmtok, vtok["bkc2"]])
                    t = rope_apply(u, fs1, s, [t0] + s.fill_waits())
                bk.used(t)
                s.ready = t
                if blk == 2:
                    statrow_toks.append(k2_stats(oc, s))
                tap_fm("k1" if blk == 1 else "k2", oc, u, s.ap, t, srcslot=s)

            def k2_stats(oc, s):
                sq = bounce.next()
                ts = P.op("A", lambda e, s=s, sq=sq: e.activation(
                    sq.ap, s.ap, AF.Square, bias=0.0, scale=1.0),
                    waits=[s.ready] + sq.fill_waits())
                sq.ready = ts
                s.used(ts)
                bk = banks[3]
                tm = P.op("T", lambda e, sq=sq, bk=bk: e.matmul(
                    bk.ap[0:1, :], onesst[:, 0:1], sq.ap, start=True, stop=True),
                    waits=[ts, tok_ones] + bk.fill_waits())
                sq.used(tm)
                t1 = P.op("V", lambda e, bk=bk: e.reduce_max(
                    st8f[0:1, 2080:2081], bk.ap[0:1, :], axis=AX.X), waits=[tm])
                bk.used(t1)
                t2 = P.op("A", lambda e: e.activation(
                    st8f[0:1, 2080:2081], st8f[0:1, 2080:2081], AF.Sqrt,
                    bias=0.0, scale=1.0), waits=[t1])
                t3 = P.op("V", lambda e, oc=oc: e.tensor_single_scalar(
                    st8f[0:1, 2048 + oc:2049 + oc], st8f[0:1, 2080:2081],
                    1.01, ALU.mult), waits=[t2])
                return t3

            wname = "a_wk" if blk == 1 else "f_wk"
            mov = [(lambda kb, u=u: ht(kb, u), lambda kb, u=u: htok[kb][u])]
            proj_pass(dt[wname], D, D, mov, kevac, g_oc=3)
            if blk == 2:
                tb = P.op("V", lambda e: e.tensor_copy(
                    aslot[32].ap[0:1, 0:16], st8f[0:1, 2048:2064]),
                    waits=statrow_toks + [tz])
                aslot[32].ready = tb

            def vevac(li, tt, og, bk, mmtok):
                s = aslot[16 + tt * 4 + og]
                t = P.op("V", lambda e, bk=bk, s=s: e.tensor_copy(
                    s.ap, bk.ap[:, :]), waits=[mmtok] + s.fill_waits())
                bk.used(t)
                s.ready = t
                if ("v1" if blk == 1 else "v2") in taps:
                    nm = "v1" if blk == 1 else "v2"
                    tcp = P.op("V", lambda e, s=s: e.tensor_copy(tapb[:, :], s.ap),
                               waits=[t] + tap_sl.fill_waits())
                    s.used(tcp)
                    tdd = P.dma("S", taps[nm][u, tt, :, og * 512:(og + 1) * 512],
                                tapb[:, :], waits=[tcp], sem="dT")
                    tap_sl.used(tdd)

            vproj_pass(dt["a_wv" if blk == 1 else "f_wv"], mov, vevac)
            sready = [aslot[i].ready for i in range(33)]
            tst = P.dma("S", gin[gi].ap(), arena[:, 0:GW], waits=sready, sem=gsem)
            for i in range(33):
                aslot[i].used(tst)
            cc_tok[gi] = P.emit("G", lambda e, gi=gi: e.collective_compute(
                "AllGather", ALU.bypass,
                replica_groups=[[0, 1, 2, 3], [4, 5, 6, 7]],
                ins=[gin[gi].ap().opt()], outs=[gout[gi].ap().opt()]),
                waits=[tst], inc="cc")

        def rope_apply(u, src_f32, dstslot, waits):
            w = list(waits) + [tok_cs[-1]]
            ta = P.op("V", lambda e: e.tensor_tensor(
                fs2[:, :], src_f32[:, :], cosq[u][:, :], ALU.mult), waits=w)
            tb0 = P.op("V", lambda e: e.tensor_tensor(
                fs3[0:64, :], src_f32[64:128, :], sinq[u][64:128, :], ALU.mult),
                waits=w)
            tb1 = P.op("V", lambda e: e.tensor_tensor(
                fs3[64:128, :], src_f32[0:64, :], sinq[u][0:64, :], ALU.mult),
                waits=w)
            t0 = P.op("V", lambda e, d=dstslot: e.tensor_tensor(
                d.ap[0:64, :], fs2[0:64, :], fs3[0:64, :], ALU.subtract),
                waits=[ta, tb0] + dstslot.fill_waits())
            t1 = P.op("V", lambda e, d=dstslot: e.tensor_tensor(
                d.ap[64:128, :], fs2[64:128, :], fs3[64:128, :], ALU.add),
                waits=[ta, tb1])
            return t1

        kv_wave(0, 1)
        kv_wave(1, 1)

        def qevac1(li, oc, bk, mmtok, ow=128):
            s = aslot[li * 16 + oc]
            t = P.op("V", lambda e, bk=bk, s=s: e.tensor_copy(s.ap, bk.ap[:, :]),
                     waits=[mmtok] + s.fill_waits())
            bk.used(t)
            s.ready = t
            tap_fm("q1", oc, li, s.ap, t, srcslot=s)

        proj_pass(dt["a_wq"], D, D,
                  [(lambda kb: ht(kb, 0), lambda kb: htok[kb][0]),
                   (lambda kb: ht(kb, 1), lambda kb: htok[kb][1])],
                  qevac1, g_oc=2)
        for li in range(2):
            rd = [aslot[li * 16 + i].ready for i in range(16)]
            tq = P.dma("S", qspill[li], arena[:, li * 16 * 512:(li + 1) * 16 * 512],
                       waits=rd, sem="dQS")
            for i in range(16):
                aslot[li * 16 + i].used(tq)
        tok_qs1 = P.full("dQS")

        def gevac(li, oc, bk, mmtok, ow=128):
            s = aslot[li * 16 + oc]
            t = P.op("A", lambda e, bk=bk, s=s: e.activation(
                s.ap, bk.ap[:, :], AF.Silu, bias=0.0, scale=1.0),
                waits=[mmtok] + s.fill_waits())
            bk.used(t)
            s.ready = t
            tap_fm("hg", oc, li, s.ap, t, srcslot=s)

        proj_pass(dt["a_wg"], D, D,
                  [(lambda kb: ht(kb, 0), lambda kb: htok[kb][0]),
                   (lambda kb: ht(kb, 1), lambda kb: htok[kb][1])],
                  gevac, g_oc=2)
        for li in range(2):
            rd = [aslot[li * 16 + i].ready for i in range(16)]
            thg = P.dma("S", hgspill[li],
                        arena[:, li * 16 * 512:(li + 1) * 16 * 512],
                        waits=rd, sem="dHS")
            for i in range(16):
                aslot[li * 16 + i].used(thg)
        tok_hs = P.full("dHS")

        if stage <= 1:
            P.join()
            finish_out_dummy(P, y_out, x_sb, xtok)
            return nc, dict(taps=taps)

        kring = Ring([Slot(arena[:, 0:16 * 512]), Slot(arena[:, 16 * 512:32 * 512])])
        vring = Ring([Slot(arena[:, 32 * 512:36 * 512]),
                      Slot(arena[:, 36 * 512:40 * 512])])
        qring = Ring([Slot(qtl[:, i * 512:(i + 1) * 512]) for i in range(6)])
        for i, s in enumerate(kring.slots):
            s.qsem = f"dKR{i}"
        for i, s in enumerate(vring.slots):
            s.qsem = f"dVR{i}"
        for i, s in enumerate(qring.slots):
            s.qsem = f"dQR{i}"
        for i, s in enumerate(bounce.slots):
            s.qsem = f"dBN{i}"
        ptr = Ring([Slot(ptile[:, 0:512]), Slot(ptile[:, 512:1024])])

        def attn_mixer(blk, ztok):
            gbase = 0 if blk == 1 else 2
            kst_tok = {}
            for qv in range(2):
                iters = [("diag", qv)] + [("slot", s) for s in
                                          range(4 if qv == 0 else 8)]
                if blk == 2:
                    for s in (range(4) if qv == 0 else range(4, 8)):
                        gg, row = (gbase + (0 if s < 4 else 1)), s % 4
                        P.dma("S", kstatr[0:1, s * 16:(s + 1) * 16],
                              gout[gg][row * 128:row * 128 + 1,
                                       KW + VW:KW + VW + 16],
                              waits=[cc_tok[gg], tok_kstat0],
                              sem=f"dKS{qv}")
                    kst_tok[qv] = [P.full(f"dKS{qv}")]
                    tb1 = P.op("V", lambda e, qv=qv: e.tensor_tensor(
                        mrow[0:1, 0:128], kstatr[0:1, :],
                        blerow[0:1, 128 * qv:128 * (qv + 1)], ALU.mult),
                        waits=kst_tok.get(0, []) + kst_tok.get(1, []) + [tok_ble])
                    tmx = P.op("V", lambda e: e.tensor_copy(
                        mrow[0:1, 272:288], mrow[0:1, 0:16]), waits=[tb1])
                    for s in range(1, 8):
                        tmx = P.op("V", lambda e, s=s: e.tensor_tensor(
                            mrow[0:1, 272:288], mrow[0:1, 272:288],
                            mrow[0:1, s * 16:(s + 1) * 16], ALU.max), waits=[tmx])
                    maxk_tok = tmx
                    bkP = banks[6]
                    tP = P.op("T", lambda e, qv=qv: e.matmul(
                        bkP.ap[:, 0:8], ones1r[:, :],
                        metapad[0:1, 8 * qv:8 * (qv + 1)], start=True, stop=True),
                        waits=[tok_e2, tok_pad] + bkP.fill_waits())
                    tPc = P.op("V", lambda e: e.tensor_copy(
                        sigb[:, 0:8], bkP.ap[:, 0:8]), waits=[tP])
                    bkP.used(tPc)
                hgroups = ([list(range(0, 6)), list(range(6, 12)),
                            list(range(12, 16))] if blk == 1 else
                           [list(range(0, 5)), list(range(5, 10)),
                            list(range(10, 15)), [15]])
                for hg_i, hgrp in enumerate(hgroups):
                    ctx_banks = {h: banks[i] for i, h in enumerate(hgrp)}
                    lbank = banks[5] if blk == 2 else None
                    qslots = {}
                    bias_sl = {}
                    if blk == 2:
                        btoks = {}
                    for h in hgrp:
                        qs = qring.next()
                        tq = P.dma("S", qs.ap,
                                   qspill[qv][:, h * 512:(h + 1) * 512],
                                   waits=qs.fill_waits(), sem=qs.qsem)
                        qs.ready = tq
                        qslots[h] = qs
                        if blk == 2:
                            sq = bounce.next()
                            ts = P.op("A", lambda e, qs=qs, sq=sq: e.activation(
                                sq.ap, qs.ap, AF.Square, bias=0.0, scale=1.0),
                                waits=[tq] + sq.fill_waits())
                            sq.ready = ts
                            qs.used(ts)
                            bkq = banks[6]
                            tm = P.op("T", lambda e, sq=sq, bkq=bkq: e.matmul(
                                bkq.ap[0:1, :], onesst[:, 0:1], sq.ap,
                                start=True, stop=True),
                                waits=[ts, tok_ones] + bkq.fill_waits())
                            sq.used(tm)
                            t1 = P.op("V", lambda e, bkq=bkq: e.reduce_max(
                                st8f[0:1, 2081:2082], bkq.ap[0:1, :], axis=AX.X),
                                waits=[tm])
                            bkq.used(t1)
                            t2 = P.op("A", lambda e: e.activation(
                                st8f[0:1, 2081:2082], st8f[0:1, 2081:2082], AF.Sqrt, bias=0.0,
                                scale=1.0), waits=[t1])
                            t3 = P.op("V", lambda e: e.tensor_single_scalar(
                                st8f[0:1, 2081:2082], st8f[0:1, 2081:2082], -1.01, ALU.mult),
                                waits=[t2])
                            t4 = P.op("V", lambda e, h=h: e.tensor_tensor(
                                brow[0:1, h:h + 1], st8f[0:1, 2081:2082],
                                mrow[0:1, 272 + h:273 + h], ALU.mult),
                                waits=[t3, maxk_tok, tok_brow0] +
                                brow_sl.fill_waits())
                            btoks[h] = t4
                    if blk == 2:
                        bkB = banks[6]
                        bbs = bbase_slot
                        tB = P.op("T", lambda e: e.matmul(
                            bkB.ap[:, 0:16], ones1r[:, :], brow[0:1, :],
                            start=True, stop=True),
                            waits=list(btoks.values()) + [tok_e2] +
                            bkB.fill_waits())
                        brow_sl.used(tB)
                        tBc = P.op("V", lambda e: e.tensor_copy(
                            bbase[:, :], bkB.ap[:, 0:16]),
                            waits=[tB] + bbs.fill_waits())
                        bkB.used(tBc)
                        bbs.ready = tBc
                        tsh = P.op("V", lambda e: e.tensor_single_scalar(
                            brow[0:1, :], brow[0:1, :], 14.0, ALU.add),
                            waits=list(btoks.values()) + brow_sl.fill_waits())
                        btoks["shift"] = tsh
                        if "att2b" in taps and qv == 0 and hg_i == 0:
                            tb0 = P.op("V", lambda e: e.tensor_copy(
                                tapb[:, 0:16], bbase[:, :]),
                                waits=[tBc] + tap_sl.fill_waits())
                            tbd = P.dma("S", taps["att2b"][:, :], tapb[:, 0:16],
                                        waits=[tb0], sem="dT")
                            tap_sl.used(tbd)
                        if "att2k" in taps and qv == 0 and hg_i == 0:
                            tk0 = P.op("V", lambda e: e.tensor_copy(
                                tapb[0:1, 0:128], kstatr[0:1, :]),
                                waits=[maxk_tok] + tap_sl.fill_waits())
                            tkd = P.dma("S", taps["att2k"][:, :], tapb[0:1, 0:128],
                                        waits=[tk0], sem="dT")
                            tap_sl.used(tkd)
                        bas = biasall_slot
                        btk = None
                        for hi, h in enumerate(hgrp):
                            btk = P.op("V", lambda e, hi=hi, h=h: e.tensor_scalar(
                                basb[:, hi * 8:(hi + 1) * 8], sigb[:, 0:8], -1.0,
                                bbase[:, h:h + 1], ALU.mult, ALU.add),
                                waits=[tPc, tBc] +
                                (bas.fill_waits() if hi == 0 else []))
                        bas.ready = btk
                    first_mm = {h: True for h in hgrp}
                    lfirst = [True]
                    ltok = None
                    tiles_sched = []
                    for it in iters:
                        for kt in range(4):
                            for h in hgrp:
                                tiles_sched.append((it, kt, h))
                    n_t = len(tiles_sched)
                    k_seq = []
                    v_seq = []
                    for it in iters:
                        k_seq.append(it)
                        for kt in range(4):
                            v_seq.append((it, kt))
                    kmap = {}
                    vmap = {}
                    kn = [0]
                    vn = [0]

                    def ksrc_of(it):
                        if it[0] == "diag":
                            return gin[gbase + qv], [cc_tok[gbase + qv]]
                        s = it[1]
                        gg = gbase + (0 if s < 4 else 1)
                        row = s % 4
                        return gout[gg][row * 128:(row + 1) * 128, :], [cc_tok[gg]]

                    def ensure_k(idx):
                        while kn[0] <= idx and kn[0] < len(k_seq):
                            it = k_seq[kn[0]]
                            ksrc, kwait = ksrc_of(it)
                            ks = kring.next()
                            tk = P.dma("S", ks.ap, ksrc[:, 0:KW],
                                       waits=kwait + ks.fill_waits(), sem=ks.qsem)
                            ks.ready = tk
                            kmap[id(it)] = ks
                            kn[0] += 1

                    def ensure_v(idx):
                        while vn[0] <= idx and vn[0] < len(v_seq):
                            it, kt = v_seq[vn[0]]
                            ksrc, kwait = ksrc_of(it)
                            v = vring.next()
                            tv = P.dma("S", v.ap,
                                       ksrc[:, KW + kt * D:KW + (kt + 1) * D],
                                       waits=kwait + v.fill_waits(), sem=v.qsem)
                            v.ready = tv
                            vmap[(id(it), kt)] = v
                            vn[0] += 1
                    pend = None
                    ltok = None

                    def stage_a(j):
                        it, kt, h = tiles_sched[j]
                        ki = k_seq.index(it)
                        vi = v_seq.index((it, kt))
                        ensure_k(ki + 1)
                        ensure_v(vi if h == hgrp[0] else vi + 1)
                        ks = kmap[id(it)]
                        v = vmap[(id(it), kt)]

                        qs = qslots[h]
                        sbk = banks[6 + (j % 2)]
                        tmm = P.op("T", lambda e, sbk=sbk, ks=ks, h=h, kt=kt,
                                   qs=qs: e.matmul(
                                       sbk.ap[:, :],
                                       ks.ap[:, h * 512 + kt * 128:
                                             h * 512 + (kt + 1) * 128],
                                       qs.ap, start=True, stop=True),
                                   waits=[ks.ready, qs.ready] + sbk.fill_waits(),
                                   inc=True)
                        ks.used(tmm)
                        qs.used(tmm)
                        return (it, kt, h, sbk, tmm, v)

                    def stage_b(state, jb):
                        nonlocal ltok
                        it, kt, h, sbk, tmm, v = state
                        hi = hgrp.index(h)
                        pt = ptr.next()
                        if blk == 1:
                            rs = growcol(h, qv, it[1] if it[0] == "slot" else -1,
                                         kt)
                            tev = P.op("A", lambda e, sbk=sbk, pt=pt,
                                       rs=rs: e.activation(
                                           pt.ap, sbk.ap[:, :], AF.Copy,
                                           bias=0.0, scale=rs),
                                       waits=[tmm, grow_tok[h]] + pt.fill_waits())
                        else:
                            if it[0] == "diag":
                                bias = bbase[:, h:h + 1]
                                bwait = [bbase_slot.ready]
                                bsl = bbase_slot
                            else:
                                bias = basb[:, hi * 8 + it[1]:hi * 8 + it[1] + 1]
                                bwait = [biasall_slot.ready]
                                bsl = biasall_slot
                            tev = P.op("A", lambda e, sbk=sbk, pt=pt,
                                       bias=bias: e.activation(
                                           pt.ap, sbk.ap[:, :], AF.Exp,
                                           bias=bias, scale=1.0),
                                       waits=[tmm] + bwait + pt.fill_waits())
                            bsl.used(tev)
                        sbk.used(tev)
                        pt.ready = tev
                        if it[0] == "diag":
                            tev = P.op("V", lambda e, pt=pt, kt=kt: e.tensor_tensor(
                                pt.ap, pt.ap, trim(kt), ALU.mult),
                                waits=[tev, tok_tri[kt]])
                            pt.ready = tev
                        if blk == 2:
                            llast = jb == n_t - 1
                            tl = P.op("T", lambda e, pt=pt, h=h, lf=lfirst[0],
                                      ll=llast: e.matmul(
                                          lbank.ap[0:16, :],
                                          lind[:, 16 * h:16 * (h + 1)],
                                          pt.ap, start=lf, stop=ll),
                                      waits=[pt.ready, tok_lind] +
                                      (lbank.fill_waits() if lfirst[0] else []),
                                      inc=True)
                            lfirst[0] = False
                            pt.used(tl)
                            if llast:
                                ltok = tl
                        cbk = ctx_banks[h]
                        clast = (it is iters[-1]) and kt == 3
                        tc = P.op("T", lambda e, cbk=cbk, pt=pt, h=h, kt=kt, v=v,
                                  cf=first_mm[h], cl=clast: e.matmul(
                                      cbk.ap[:, :],
                                      v.ap[:, h * 128:(h + 1) * 128],
                                      pt.ap, start=cf, stop=cl),
                                  waits=[pt.ready, v.ready] +
                                  (cbk.fill_waits() if first_mm[h] else []),
                                  inc=True)
                        first_mm[h] = False
                        pt.used(tc)
                        v.used(tc)
                        if clast:
                            ctx_banks[h] = (cbk, tc)

                    for j in range(n_t + 1):
                        if j < n_t:
                            st = stage_a(j)
                        if j > 0:
                            stage_b(pend, j - 1)
                        if j < n_t:
                            pend = st
                    if blk == 2:
                        if "att2l" in taps and qv == 0 and hg_i == 0:
                            tl0 = P.op("V", lambda e: e.tensor_copy(
                                tapb[0:16, :], lbank.ap[0:16, :]),
                                waits=[ltok] + tap_sl.fill_waits())
                            tld = P.dma("S", taps["att2l"][:, :], tapb[0:16, :],
                                        waits=[tl0], sem="dT")
                            tap_sl.used(tld)
                            lbank.used(tl0)
                        trc = P.op("V", lambda e: e.tensor_single_scalar(
                            st16[:, :], lbank.ap[0:16, :], 1e-11, ALU.add),
                            waits=[ltok] + st16_sl.fill_waits())
                        lbank.used(trc)
                        trc = P.op("V", lambda e: e.reciprocal(
                            st16[:, :], st16[:, :]), waits=[trc])
                    else:
                        trc = None
                    for h in hgrp:
                        cbk, ctok = ctx_banks[h]
                        if blk == 1:
                            fin_retention(h, qv, cbk, ctok, ztok)
                        else:
                            fin_softmax(h, qv, cbk, ctok, trc, ztok)

        def fin_retention(h, qv, cbk, ctok, ztok):
            t0 = P.op("V", lambda e, cbk=cbk: e.tensor_copy(fs1[:, :], cbk.ap[:, :]),
                      waits=[ctok])
            cbk.used(t0)
            tg = P.op("T", lambda e, h=h, cbk=cbk: e.matmul(
                cbk.ap[:, :], e16[:, 128 * h:128 * (h + 1)], gcols[0:16, :],
                start=True, stop=True), waits=[t0, tok_e16, tok_gcols])
            t1 = P.op("V", lambda e, cbk=cbk: e.tensor_tensor(
                fs1[:, :], fs1[:, :], cbk.ap[:, :], ALU.mult), waits=[tg])
            cbk.used(t1)
            b1, b2 = bounce.next(), bounce.next()
            tb = P.op("V", lambda e, b1=b1: e.tensor_copy(b1.ap, fs1[:, :]),
                      waits=[t1] + b1.fill_waits())
            b1.ready = tb
            tsq = P.op("A", lambda e, b2=b2: e.activation(
                b2.ap, fs1[:, :], AF.Square, bias=0.0, scale=1.0),
                waits=[t1] + b2.fill_waits())
            b2.ready = tsq
            tm1 = P.op("T", lambda e, b1=b1, cbk=cbk: e.matmul(
                cbk.ap[0:1, :], onesst[:, 0:1], b1.ap, start=True, stop=True),
                waits=[tb, tok_ones, t1], inc=True)
            b1.used(tm1)
            t2 = P.op("V", lambda e, cbk=cbk: e.tensor_single_scalar(
                stc(0), cbk.ap[0:1, :], -1.0 / DHD, ALU.mult),
                waits=[tm1] + st8_sl.fill_waits())
            cbk.used(t2)
            tm2 = P.op("T", lambda e, b2=b2, cbk=cbk: e.matmul(
                cbk.ap[0:1, :], onesst[:, 0:1], b2.ap, start=True, stop=True),
                waits=[tsq, t2], inc=True)
            b2.used(tm2)
            t3 = P.op("V", lambda e, cbk=cbk: e.tensor_scalar(
                stc(2), cbk.ap[0:1, :], 1.0 / DHD, EPS1, ALU.mult, ALU.add),
                waits=[tm2])
            cbk.used(t3)
            t4 = P.op("V", lambda e: e.tensor_tensor(
                stc(3), stc(0), stc(0), ALU.mult), waits=[t2])
            t5 = P.op("V", lambda e: e.tensor_tensor(
                stc(2), stc(2), stc(3), ALU.subtract), waits=[t3, t4])
            t6 = P.op("A", lambda e: e.activation(
                stc(2), stc(2), AF.Sqrt, bias=0.0, scale=1.0), waits=[t5])
            t7 = P.op("V", lambda e: e.reciprocal(stc(1), stc(2)), waits=[t6])
            tM = P.op("T", lambda e, cbk=cbk: e.matmul(
                cbk.ap[:, :], ones1r[:, :], stc(0), start=True, stop=True),
                waits=[t2, t7, tok_e2])
            st8_sl.used(tM)
            t8 = P.op("V", lambda e, cbk=cbk: e.tensor_tensor(
                fs1[:, :], fs1[:, :], cbk.ap[:, :], ALU.add), waits=[tM])
            cbk.used(t8)
            tI = P.op("T", lambda e, cbk=cbk: e.matmul(
                cbk.ap[:, :], ones1r[:, :], stc(1), start=True, stop=True),
                waits=[t8, tok_e2])
            st8_sl.used(tI)
            hgs = bounce.next()
            thg = P.dma("S", hgs.ap, hgspill[qv][:, h * 512:(h + 1) * 512],
                        waits=hgs.fill_waits() + [tok_hs], sem=hgs.qsem)
            hgs.ready = thg
            t9 = P.op("V", lambda e, cbk=cbk: e.tensor_tensor(
                fs1[:, :], fs1[:, :], cbk.ap[:, :], ALU.mult), waits=[tI])
            cbk.used(t9)
            t10 = P.op("V", lambda e, h=h, qv=qv, hgs=hgs: e.tensor_tensor(
                ht(h, qv), fs1[:, :], hgs.ap, ALU.mult), waits=[t9, thg])
            hgs.used(t10)
            ztok[h][qv] = t10
            tap_fm("retz", h, qv, ht(h, qv), t10)

        def fin_softmax(h, qv, cbk, ctok, trc, ztok):
            rb = banks[7] if cbk is not banks[7] else banks[6]
            tR = P.op("T", lambda e, h=h, rb=rb: e.matmul(
                rb.ap[:, :], e16[:, 128 * h:128 * (h + 1)], st16[0:16, :],
                start=True, stop=True),
                waits=[trc, tok_e16] + rb.fill_waits())
            st16_sl.used(tR)
            t0 = P.op("V", lambda e, cbk=cbk: e.tensor_copy(fs1[:, :], cbk.ap[:, :]),
                      waits=[ctok])
            cbk.used(t0)
            t1 = P.op("V", lambda e, rb=rb: e.tensor_tensor(
                fs1[:, :], fs1[:, :], rb.ap[:, :], ALU.mult), waits=[t0, tR])
            rb.used(t1)
            t2 = P.op("V", lambda e, h=h, qv=qv: e.tensor_scalar(
                ht(h, qv), fs1[:, :], vecs["bv2"][:, h:h + 1], None, ALU.add),
                waits=[t1, tok_vecs])
            ztok[h][qv] = t2
            tap_fm("ctx2", h, qv, ht(h, qv), t2)

        P.join()
        ztok = [[None, None] for _ in range(NH)]
        attn_mixer(1, ztok)

        def woevac1(li, oc, bk, mmtok, ow=128):
            t = P.op("V", lambda e, oc=oc, li=li, bk=bk: e.tensor_tensor(
                xt(oc, li), xt(oc, li), bk.ap[:, :], ALU.add),
                waits=[mmtok, xtok[oc][li]])
            bk.used(t)
            xtok[oc][li] = t
            tap_fm("xwo", oc, li, xt(oc, li), t)

        proj_pass(dt["a_wo"], D, D,
                  [(lambda kb: ht(kb, 0), lambda kb: ztok[kb][0]),
                   (lambda kb: ht(kb, 1), lambda kb: ztok[kb][1])],
                  woevac1, g_oc=2)

        if stage <= 2:
            P.join()
            finish_out_dummy(P, y_out, x_sb, xtok)
            return nc, dict(taps=taps)

        P.join()
        h2tok = [[None, None] for _ in range(NDB)]
        xmid_tok = [[None, None] for _ in range(NDB)]
        for u in range(2):
            rt = ln_stats(lambda db, u=u: xt(db, u), lambda db, u=u: xtok[db][u],
                          rms=False, n=D, eps=EPS1)
            tM, tI = ln_bcast(rt)
            toks = ln_apply(lambda db, u=u: ht(db, u), lambda db, u=u: xt(db, u),
                            lambda db, u=u: xtok[db][u], vecs["n2w"], vecs["n2b"],
                            tM, tI, extra=[tok_vecs])
            for db in range(NDB):
                h2tok[db][u] = toks[db]
                tap_fm("h2", db, u, ht(db, u), toks[db])
            utok = [None] * NKB1
            usl = [aslot[i] for i in range(NKB1)]

            def w1evac(li, oc, bk, mmtok, ow=128):
                s = usl[oc]
                t = P.op("A", lambda e, bk=bk, s=s, ow=ow: e.activation(
                    s.ap[0:ow, :], bk.ap[0:ow, :], AF.Silu, bias=0.0, scale=1.0),
                    waits=[mmtok] + s.fill_waits())
                bk.used(t)
                s.ready = t

            mv = [(lambda kb, u=u: ht(kb, u), lambda kb, u=u: h2tok[kb][u])]
            proj_pass(dt["a_w1"], D, HID1, mv, w1evac, g_oc=4)

            def w2evac(li, oc, bk, mmtok, ow=128):
                s = usl[oc]
                t = P.op("V", lambda e, bk=bk, s=s, ow=ow: e.tensor_tensor(
                    s.ap[0:ow, :], s.ap[0:ow, :], bk.ap[0:ow, :], ALU.mult),
                    waits=[mmtok, s.ready])
                bk.used(t)
                s.ready = t
                utok[oc] = t

            proj_pass(dt["a_w2"], D, HID1, mv, w2evac, g_oc=4)

            def w3evac(li, oc, bk, mmtok, ow=128):
                t = P.op("V", lambda e, oc=oc, u=u, bk=bk: e.tensor_tensor(
                    xt(oc, u), xt(oc, u), bk.ap[:, :], ALU.add),
                    waits=[mmtok, xtok[oc][u]])
                bk.used(t)
                xtok[oc][u] = t

            proj_pass(dt["a_w3"], HID1, D,
                      [(lambda kb: usl[kb].ap, lambda kb: utok[kb])],
                      w3evac, g_oc=4)
            for kb in range(NKB1):
                usl[kb].used(utok[kb])

            for db in range(NDB):
                sl = fsl2 if db % 2 == 0 else fsl3
                tld = P.dma("A", sl.ap[:, :],
                            dt["x_fm"][db, :, u * CH:(u + 1) * CH],
                            waits=sl.fill_waits() + [P.full("cV"), P.full("cA")],
                            sem=sl.qsem)
                t = P.op("V", lambda e, db=db, u=u, sl=sl: e.tensor_tensor(
                    xt(db, u), xt(db, u), sl.ap[:, :], ALU.add),
                    waits=[tld, xtok[db][u]])
                sl.used(t)
                xtok[db][u] = t
            rt = ln_stats(lambda db, u=u: xt(db, u), lambda db, u=u: xtok[db][u],
                          rms=False, n=D, eps=EPS1)
            tM, tI = ln_bcast(rt)
            toks = ln_apply(lambda db, u=u: xt(db, u), lambda db, u=u: xt(db, u),
                            lambda db, u=u: xtok[db][u], None, None, tM, tI)
            for db in range(NDB):
                xtok[db][u] = toks[db]
                tap_fm("xmid", db, u, xt(db, u), toks[db])
                xmid_tok[db][u] = P.dma(
                    "S", xmidspill[db, :, u * CH:(u + 1) * CH], xt(db, u),
                    waits=[toks[db]], sem=f"dXS{u}")
                xtok[db][u] = xmid_tok[db][u]

            rt = ln_stats(lambda db, u=u: xt(db, u), lambda db, u=u: xtok[db][u],
                          rms=True, n=D, eps=EPS2)
            tM, tI = ln_bcast(rt)
            toks = rms_apply(lambda db, u=u: ht(db, u), lambda db, u=u: xt(db, u),
                             lambda db, u=u: xtok[db][u], vecs["lnf1"], tI)
            for db in range(NDB):
                htok[db][u] = toks[db]

            def qevac2(li, oc, bk, mmtok, ow=128, u=u):
                s = aslot[oc]
                t0 = P.op("V", lambda e, bk=bk, oc=oc: e.tensor_scalar(
                    fs1[:, :], bk.ap[:, :], 1.0, vecs["bq2"][:, oc:oc + 1],
                    ALU.mult, ALU.add), waits=[mmtok, tok_vecs])
                t = rope_apply(u, fs1, s, [t0] + s.fill_waits())
                bk.used(t)
                s.ready = t
                tap_fm("q2", oc, u, s.ap, t, srcslot=s)

            proj_pass(dt["f_wq"], D, D,
                      [(lambda kb, u=u: ht(kb, u), lambda kb, u=u: htok[kb][u])],
                      qevac2, g_oc=3)
            rd = [aslot[i].ready for i in range(16)]
            tq2 = P.dma("S", qspill[u], arena[:, 0:16 * 512], waits=rd,
                        sem="dQS2")
            for i in range(16):
                aslot[i].used(tq2)
            kv_wave(u, 2)

        if stage <= 3:
            P.join()
            finish_out_dummy(P, y_out, x_sb, xtok)
            return nc, dict(taps=taps)

        P.join()
        ztok2 = [[None, None] for _ in range(NH)]
        attn_mixer(2, ztok2)

        def woevac2(li, oc, bk, mmtok, ow=128):
            t = P.op("V", lambda e, oc=oc, li=li, bk=bk: e.tensor_tensor(
                xt(oc, li), xt(oc, li), bk.ap[:, :], ALU.add),
                waits=[mmtok, xtok[oc][li]])
            bk.used(t)
            xtok[oc][li] = t
            tap_fm("xwo2", oc, li, xt(oc, li), t)

        proj_pass(dt["f_wo"], D, D,
                  [(lambda kb: ht(kb, 0), lambda kb: ztok2[kb][0]),
                   (lambda kb: ht(kb, 1), lambda kb: ztok2[kb][1])],
                  woevac2, g_oc=2)

        P.join()
        for u in range(2):
            rt = ln_stats(lambda db, u=u: xt(db, u), lambda db, u=u: xtok[db][u],
                          rms=True, n=D, eps=EPS2)
            tM, tI = ln_bcast(rt)
            toks = rms_apply(lambda db, u=u: ht(db, u), lambda db, u=u: xt(db, u),
                             lambda db, u=u: xtok[db][u], vecs["lnf2"], tI)
            for db in range(NDB):
                h2tok[db][u] = toks[db]
                tap_fm("h2r", db, u, ht(db, u), toks[db])

        cmb_tok = [None, None]
        for u in range(2):
            ttoks = []
            for tt in range(4):
                bk = banks[tt % 2]
                first = True
                for kb in range(NDB):
                    last = kb == NDB - 1
                    tmm = P.op("T", lambda e, bk=bk, kb=kb, tt=tt, u=u,
                               first=first, last=last: e.matmul(
                                   bk.ap[0:128, 0:8],
                                   h_sb[:, kb * 2 * CH + u * CH + tt * 128:
                                        kb * 2 * CH + u * CH + (tt + 1) * 128],
                                   router_sb[:, kb * 8:(kb + 1) * 8],
                                   start=first, stop=last),
                               waits=[h2tok[kb][u], rtok[kb]] +
                               (bk.fill_waits() if first else []), inc=True)
                    first = False
                t1 = P.op("V", lambda e, bk=bk: e.reduce_max(
                    fs1[:, 0:1], bk.ap[:, 0:8], axis=AX.X,
                    negate=True), waits=[tmm])
                bk.used(t1)
                t2 = P.op("A", lambda e, bk=bk: e.activation(
                    fs2[:, 0:8], bk.ap[:, 0:8], AF.Exp, bias=fs1[:, 0:1],
                    scale=1.0, accum_out=fs1[:, 1:2]), waits=[t1])
                bk.used(t2)
                t3 = P.op("V", lambda e: e.reciprocal(
                    fs1[:, 2:3], fs1[:, 1:2]), waits=[t2])
                t4 = P.op("V", lambda e: e.tensor_scalar(
                    fs2[:, 0:8], fs2[:, 0:8], fs1[:, 2:3], None, ALU.mult),
                    waits=[t3])
                t5 = P.op("V", lambda e: e.reduce_max(
                    fs1[:, 3:4], fs2[:, 0:8], axis=AX.X, negate=True), waits=[t4])
                t6 = P.op("V", lambda e: e.tensor_scalar(
                    fs3[:, 0:8], fs2[:, 0:8], fs1[:, 3:4], None, ALU.add),
                    waits=[t5])
                t7 = P.op("V", lambda e: e.tensor_single_scalar(
                    fs3[:, 0:8], fs3[:, 0:8], 0.0, ALU.is_equal), waits=[t6])
                t8 = P.op("V", lambda e: e.scalar_tensor_tensor(
                    fs3[:, 0:8], fs3[:, 0:8], -2.0, fs2[:, 0:8],
                    ALU.mult, ALU.add), waits=[t7])
                t9 = P.op("V", lambda e: e.reduce_max(
                    fs1[:, 4:5], fs3[:, 0:8], axis=AX.X, negate=True), waits=[t8])
                t10 = P.op("V", lambda e: e.tensor_scalar(
                    fs3[:, 0:8], fs2[:, 0:8], fs1[:, 4:5], None, ALU.add),
                    waits=[t9])
                t11 = P.op("V", lambda e: e.tensor_single_scalar(
                    fs3[:, 0:8], fs3[:, 0:8], 0.0, ALU.is_ge), waits=[t10])
                t12 = P.op("V", lambda e: e.tensor_tensor(
                    fs1[:, 8:16], fs2[:, 0:8], fs3[:, 0:8], ALU.mult),
                    waits=[t11])
                bk6 = banks[6]
                tt_ = P.op("T", lambda e, tt=tt, bk6=bk6: e.transpose(
                    bk6.ap[0:8, tt * 128:(tt + 1) * 128], fs1[:, 8:16],
                    ident[:, :]),
                    waits=[t12, tok_id] + (bk6.fill_waits() if tt == 0 else []))
                ttoks.append(tt_)
            tc = P.op("V", lambda e, u=u: e.tensor_copy(
                cmbT[0:8, u * 512:(u + 1) * 512], banks[6].ap[0:8, :]),
                waits=ttoks)
            banks[6].used(tc)
            cmb_tok[u] = tc
            if "comb" in taps:
                P.dma("S", taps["comb"][0:8, u * 512:(u + 1) * 512],
                      cmbT[0:8, u * 512:(u + 1) * 512], waits=[tc], sem="dT")

        for ei in range(NE):
            ztoks = [[None, None] for _ in range(MI // 128)]
            zsl = [[aslot[oc], aslot[11 + oc]] for oc in range(MI // 128)]
            cb = [None, None]
            for li in range(2):
                cbb = banks[3]
                dstf = fs2 if li == 0 else fs3
                tcb = P.op("T", lambda e, ei=ei, li=li, cbb=cbb: e.matmul(
                    cbb.ap[:, :], e16[0:8, 128 * ei:128 * ei + 128],
                    cmbT[0:8, li * 512:(li + 1) * 512], start=True, stop=True),
                    waits=[cmb_tok[li], tok_e16] + cbb.fill_waits())
                tcp = P.op("V", lambda e, dstf=dstf, cbb=cbb: e.tensor_copy(
                    dstf[:, :], cbb.ap[:, :]), waits=[tcb])
                cbb.used(tcp)
                cb[li] = (dstf, tcp)

            def gevac_e(li, oc, bk, mmtok, ow=128, ei=ei):
                s = zsl[oc][li]
                t = P.op("A", lambda e, bk=bk, s=s: e.activation(
                    s.ap, bk.ap[:, :], AF.Silu, bias=0.0, scale=1.0),
                    waits=[mmtok] + s.fill_waits())
                bk.used(t)
                s.ready = t

            mv2 = [(lambda kb: ht(kb, 0), lambda kb: h2tok[kb][0]),
                   (lambda kb: ht(kb, 1), lambda kb: h2tok[kb][1])]
            proj_pass(dt["f_wge"][ei], D, MI, mv2, gevac_e, g_oc=2)

            def uevac_e(li, oc, bk, mmtok, ow=128, ei=ei):
                s = zsl[oc][li]
                dstf, ctk = cb[li]
                t0 = P.op("V", lambda e, bk=bk, s=s: e.tensor_tensor(
                    fs1[:, :], s.ap, bk.ap[:, :], ALU.mult),
                    waits=[mmtok, s.ready])
                bk.used(t0)
                t1 = P.op("V", lambda e, s=s, dstf=dstf: e.tensor_tensor(
                    s.ap, fs1[:, :], dstf[:, :], ALU.mult), waits=[t0, ctk])
                s.ready = t1
                ztoks[oc][li] = t1

            proj_pass(dt["f_wue"][ei], D, MI, mv2, uevac_e, g_oc=2)

            def devac_e(li, oc, bk, mmtok, ow=128):
                t = P.op("V", lambda e, oc=oc, li=li, bk=bk: e.tensor_tensor(
                    xt(oc, li), xt(oc, li), bk.ap[:, :], ALU.add),
                    waits=[mmtok, xtok[oc][li]])
                bk.used(t)
                xtok[oc][li] = t

            proj_pass(dt["f_wde"][ei], MI, D,
                      [(lambda kb: zsl[kb][0].ap, lambda kb: ztoks[kb][0]),
                       (lambda kb: zsl[kb][1].ap, lambda kb: ztoks[kb][1])],
                      devac_e, g_oc=2)
            for oc in range(MI // 128):
                for li in range(2):
                    zsl[oc][li].used(ztoks[oc][li])

        for u in range(2):
            ssl = [aslot[i] for i in range(44)]
            stoks = [None] * 44

            def shg_evac(li, oc, bk, mmtok, ow=128):
                s = ssl[oc]
                t = P.op("A", lambda e, bk=bk, s=s: e.activation(
                    s.ap, bk.ap[:, :], AF.Silu, bias=0.0, scale=1.0),
                    waits=[mmtok] + s.fill_waits())
                bk.used(t)
                s.ready = t

            mv = [(lambda kb, u=u: ht(kb, u), lambda kb, u=u: h2tok[kb][u])]
            proj_pass(dt["f_shg"], D, SI, mv, shg_evac, g_oc=4)

            def shu_evac(li, oc, bk, mmtok, ow=128):
                s = ssl[oc]
                t = P.op("V", lambda e, bk=bk, s=s: e.tensor_tensor(
                    s.ap, s.ap, bk.ap[:, :], ALU.mult), waits=[mmtok, s.ready])
                bk.used(t)
                s.ready = t
                stoks[oc] = t

            proj_pass(dt["f_shu"], D, SI, mv, shu_evac, g_oc=4)

            bk3 = banks[3]
            first = True
            for kb in range(NDB):
                slot = wring.next()
                wt = P.dma("G", slot.ap[0:128, 0:1],
                           dt["f_sheg"][kb * 128:(kb + 1) * 128, :],
                           waits=slot.fill_waits(), sem=slot.qsem)
                slot.ready = wt
                last = kb == NDB - 1
                tmm = P.op("T", lambda e, slot=slot, kb=kb, u=u, first=first,
                           last=last, bk3=bk3: e.matmul(
                               bk3.ap[0:1, :], slot.ap[0:128, 0:1],
                               ht(kb, u), start=first, stop=last),
                           waits=[wt, h2tok[kb][u]] +
                           (bk3.fill_waits() if first else []), inc=True)
                slot.used(tmm)
                first = False
            tsg = P.op("A", lambda e, u=u, bk3=bk3: e.activation(
                sigr[0:1, u * 512:(u + 1) * 512], bk3.ap[0:1, :], AF.Sigmoid,
                bias=0.0, scale=1.0),
                waits=[tmm, tok_sig0] + sigr_sl.fill_waits())
            bk3.used(tsg)
            bkS = banks[3]
            tsb = P.op("T", lambda e, u=u, bkS=bkS: e.matmul(
                bkS.ap[:, :], ones1r[:, :], sigr[0:1, u * 512:(u + 1) * 512],
                start=True, stop=True),
                waits=[tsg, tok_e2] + bkS.fill_waits())
            sigr_sl.used(tsb)
            tsb2 = P.op("V", lambda e, bkS=bkS: e.tensor_copy(
                sigb[:, :], bkS.ap[:, :]), waits=[tsb])
            bkS.used(tsb2)

            def shd_evac(li, oc, bk, mmtok, ow=128, u=u, tsb2=tsb2):
                t0 = P.op("V", lambda e, bk=bk: e.tensor_tensor(
                    fs1[:, :], sigb[:, :], bk.ap[:, :], ALU.mult),
                    waits=[mmtok, tsb2])
                bk.used(t0)
                t1 = P.op("V", lambda e, oc=oc, u=u: e.tensor_tensor(
                    xt(oc, u), xt(oc, u), fs1[:, :], ALU.add),
                    waits=[t0, xtok[oc][u]])
                xtok[oc][u] = t1
                tap_fm("moe", oc, u, xt(oc, u), t1)

            proj_pass(dt["f_shd"], SI, D,
                      [(lambda kb: ssl[kb].ap, lambda kb: stoks[kb])],
                      shd_evac, g_oc=4)
            for kb in range(44):
                ssl[kb].used(stoks[kb])

        P.join()
        out_dma = []
        for u in range(2):
            for db in range(NDB):
                sl = fsl2 if db % 2 == 0 else fsl3
                tld = P.dma("A", sl.ap[:, :],
                            xmidspill[db, :, u * CH:(u + 1) * CH],
                            waits=[xmid_tok[db][u]] + sl.fill_waits(),
                            sem=sl.qsem)
                t = P.op("V", lambda e, db=db, u=u, sl=sl: e.tensor_tensor(
                    xt(db, u), xt(db, u), sl.ap[:, :], ALU.add),
                    waits=[tld, xtok[db][u]])
                sl.used(t)
                xtok[db][u] = t
            rt = ln_stats(lambda db, u=u: xt(db, u), lambda db, u=u: xtok[db][u],
                          rms=False, n=D, eps=EPS1)
            tM, tI = ln_bcast(rt)
            toks = ln_apply(lambda db, u=u: xt(db, u), lambda db, u=u: xt(db, u),
                            lambda db, u=u: xtok[db][u], None, None, tM, tI)
            for db in range(NDB):
                xtok[db][u] = toks[db]
                out_dma.append(P.dma("A", y_out[db, :, u * CH:(u + 1) * CH],
                                     xt(db, u), waits=[toks[db]], sem="dY"))

        P.emit("S", lambda e: e.nop(),
               waits=[P.full(nm) for nm in list(P.cnt)])

    return nc, dict(taps=taps)


def finish_out_dummy(P, y_out, x_sb, xtok):
    for u in range(2):
        for db in range(16):
            P.dma("A", y_out[db, :, u * CH:(u + 1) * CH],
                  x_sb[:, db * 2 * CH + u * CH: db * 2 * CH + (u + 1) * CH],
                  waits=[xtok[db][u]], sem="dY")
    P.emit("S", lambda e: e.nop(), waits=[P.full(nm) for nm in list(P.cnt)])


def _core_meta(r):
    cA, cB = r, 7 - r
    chunks = [cA, cB]
    pos = np.zeros((1, 2 * CH), np.float32)
    for qv in range(2):
        pos[0, qv * CH:(qv + 1) * CH] = (np.arange(CH, dtype=np.float32)
                                         + 512 * chunks[qv])

    def chunk_at_slot(s):
        rho, w = s % 4, s // 4
        return rho if w == 0 else 7 - rho

    rs = np.zeros((1, 2 * NMC), np.float32)
    for qv in range(2):
        qc = chunks[qv]
        for s in range(8):
            kc = chunk_at_slot(s)
            for kt in range(4):
                c = (qv * 8 + s) * 4 + kt
                if kc < qc:
                    rs[0, c] = 512 * qc - (512 * kc + 128 * kt)
                    rs[0, NMC + c] = 1.0
    for kt in range(4):
        rs[0, 64 + kt] = -128.0 * kt
        rs[0, NMC + 64 + kt] = 1.0

    pad = np.full((1, 16), 1000.0, np.float32)
    ble = np.zeros((1, 256), np.float32)
    for qv in range(2):
        qc = chunks[qv]
        for s in range(8):
            kc = chunk_at_slot(s)
            if kc < qc:
                pad[0, qv * 8 + s] = 0.0
            for h in range(NH):
                ble[0, qv * 128 + s * 16 + h] = 1.0 if kc <= qc else 0.0
    return pos, rs, pad, ble


def _vec(a):
    return np.ascontiguousarray(np.asarray(a, np.float32).reshape(NDB, 128).T)


def kernel(**inputs):
    nc, meta = build_program()
    wmap = {
        "a_wq": "L0_attn_wq", "a_wk": "L0_attn_wk", "a_wv": "L0_attn_wv",
        "a_wg": "L0_attn_wg", "a_wo": "L0_attn_wo", "a_w1": "L0_attn_w1",
        "a_w2": "L0_attn_w2", "a_w3": "L0_attn_w3",
        "f_wq": "L0_ffn_wq", "f_wk": "L0_ffn_wk", "f_wv": "L0_ffn_wv",
        "f_wo": "L0_ffn_wo", "f_router": "L0_ffn_w_router",
        "f_wge": "L0_ffn_w_gate_e", "f_wue": "L0_ffn_w_up_e",
        "f_wde": "L0_ffn_w_down_e", "f_shg": "L0_ffn_w_sh_gate",
        "f_shu": "L0_ffn_w_sh_up", "f_shd": "L0_ffn_w_sh_down",
    }
    vmap = {
        "n1w": "L0_attn_norm1_w", "n1b": "L0_attn_norm1_b",
        "n2w": "L0_attn_norm2_w", "n2b": "L0_attn_norm2_b",
        "lnf1": "L0_ffn_w_ln1", "lnf2": "L0_ffn_w_ln2",
        "bq2": "L0_ffn_bq", "bk2": "L0_ffn_bk", "bv2": "L0_ffn_bv",
    }
    shared = {k: np.ascontiguousarray(np.asarray(inputs[v], np.float32))
              for k, v in wmap.items()}
    shared["f_sheg"] = np.ascontiguousarray(
        np.asarray(inputs["L0_ffn_w_sh_eg"], np.float32).reshape(D, 1))
    for k, v in vmap.items():
        shared[k] = _vec(inputs[v])
    X = np.asarray(inputs["L0_attn_x"], np.float32)

    in_maps = []
    for core in range(8):
        b, r = core // 4, core % 4
        cA, cB = r, 7 - r
        xa = X[b, 512 * cA:512 * cA + 512]
        xb = X[b, 512 * cB:512 * cB + 512]
        xf = np.concatenate([xa, xb], 0).T.reshape(NDB, 128, 2 * CH)
        pos, rs, pad, ble = _core_meta(r)
        m = dict(shared)
        m["x_fm"] = np.ascontiguousarray(xf)
        m["pos"] = pos
        m["rsmeta"] = rs
        m["padmeta"] = pad
        m["blerow"] = ble
        in_maps.append(m)

    res = run_bass_kernel_spmd(nc, in_maps, core_ids=list(range(8)))

    out = np.empty((2, 4096, D), np.float32)
    for core in range(8):
        b, r = core // 4, core % 4
        cA, cB = r, 7 - r
        y = np.asarray(res.results[core]["y"])
        yt = y.reshape(D, 2 * CH).T
        out[b, 512 * cA:512 * cA + 512] = yt[0:CH]
        out[b, 512 * cB:512 * cB + 512] = yt[CH:2 * CH]
    return out


# version 06 (final)
# speedup vs baseline: 1.3023x; 1.0060x over previous
import math
import numpy as np
import concourse.bass as bass
import concourse.mybir as mybir
from concourse.bass_utils import run_bass_kernel_spmd

F32 = mybir.dt.float32
BF16 = mybir.dt.bfloat16
AF = mybir.ActivationFunctionType
ALU = mybir.AluOpType
AX = mybir.AxisListType

D = 2048
NDB = 16
CH = 512
NH = 16
DHD = 128
HID1 = 5461
NKB1 = 43
NE = 8
MI = 1408
SI = 5632
EPS1 = 1e-5
EPS2 = 1e-6
CINV = DHD ** -0.5
LN2 = math.log(2.0)
LN1E6 = math.log(1.0e6)
TWO_PI = 2.0 * math.pi
GAMMA = [1.0 - 2.0 ** (-5.0 - h) for h in range(NH)]
LNG = [math.log(g) for g in GAMMA]

KW = NDB * CH
VW = 4 * D
SW = CH
GW = KW + VW + SW
NMC = 68


def ceil_div(a, b):
    return (a + b - 1) // b


class Prog:
    def __init__(self, nc, block):
        self.nc = nc
        self.block = block
        self.sec = {"T": block.tensor, "V": block.vector, "A": block.scalar,
                    "G": block.gpsimd, "S": block.sync}
        self.sem = {}
        self.cnt = {}
        for nm in ("cT", "cV", "cA", "cG", "dS", "dA", "dG", "cc"):
            self.sem[nm] = nc.alloc_semaphore(nm)
            self.cnt[nm] = 0
        self.seen = {e: {} for e in self.sec}

    def emit(self, eng, fn, waits=(), inc=None, inc_by=1):
        need = []
        seen = self.seen[eng]
        for t in waits:
            if t is None:
                continue
            nm, val = t
            if seen.get(nm, 0) >= val:
                continue
            seen[nm] = val
            need.append((nm, val))
        tok = None
        if inc is not None:
            self.cnt[inc] += inc_by
            tok = (inc, self.cnt[inc])
        sems = self.sem

        def run(e):
            for nm, val in need:
                e.wait_ge(sems[nm], val)
            inst = fn(e)
            if inc is not None:
                inst.then_inc(sems[inc], inc_by)
        self.sec[eng](run)
        return tok

    ESEM = {"T": "cT", "V": "cV", "A": "cA", "G": "cG"}
    DSEM = {"S": "dS", "A": "dA", "G": "dG"}

    def op(self, eng, fn, waits=(), inc=True):
        return self.emit(eng, fn, waits, inc=self.ESEM[eng] if inc else None)

    def getsem(self, nm):
        if nm not in self.sem:
            self.sem[nm] = self.nc.alloc_semaphore(nm)
            self.cnt[nm] = 0
        return nm

    def dma(self, eng, out, in_, waits=(), sem=None):
        if sem is not None:
            sem = self.getsem(sem)
        return self.emit(eng, lambda e: e.dma_start(out=out, in_=in_),
                         waits, inc=sem or self.DSEM[eng], inc_by=16)

    def full(self, nm):
        return (nm, self.cnt.get(nm, 0))

    def join(self):
        toks = [(nm, self.cnt[nm]) for nm in self.cnt]
        for eng in ("T", "V", "A", "G", "S"):
            self.emit(eng, lambda e: e.nop(), waits=toks)


class Slot:
    def __init__(self, ap):
        self.ap = ap
        self.ready = None
        self.users = []
        self.qsem = None

    def fill_waits(self):
        w = list(self.users)
        self.users = []
        return w

    def used(self, tok):
        if tok is not None:
            self.users.append(tok)


class Ring:
    def __init__(self, slots):
        self.slots = slots
        self.i = 0

    def next(self):
        s = self.slots[self.i % len(self.slots)]
        self.i += 1
        return s


def build_program(stage=99, debug=()):
    nc = bass.Bass("TRN2", target_bir_lowering=False, debug=False)

    dt = {}
    dt["x_fm"] = nc.dram_tensor("x_fm", [NDB, 128, 2 * CH], F32, kind="ExternalInput")
    for nm, sh in [
        ("a_wq", [D, D]), ("a_wk", [D, D]), ("a_wv", [D, D]), ("a_wg", [D, D]),
        ("a_wo", [D, D]), ("a_w1", [D, HID1]), ("a_w2", [D, HID1]), ("a_w3", [HID1, D]),
        ("f_wq", [D, D]), ("f_wk", [D, D]), ("f_wv", [D, D]), ("f_wo", [D, D]),
        ("f_router", [D, NE]),
        ("f_wge", [NE, D, MI]), ("f_wue", [NE, D, MI]), ("f_wde", [NE, MI, D]),
        ("f_shg", [D, SI]), ("f_shu", [D, SI]), ("f_shd", [SI, D]),
        ("f_sheg", [D, 1]),
    ]:
        dt[nm] = nc.dram_tensor(nm, sh, F32, kind="ExternalInput")
    for nm in ("n1w", "n1b", "n2w", "n2b", "lnf1", "lnf2", "bq2", "bk2", "bv2"):
        dt[nm] = nc.dram_tensor(nm, [128, NDB], F32, kind="ExternalInput")
    dt["pos"] = nc.dram_tensor("pos", [1, 2 * CH], F32, kind="ExternalInput")
    dt["rsmeta"] = nc.dram_tensor("rsmeta", [1, 2 * NMC], F32, kind="ExternalInput")
    dt["padmeta"] = nc.dram_tensor("padmeta", [1, 16], F32, kind="ExternalInput")
    dt["blerow"] = nc.dram_tensor("blerow", [2, 128], F32, kind="ExternalInput")

    y_out = nc.dram_tensor("y", [NDB, 128, 2 * CH], F32, kind="ExternalOutput")

    gin = [nc.dram_tensor(f"gin{i}", [128, GW], BF16) for i in range(4)]
    gout = [nc.dram_tensor(f"gout{i}", [4 * 128, GW], BF16) for i in range(4)]
    qspill = nc.dram_tensor("qspill", [2, 128, NDB * CH], BF16)
    hgspill = nc.dram_tensor("hgspill", [2, 128, NDB * CH], BF16)
    xmidspill = nc.dram_tensor("xmidspill", [NDB, 128, 2 * CH], F32)

    TAPSHAPES = {
        "h1": [NDB, 128, 2 * CH], "k1": [NDB, 128, 2 * CH], "v1": [2, 4, 128, D],
        "q1": [NDB, 128, 2 * CH], "hg": [NDB, 128, 2 * CH],
        "retz": [NDB, 128, 2 * CH], "xwo": [NDB, 128, 2 * CH],
        "h2": [NDB, 128, 2 * CH], "xmid": [NDB, 128, 2 * CH],
        "q2": [NDB, 128, 2 * CH], "k2": [NDB, 128, 2 * CH], "v2": [2, 4, 128, D],
        "ctx2": [NDB, 128, 2 * CH], "xwo2": [NDB, 128, 2 * CH],
        "h2r": [NDB, 128, 2 * CH], "comb": [8, 2 * CH],
        "gin1": [2, 128, GW], "gin2": [2, 128, GW],
        "rope": [4, 128, 512], "att2l": [16, 512], "att2b": [128, 16],
        "att2k": [1, 128],
        "moe": [NDB, 128, 2 * CH],
    }
    taps = {t: nc.dram_tensor("tap_" + t, TAPSHAPES[t], F32, kind="ExternalOutput")
            for t in debug}

    x_sb = nc.alloc_sbuf_tensor("x_sb", [128, NDB * 2 * CH], F32)
    h_sb = nc.alloc_sbuf_tensor("h_sb", [128, NDB * 2 * CH], BF16)
    arena = nc.alloc_sbuf_tensor("arena", [128, 44 * 512], BF16)
    NWR = 7 if debug else 9
    wr_sb = nc.alloc_sbuf_tensor("wr_sb", [128, NWR * 512], BF16)
    fs1 = nc.alloc_sbuf_tensor("fs1", [128, 512], F32)
    fs2 = nc.alloc_sbuf_tensor("fs2", [128, 512], F32)
    fs3 = nc.alloc_sbuf_tensor("fs3", [128, 512], F32)
    cosq = [nc.alloc_sbuf_tensor(f"cos{u}", [128, 512], BF16) for u in range(2)]
    sinq = [nc.alloc_sbuf_tensor(f"sin{u}", [128, 512], BF16) for u in range(2)]
    tri = nc.alloc_sbuf_tensor("tri", [128, 896], BF16)
    grow = nc.alloc_sbuf_tensor("grow", [128, NH * NMC], F32)
    gcols = nc.alloc_sbuf_tensor("gcols", [NH, 512], F32)
    st16 = gcols
    st8f = nc.alloc_sbuf_tensor("st8f", [1, 2112], F32)
    ones1r = nc.alloc_sbuf_tensor("ones1r", [1, 128], F32)
    e16 = nc.alloc_sbuf_tensor("e16", [16, 16 * 128], F32)
    onesst = nc.alloc_sbuf_tensor("onesst", [128, 1], BF16)
    lind = nc.alloc_sbuf_tensor("lind", [128, 16 * 16], BF16)
    iotac = nc.alloc_sbuf_tensor("iotac", [128, 1], F32)
    lng_sb = nc.alloc_sbuf_tensor("lng_sb", [16, 1], F32)
    invf = nc.alloc_sbuf_tensor("invf", [128, 1], F32)
    metars = nc.alloc_sbuf_tensor("metars", [1, 2 * NMC], F32)
    metapad = nc.alloc_sbuf_tensor("metapad", [1, 16], F32)
    blerow = nc.alloc_sbuf_tensor("ble_sb", [1, 256], BF16)
    kstatr = nc.alloc_sbuf_tensor("kstatr", [1, 128], BF16)
    mrow = nc.alloc_sbuf_tensor("mrow", [1, 288], F32)
    brow = nc.alloc_sbuf_tensor("brow", [1, 16], F32)
    bbase = nc.alloc_sbuf_tensor("bbase", [128, 16], F32)
    basb = nc.alloc_sbuf_tensor("basb", [128, 48], F32)
    vecs = {}
    for nm in ("n1w", "n1b", "n2w", "n2b", "lnf1", "lnf2", "bq2", "bkc2", "bv2"):
        vecs[nm] = nc.alloc_sbuf_tensor("v_" + nm, [128, NDB], F32)
    ident = nc.alloc_sbuf_tensor("ident", [128, 128], F32)
    qtl = nc.alloc_sbuf_tensor("qtl", [128, 6 * 512], BF16)
    cmbT = nc.alloc_sbuf_tensor("cmbT", [8, 2 * 512], F32)
    router_sb = nc.alloc_sbuf_tensor("router_sb", [128, 16 * 8], BF16)
    sigr = nc.alloc_sbuf_tensor("sigr", [1, 1024], F32)
    metap = sigr
    sigb = nc.alloc_sbuf_tensor("sigb", [128, 512], F32)
    ptile = nc.alloc_sbuf_tensor("ptile", [128, 2 * 512], BF16)
    i32t = nc.alloc_sbuf_tensor("i32t", [128, 256], mybir.dt.int32)

    tapb = nc.alloc_sbuf_tensor("tapb", [128, 512], F32) if debug else None

    pb = [nc.alloc_psum_tensor(f"pb{i}", [128, 512], F32) for i in range(8)]

    rem = nc.sbuf_bytes_remaining
    assert rem >= 0, f"SBUF over budget by {-rem} bytes"

    with nc.Block() as block:
        P = Prog(nc, block)
        banks = [Slot(pb[i]) for i in range(8)]
        wring = Ring([Slot(wr_sb[:, i * 512:(i + 1) * 512]) for i in range(NWR)])
        for i, s in enumerate(wring.slots):
            s.qsem = f"dW{i}"
        aslot = [Slot(arena[:, i * 512:(i + 1) * 512]) for i in range(44)]
        bounce = Ring([aslot[40], aslot[41], aslot[42], aslot[43]])
        fsl2 = Slot(fs2)
        fsl2.qsem = "dF0"
        fsl3 = Slot(fs3)
        fsl3.qsem = "dF1"
        tap_sl = Slot(None)
        bbase_slot = Slot(None)
        biasall_slot = Slot(None)
        st8_sl = Slot(None)
        st16_sl = Slot(None)
        brow_sl = Slot(None)
        sigr_sl = Slot(None)

        def stc(i, w=512):
            return st8f[0:1, i * 512:i * 512 + w]

        def xt(db, u):
            return x_sb[:, db * 2 * CH + u * CH: db * 2 * CH + (u + 1) * CH]

        def ht(db, u):
            return h_sb[:, db * 2 * CH + u * CH: db * 2 * CH + (u + 1) * CH]

        for nm in ("n1w", "n1b", "n2w", "n2b", "lnf1", "lnf2", "bq2", "bv2"):
            P.dma("A", vecs[nm][:, :], dt[nm][:, :], sem="dV")
        P.dma("A", vecs["bkc2"][:, :], dt["bk2"][:, :], sem="dV")
        P.dma("A", metap[:, :], dt["pos"][:, :], sem="dV")
        P.dma("A", metars[:, :], dt["rsmeta"][:, :], sem="dV")
        P.dma("A", metapad[:, :], dt["padmeta"][:, :], sem="dV")
        tok_vecs = P.full("dV")
        tok_pos = tok_rs = tok_pad = tok_vecs
        vtok = {nm: tok_vecs for nm in
                ("n1w", "n1b", "n2w", "n2b", "lnf1", "lnf2", "bq2", "bv2")}
        vtok["bkc2"] = P.op("V", lambda e: e.tensor_single_scalar(
            vecs["bkc2"][:, :], vecs["bkc2"][:, :], CINV, ALU.mult),
            waits=[tok_vecs])
        xtok = [[None, None] for _ in range(NDB)]
        for db in range(NDB):
            P.dma("A", x_sb[:, db * 2 * CH:(db + 1) * 2 * CH], dt["x_fm"][db],
                  sem="dX")
        tok_x = P.full("dX")
        for db in range(NDB):
            xtok[db][0] = xtok[db][1] = tok_x
        for kb in range(NDB):
            P.dma("G", router_sb[:, kb * 8:(kb + 1) * 8],
                  dt["f_router"][kb * 128:(kb + 1) * 128, :], sem="dM")
        P.dma("G", blerow[:, :], dt["blerow"][:, :], sem="dM")
        tok_misc = P.full("dM")
        tok_ble = tok_misc
        rtok = [tok_misc] * NDB

        tok_iota = P.op("G", lambda e: e.iota(
            iotac[:, :], [[0, 1]], channel_multiplier=1,
            allow_small_or_imprecise_dtypes=True))
        t_ = P.op("G", lambda e: e.iota(
            lng_sb[:, :], [[0, 1]], base=5, channel_multiplier=1,
            allow_small_or_imprecise_dtypes=True))
        t_ = P.op("A", lambda e: e.activation(
            lng_sb[:, :], lng_sb[:, :], AF.Exp, bias=0.0, scale=-LN2),
            waits=[t_])
        t_ = P.op("V", lambda e: e.tensor_scalar(
            lng_sb[:, :], lng_sb[:, :], -1.0, 1.0, ALU.mult, ALU.add), waits=[t_])
        tok_lng = P.op("A", lambda e: e.activation(
            lng_sb[:, :], lng_sb[:, :], AF.Ln, bias=0.0, scale=1.0), waits=[t_])

        t_ = P.op("V", lambda e: e.tensor_single_scalar(
            invf[:, :], iotac[:, :], 64.0, ALU.is_ge), waits=[tok_iota])
        t_ = P.op("V", lambda e: e.scalar_tensor_tensor(
            invf[:, :], invf[:, :], -64.0, iotac[:, :], ALU.mult, ALU.add),
            waits=[t_])
        tok_invf = P.op("A", lambda e: e.activation(
            invf[:, :], invf[:, :], AF.Exp, bias=0.0, scale=-LN1E6 / 64.0), waits=[t_])

        tok_e2 = P.op("G", lambda e: e.memset(ones1r[:, :], 1.0))

        t_ = P.op("G", lambda e: e.memset(e16[:, :], 1.0))
        t_ = P.op("G", lambda e: e.affine_select(
            e16[:, :], e16[:, :], pattern=[[1, 16 * 128]], compare_op=ALU.is_ge,
            fill=0.0, base=0, channel_multiplier=-128), waits=[t_])
        tok_e16 = P.op("G", lambda e: e.affine_select(
            e16[:, :], e16[:, :], pattern=[[-1, 16 * 128]], compare_op=ALU.is_ge,
            fill=0.0, base=127, channel_multiplier=128), waits=[t_])

        tok_ones = P.op("G", lambda e: e.memset(onesst[:, :], 1.0))

        t_ = P.op("G", lambda e: e.memset(lind[:, :], 0.0))
        tok_lind = t_
        for h in range(NH):
            tok_lind = P.op("G", lambda e, h=h: e.memset(
                lind[:, 16 * h + h:16 * h + h + 1], 1.0), waits=[tok_lind])

        t_ = P.op("G", lambda e: e.memset(ident[:, :], 1.0))
        tok_id = P.op("G", lambda e: e.affine_select(
            ident[:, :], ident[:, :], pattern=[[1, 128]], compare_op=ALU.is_equal,
            fill=0.0, base=0, channel_multiplier=-1), waits=[t_])

        t_ = P.op("G", lambda e: e.memset(tri[:, :], 1.0))
        _tt = P.op("G", lambda e: e.affine_select(
            tri[:, :], tri[:, :], pattern=[[1, 896]], compare_op=ALU.is_ge,
            fill=0.0, base=-384, channel_multiplier=-1), waits=[t_])
        tok_tri = [_tt] * 4

        def trim(m):
            return tri[:, 384 - 128 * m:896 - 128 * m]

        t_ = P.op("G", lambda e: e.iota(
            gcols[:, :], [[1, 512]], channel_multiplier=0,
            allow_small_or_imprecise_dtypes=True))
        tok_gcols = P.op("A", lambda e: e.activation(
            gcols[:, :], gcols[:, :], AF.Exp, bias=0.0, scale=lng_sb[:, :]),
            waits=[t_, tok_lng])

        tok_kstat0 = P.op("G", lambda e: e.memset(kstatr[:, :], 0.0))
        tok_sig0 = None
        tok_brow0 = P.op("G", lambda e: e.memset(brow[:, :], 0.0))

        bkD, bkV = banks[6], banks[7]
        tD = P.op("T", lambda e: e.matmul(
            bkD.ap[:, 0:NMC], ones1r[:, :], metars[0:1, 0:NMC],
            start=True, stop=True), waits=[tok_e2, tok_rs] + bkD.fill_waits())
        tV = P.op("T", lambda e: e.matmul(
            bkV.ap[:, 0:NMC], ones1r[:, :], metars[0:1, NMC:2 * NMC],
            start=True, stop=True), waits=[tok_e2, tok_rs] + bkV.fill_waits())
        tDm = P.op("V", lambda e: e.tensor_scalar(
            fs1[:, 0:NMC], bkD.ap[:, 0:NMC], iotac[:, :], None, ALU.subtract),
            waits=[tD, tok_iota])
        bkD.used(tDm)
        grow_tok = [None] * NH
        for h in range(NH):
            te = P.op("A", lambda e, h=h: e.activation(
                grow[:, NMC * h:NMC * (h + 1)], fs1[:, 0:NMC], AF.Exp, bias=0.0,
                scale=LNG[h]), waits=[tDm])
            tm = P.op("V", lambda e, h=h: e.tensor_tensor(
                grow[:, NMC * h:NMC * (h + 1)], grow[:, NMC * h:NMC * (h + 1)],
                bkV.ap[:, 0:NMC], ALU.mult), waits=[te, tV])
            bkV.used(tm)
            grow_tok[h] = tm

        def growcol(h, qv, s, kt):
            c = NMC * h + ((qv * 8 + s) * 4 + kt if s >= 0 else 64 + kt)
            return grow[:, c:c + 1]

        tok_cs = []
        for u in range(2):
            bA = banks[6]
            tb = P.op("T", lambda e, u=u: e.matmul(
                bA.ap[:, :], ones1r[:, :], metap[0:1, u * 512:(u + 1) * 512],
                start=True, stop=True),
                waits=[tok_e2, tok_pos] + bA.fill_waits())
            t1 = P.op("V", lambda e: e.tensor_scalar(
                fs2[:, :], bA.ap[:, :], invf[:, :], None, ALU.mult),
                waits=[tb, tok_invf])
            bA.used(t1)
            prev_act = None
            for trig, dst in (("cos", cosq[u]), ("sin", sinq[u])):
                shift = 0.0 if trig == "sin" else math.pi / 2.0
                t2 = P.op("V", lambda e, shift=shift: e.tensor_scalar(
                    fs3[:, :], fs2[:, :], shift, 1.0 / TWO_PI, ALU.add, ALU.mult),
                    waits=[t1, prev_act] + tok_cs)
                t3 = t2
                for hf in range(2):
                    t3a = P.op("V", lambda e, hf=hf: e.tensor_copy(
                        i32t[:, :], fs3[:, hf * 256:(hf + 1) * 256]), waits=[t3])
                    t3 = P.op("V", lambda e, hf=hf: e.tensor_copy(
                        fs1[:, hf * 256:(hf + 1) * 256], i32t[:, :]), waits=[t3a])
                t4a = P.op("V", lambda e, shift=shift: e.tensor_scalar(
                    fs3[:, :], fs2[:, :], shift, None, ALU.add), waits=[t2])
                t4 = P.op("V", lambda e: e.scalar_tensor_tensor(
                    fs3[:, :], fs1[:, :], -TWO_PI, fs3[:, :], ALU.mult, ALU.add),
                    waits=[t3, t4a])
                if trig == "cos":
                    t5 = P.op("A", lambda e, dst=dst: e.activation(
                        dst[:, :], fs3[:, :], AF.Sin, bias=0.0, scale=1.0),
                        waits=[t4, t3, t4a])
                else:
                    t5a = P.op("A", lambda e: e.activation(
                        fs1[:, :], fs3[:, :], AF.Sin, bias=0.0, scale=1.0),
                        waits=[t4, t3, t4a])
                    t5b = P.op("V", lambda e, dst=dst: e.tensor_copy(
                        dst[0:64, :], fs1[64:128, :]), waits=[t5a])
                    t5 = P.op("V", lambda e, dst=dst: e.tensor_copy(
                        dst[64:128, :], fs1[0:64, :]), waits=[t5b])
                prev_act = t5
                tok_cs.append(t5)
        if "rope" in taps:
            for i, tile in enumerate([cosq[0], sinq[0], cosq[1], sinq[1]]):
                tcp = P.op("V", lambda e, tile=tile: e.tensor_copy(
                    tapb[:, :], tile[:, :]), waits=tok_cs + tap_sl.fill_waits())
                td = P.dma("S", taps["rope"][i], tapb[:, :], waits=[tcp],
                           sem="dT")
                tap_sl.used(td)
        tok_sig0 = P.op("G", lambda e: e.memset(sigr[:, :], 0.0),
                        waits=[tok_cs[-1]] + tok_cs)

        def ln_stats(src, src_toks, rms, n, eps):
            bkS, bkQ = banks[5], banks[4]
            firstS = True
            firstQ = True
            prep = []
            for db in range(NDB):
                xb, sq = aslot[db], aslot[NDB + db]
                if not rms:
                    tc = P.op("V", lambda e, db=db, xb=xb: e.tensor_copy(
                        xb.ap, src(db)), waits=[src_toks(db)] + xb.fill_waits())
                    xb.ready = tc
                ts = P.op("A", lambda e, db=db, sq=sq: e.activation(
                    sq.ap, src(db), AF.Square, bias=0.0, scale=1.0),
                    waits=[src_toks(db)] + sq.fill_waits())
                sq.ready = ts
                prep.append((xb, sq))
            tS = None
            tQ = None
            for db in range(NDB):
                xb, sq = prep[db]
                last = db == NDB - 1
                if not rms:
                    w = [xb.ready, tok_ones] + (bkS.fill_waits() if firstS else [])
                    tS = P.op("T", lambda e, xb=xb, f=firstS, l=last: e.matmul(
                        bkS.ap[0:1, :], onesst[:, 0:1], xb.ap, start=f, stop=l),
                        waits=w, inc=True)
                    xb.used(tS)
                    firstS = False
                w = [sq.ready, tok_ones] + (bkQ.fill_waits() if firstQ else [])
                tQ = P.op("T", lambda e, sq=sq, f=firstQ, l=last: e.matmul(
                    bkQ.ap[0:1, :], onesst[:, 0:1], sq.ap, start=f, stop=l),
                    waits=w, inc=True)
                sq.used(tQ)
                firstQ = False
            stw = st8_sl.fill_waits()
            if rms:
                t1 = P.op("V", lambda e: e.memset(stc(0), 0.0), waits=stw)
            else:
                t1 = P.op("V", lambda e: e.tensor_single_scalar(
                    stc(0), bkS.ap[0:1, :], -1.0 / n, ALU.mult),
                    waits=[tS] + stw)
                bkS.used(t1)
            t2 = P.op("V", lambda e, eps=eps: e.tensor_scalar(
                stc(2), bkQ.ap[0:1, :], 1.0 / n, eps, ALU.mult, ALU.add),
                waits=[tQ] + stw)
            bkQ.used(t2)
            if not rms:
                t3 = P.op("V", lambda e: e.tensor_tensor(
                    stc(3), stc(0), stc(0), ALU.mult), waits=[t1])
                t2 = P.op("V", lambda e: e.tensor_tensor(
                    stc(2), stc(2), stc(3), ALU.subtract), waits=[t2, t3])
            t4 = P.op("A", lambda e: e.activation(
                stc(2), stc(2), AF.Sqrt, bias=0.0, scale=1.0), waits=[t2])
            t5 = P.op("V", lambda e: e.reciprocal(stc(1), stc(2)), waits=[t4])
            return (t1, t5)

        def ln_bcast(rows_toks):
            bM, bI = banks[5], banks[4]
            tM = P.op("T", lambda e: e.matmul(
                bM.ap[:, :], ones1r[:, :], stc(0), start=True, stop=True),
                waits=list(rows_toks) + [tok_e2] + bM.fill_waits())
            tI = P.op("T", lambda e: e.matmul(
                bI.ap[:, :], ones1r[:, :], stc(1), start=True, stop=True),
                waits=list(rows_toks) + [tok_e2] + bI.fill_waits())
            st8_sl.used(tM)
            st8_sl.used(tI)
            return tM, tI

        def ln_apply(dst, src, src_toks, wv, bv, tM, tI, extra=()):
            bM, bI = banks[5], banks[4]
            toks = []
            for db in range(NDB):
                t1 = P.op("V", lambda e, db=db: e.tensor_tensor(
                    fs1[:, :], src(db), bM.ap[:, :], ALU.add),
                    waits=[src_toks(db), tM] + list(extra))
                bM.used(t1)
                if wv is not None:
                    t2 = P.op("V", lambda e, db=db: e.scalar_tensor_tensor(
                        fs2[:, :], fs1[:, :], wv[:, db:db + 1], bI.ap[:, :],
                        ALU.mult, ALU.mult), waits=[t1, tI])
                    bI.used(t2)
                    t2 = P.op("V", lambda e, db=db: e.tensor_scalar(
                        dst(db), fs2[:, :], bv[:, db:db + 1], None, ALU.add),
                        waits=[t2])
                else:
                    t2 = P.op("V", lambda e, db=db: e.tensor_tensor(
                        dst(db), fs1[:, :], bI.ap[:, :], ALU.mult), waits=[t1, tI])
                    bI.used(t2)
                toks.append(t2)
            return toks

        def rms_apply(dst, src, src_toks, wv, tI):
            bI = banks[4]
            toks = []
            for db in range(NDB):
                t2 = P.op("V", lambda e, db=db: e.scalar_tensor_tensor(
                    dst(db), src(db), wv[:, db:db + 1], bI.ap[:, :],
                    ALU.mult, ALU.mult), waits=[src_toks(db), tI])
                bI.used(t2)
                toks.append(t2)
            return toks

        def proj_pass(W, Ktot, O, movings, evac, g_oc=2, paired=False):
            NKB = ceil_div(Ktot, 128)
            NOC = ceil_div(O, 128)
            nl = len(movings)
            assert g_oc * nl <= 4
            ocs = list(range(NOC))
            ogs = [ocs[i:i + g_oc] for i in range(0, NOC, g_oc)]
            if paired:
                prs = [(ogs[i], ogs[i + 1] if i + 1 < len(ogs) else None)
                       for i in range(0, len(ogs), 2)]
                for ogA, ogB in prs:
                    both = [(0, ogA)] + ([(1, ogB)] if ogB else [])
                    c0 = ogA[0] * 128
                    lastog = ogB if ogB else ogA
                    c1 = min(lastog[-1] * 128 + 128, O)
                    width = c1 - c0
                    mm_tok = {}
                    for kb in range(NKB):
                        r0 = kb * 128
                        rows = min(128, Ktot - r0)
                        slot = wring.next()
                        wt = P.dma("G", slot.ap[0:rows, 0:width],
                                   W[r0:r0 + rows, c0:c1],
                                   waits=slot.fill_waits(), sem=slot.qsem)
                        slot.ready = wt
                        n_in_kb = sum(len(og) for _, og in both) * nl
                        emitted = 0
                        for half, og in both:
                            base = 0 if half == 0 else 4
                            for oi, oc in enumerate(og):
                                w0 = oc * 128 - c0
                                w1 = min(w0 + 128, width)
                                for li, (tiles, ttok) in enumerate(movings):
                                    bk = banks[base + oi * nl + li]
                                    first = kb == 0
                                    last = kb == NKB - 1
                                    emitted += 1
                                    kb_final = emitted == n_in_kb
                                    waits = [wt, ttok(kb)]
                                    if first:
                                        waits += bk.fill_waits()
                                    tok = P.op("T", lambda e, bk=bk, slot=slot,
                                               rows=rows, w0=w0, w1=w1,
                                               tiles=tiles, kb=kb, first=first,
                                               last=last: e.matmul(
                                                   bk.ap[0:(w1 - w0), :],
                                                   slot.ap[0:rows, w0:w1],
                                                   tiles(kb)[0:rows, :],
                                                   start=first, stop=last),
                                               waits=waits,
                                               inc=(last or kb_final))
                                    if kb_final:
                                        slot.used(tok)
                                    if last:
                                        mm_tok[(half, oi, li)] = tok
                    for half, og in both:
                        base = 0 if half == 0 else 4
                        for oi, oc in enumerate(og):
                            ow = min(oc * 128 + 128, O) - oc * 128
                            for li in range(nl):
                                bk = banks[base + oi * nl + li]
                                evac(li, oc, bk, mm_tok[(half, oi, li)], ow)
                return
            for ogi, og in enumerate(ogs):
                base = 0 if ogi % 2 == 0 else 4
                c0 = og[0] * 128
                c1 = min(og[-1] * 128 + 128, O)
                width = c1 - c0
                mm_tok = {}
                for kb in range(NKB):
                    r0 = kb * 128
                    rows = min(128, Ktot - r0)
                    slot = wring.next()
                    wt = P.dma("G", slot.ap[0:rows, 0:width],
                               W[r0:r0 + rows, c0:c1], waits=slot.fill_waits(),
                               sem=slot.qsem)
                    slot.ready = wt
                    for oi, oc in enumerate(og):
                        w0 = oc * 128 - c0
                        w1 = min(w0 + 128, width)
                        for li, (tiles, ttok) in enumerate(movings):
                            bk = banks[base + oi * nl + li]
                            first = kb == 0
                            last = kb == NKB - 1
                            kb_final = (oi == len(og) - 1) and (li == nl - 1)
                            waits = [wt, ttok(kb)]
                            if first:
                                waits += bk.fill_waits()
                            tok = P.op("T", lambda e, bk=bk, slot=slot, rows=rows,
                                       w0=w0, w1=w1, tiles=tiles, kb=kb,
                                       first=first, last=last: e.matmul(
                                           bk.ap[0:(w1 - w0), :],
                                           slot.ap[0:rows, w0:w1],
                                           tiles(kb)[0:rows, :],
                                           start=first, stop=last),
                                       waits=waits, inc=(last or kb_final))
                            if kb_final:
                                slot.used(tok)
                            if last:
                                mm_tok[(oi, li)] = tok
                for oi, oc in enumerate(og):
                    ow = min(oc * 128 + 128, O) - oc * 128
                    for li in range(nl):
                        bk = banks[base + oi * nl + li]
                        evac(li, oc, bk, mm_tok[(oi, li)], ow)

        def vproj_pass(W, movings, evac):
            nl = len(movings)
            assert nl == 1
            tiles, ttok = movings[0]
            for og in range(4):
                base = 0 if og % 2 == 0 else 4
                mm_tok = {}
                for kb in range(NDB):
                    slot = wring.next()
                    wt = P.dma("G", slot.ap[:, :],
                               W[kb * 128:(kb + 1) * 128, og * 512:(og + 1) * 512],
                               waits=slot.fill_waits(), sem=slot.qsem)
                    slot.ready = wt
                    for tt in range(4):
                        bk = banks[base + tt]
                        first = kb == 0
                        last = kb == NDB - 1
                        waits = [wt, ttok(kb)]
                        if first:
                            waits += bk.fill_waits()
                        tok = P.op("T", lambda e, bk=bk, slot=slot, tiles=tiles,
                                   kb=kb, tt=tt, first=first, last=last: e.matmul(
                                       bk.ap[:, :],
                                       tiles(kb)[:, tt * 128:(tt + 1) * 128],
                                       slot.ap[:, :], start=first, stop=last),
                                   waits=waits, inc=(last or tt == 3))
                        if tt == 3:
                            slot.used(tok)
                        if last:
                            mm_tok[tt] = tok
                for tt in range(4):
                    evac(0, tt, og, banks[base + tt], mm_tok[tt])

        def tap_fm(name, db, u, src_ap, wtok, srcslot=None):
            if name in taps:
                t = P.op("V", lambda e: e.tensor_copy(tapb[:, :], src_ap),
                         waits=[wtok] + tap_sl.fill_waits())
                td = P.dma("S", taps[name][db, :, u * CH:(u + 1) * CH],
                           tapb[:, :], waits=[t], sem="dT")
                tap_sl.used(td)
                if srcslot is not None:
                    srcslot.used(t)

        htok = [[None, None] for _ in range(NDB)]
        for u in range(2):
            rt = ln_stats(lambda db, u=u: xt(db, u), lambda db, u=u: xtok[db][u],
                          rms=False, n=D, eps=EPS1)
            tM, tI = ln_bcast(rt)
            toks = ln_apply(lambda db, u=u: ht(db, u), lambda db, u=u: xt(db, u),
                            lambda db, u=u: xtok[db][u], vecs["n1w"], vecs["n1b"],
                            tM, tI, extra=[vtok["n1w"], vtok["n1b"]])
            for db in range(NDB):
                htok[db][u] = toks[db]
                tap_fm("h1", db, u, ht(db, u), toks[db])

        cc_tok = [None] * 4

        def kv_wave(u, blk):
            gi = (0 if blk == 1 else 2) + u
            gsem = f"dG{gi}"
            statrow_toks = []
            tz = P.op("V", lambda e: e.memset(aslot[32].ap, 0.0),
                      waits=aslot[32].fill_waits())
            aslot[32].ready = tz

            def kevac(li, oc, bk, mmtok, ow=128):
                s = aslot[oc]
                if blk == 1:
                    t = P.op("V", lambda e, bk=bk, s=s: e.tensor_single_scalar(
                        s.ap, bk.ap[:, :], CINV, ALU.mult),
                        waits=[mmtok] + s.fill_waits())
                else:
                    t0 = P.op("V", lambda e, bk=bk, oc=oc: e.tensor_scalar(
                        fs1[:, :], bk.ap[:, :], CINV, vecs["bkc2"][:, oc:oc + 1],
                        ALU.mult, ALU.add), waits=[mmtok, vtok["bkc2"]])
                    t = rope_apply(u, fs1, s, [t0] + s.fill_waits())
                bk.used(t)
                s.ready = t
                if blk == 2:
                    statrow_toks.append(k2_stats(oc, s))
                tap_fm("k1" if blk == 1 else "k2", oc, u, s.ap, t, srcslot=s)

            def k2_stats(oc, s):
                sq = bounce.next()
                ts = P.op("A", lambda e, s=s, sq=sq: e.activation(
                    sq.ap, s.ap, AF.Square, bias=0.0, scale=1.0),
                    waits=[s.ready] + sq.fill_waits())
                sq.ready = ts
                s.used(ts)
                bk = banks[3]
                tm = P.op("T", lambda e, sq=sq, bk=bk: e.matmul(
                    bk.ap[0:1, :], onesst[:, 0:1], sq.ap, start=True, stop=True),
                    waits=[ts, tok_ones] + bk.fill_waits())
                sq.used(tm)
                t1 = P.op("V", lambda e, bk=bk: e.reduce_max(
                    st8f[0:1, 2080:2081], bk.ap[0:1, :], axis=AX.X), waits=[tm])
                bk.used(t1)
                t2 = P.op("A", lambda e: e.activation(
                    st8f[0:1, 2080:2081], st8f[0:1, 2080:2081], AF.Sqrt,
                    bias=0.0, scale=1.0), waits=[t1])
                t3 = P.op("V", lambda e, oc=oc: e.tensor_single_scalar(
                    st8f[0:1, 2048 + oc:2049 + oc], st8f[0:1, 2080:2081],
                    1.01, ALU.mult), waits=[t2])
                return t3

            wname = "a_wk" if blk == 1 else "f_wk"
            mov = [(lambda kb, u=u: ht(kb, u), lambda kb, u=u: htok[kb][u])]
            proj_pass(dt[wname], D, D, mov, kevac, g_oc=3)
            if blk == 2:
                tb = P.op("V", lambda e: e.tensor_copy(
                    aslot[32].ap[0:1, 0:16], st8f[0:1, 2048:2064]),
                    waits=statrow_toks + [tz])
                aslot[32].ready = tb

            def vevac(li, tt, og, bk, mmtok):
                s = aslot[16 + tt * 4 + og]
                t = P.op("V", lambda e, bk=bk, s=s: e.tensor_copy(
                    s.ap, bk.ap[:, :]), waits=[mmtok] + s.fill_waits())
                bk.used(t)
                s.ready = t
                if ("v1" if blk == 1 else "v2") in taps:
                    nm = "v1" if blk == 1 else "v2"
                    tcp = P.op("V", lambda e, s=s: e.tensor_copy(tapb[:, :], s.ap),
                               waits=[t] + tap_sl.fill_waits())
                    s.used(tcp)
                    tdd = P.dma("S", taps[nm][u, tt, :, og * 512:(og + 1) * 512],
                                tapb[:, :], waits=[tcp], sem="dT")
                    tap_sl.used(tdd)

            vproj_pass(dt["a_wv" if blk == 1 else "f_wv"], mov, vevac)
            sready = [aslot[i].ready for i in range(33)]
            tst = P.dma("S", gin[gi].ap(), arena[:, 0:GW], waits=sready, sem=gsem)
            for i in range(33):
                aslot[i].used(tst)
            cc_tok[gi] = P.emit("G", lambda e, gi=gi: e.collective_compute(
                "AllGather", ALU.bypass,
                replica_groups=[[0, 1, 2, 3], [4, 5, 6, 7]],
                ins=[gin[gi].ap().opt()], outs=[gout[gi].ap().opt()]),
                waits=[tst], inc="cc")

        def rope_apply(u, src_f32, dstslot, waits):
            w = list(waits) + [tok_cs[-1]]
            ta = P.op("V", lambda e: e.tensor_tensor(
                fs2[:, :], src_f32[:, :], cosq[u][:, :], ALU.mult), waits=w)
            tb0 = P.op("V", lambda e: e.tensor_tensor(
                fs3[0:64, :], src_f32[64:128, :], sinq[u][64:128, :], ALU.mult),
                waits=w)
            tb1 = P.op("V", lambda e: e.tensor_tensor(
                fs3[64:128, :], src_f32[0:64, :], sinq[u][0:64, :], ALU.mult),
                waits=w)
            t0 = P.op("V", lambda e, d=dstslot: e.tensor_tensor(
                d.ap[0:64, :], fs2[0:64, :], fs3[0:64, :], ALU.subtract),
                waits=[ta, tb0] + dstslot.fill_waits())
            t1 = P.op("V", lambda e, d=dstslot: e.tensor_tensor(
                d.ap[64:128, :], fs2[64:128, :], fs3[64:128, :], ALU.add),
                waits=[ta, tb1])
            return t1

        kv_wave(0, 1)
        kv_wave(1, 1)

        def qevac1(li, oc, bk, mmtok, ow=128):
            s = aslot[li * 16 + oc]
            t = P.op("V", lambda e, bk=bk, s=s: e.tensor_copy(s.ap, bk.ap[:, :]),
                     waits=[mmtok] + s.fill_waits())
            bk.used(t)
            s.ready = t
            tap_fm("q1", oc, li, s.ap, t, srcslot=s)

        proj_pass(dt["a_wq"], D, D,
                  [(lambda kb: ht(kb, 0), lambda kb: htok[kb][0]),
                   (lambda kb: ht(kb, 1), lambda kb: htok[kb][1])],
                  qevac1, g_oc=2)
        for li in range(2):
            rd = [aslot[li * 16 + i].ready for i in range(16)]
            tq = P.dma("S", qspill[li], arena[:, li * 16 * 512:(li + 1) * 16 * 512],
                       waits=rd, sem="dQS")
            for i in range(16):
                aslot[li * 16 + i].used(tq)
        tok_qs1 = P.full("dQS")

        def gevac(li, oc, bk, mmtok, ow=128):
            s = aslot[li * 16 + oc]
            t = P.op("A", lambda e, bk=bk, s=s: e.activation(
                s.ap, bk.ap[:, :], AF.Silu, bias=0.0, scale=1.0),
                waits=[mmtok] + s.fill_waits())
            bk.used(t)
            s.ready = t
            tap_fm("hg", oc, li, s.ap, t, srcslot=s)

        proj_pass(dt["a_wg"], D, D,
                  [(lambda kb: ht(kb, 0), lambda kb: htok[kb][0]),
                   (lambda kb: ht(kb, 1), lambda kb: htok[kb][1])],
                  gevac, g_oc=2)
        for li in range(2):
            rd = [aslot[li * 16 + i].ready for i in range(16)]
            thg = P.dma("S", hgspill[li],
                        arena[:, li * 16 * 512:(li + 1) * 16 * 512],
                        waits=rd, sem="dHS")
            for i in range(16):
                aslot[li * 16 + i].used(thg)
        tok_hs = P.full("dHS")

        if stage <= 1:
            P.join()
            finish_out_dummy(P, y_out, x_sb, xtok)
            return nc, dict(taps=taps)

        kring = Ring([Slot(arena[:, 0:16 * 512]), Slot(arena[:, 16 * 512:32 * 512])])
        vring = Ring([Slot(arena[:, 32 * 512:36 * 512]),
                      Slot(arena[:, 36 * 512:40 * 512])])
        qring = Ring([Slot(qtl[:, i * 512:(i + 1) * 512]) for i in range(6)])
        for i, s in enumerate(kring.slots):
            s.qsem = f"dKR{i}"
        for i, s in enumerate(vring.slots):
            s.qsem = f"dVR{i}"
        for i, s in enumerate(qring.slots):
            s.qsem = f"dQR{i}"
        for i, s in enumerate(bounce.slots):
            s.qsem = f"dBN{i}"
        ptr = Ring([Slot(ptile[:, 0:512]), Slot(ptile[:, 512:1024])])

        def attn_mixer(blk, ztok):
            gbase = 0 if blk == 1 else 2
            kst_tok = {}
            for qv in range(2):
                iters = [("diag", qv)] + [("slot", s) for s in
                                          range(4 if qv == 0 else 8)]
                if blk == 2:
                    for s in (range(4) if qv == 0 else range(4, 8)):
                        gg, row = (gbase + (0 if s < 4 else 1)), s % 4
                        P.dma("S", kstatr[0:1, s * 16:(s + 1) * 16],
                              gout[gg][row * 128:row * 128 + 1,
                                       KW + VW:KW + VW + 16],
                              waits=[cc_tok[gg], tok_kstat0],
                              sem=f"dKS{qv}")
                    kst_tok[qv] = [P.full(f"dKS{qv}")]
                    tb1 = P.op("V", lambda e, qv=qv: e.tensor_tensor(
                        mrow[0:1, 0:128], kstatr[0:1, :],
                        blerow[0:1, 128 * qv:128 * (qv + 1)], ALU.mult),
                        waits=kst_tok.get(0, []) + kst_tok.get(1, []) + [tok_ble])
                    tmx = P.op("V", lambda e: e.tensor_copy(
                        mrow[0:1, 272:288], mrow[0:1, 0:16]), waits=[tb1])
                    for s in range(1, 8):
                        tmx = P.op("V", lambda e, s=s: e.tensor_tensor(
                            mrow[0:1, 272:288], mrow[0:1, 272:288],
                            mrow[0:1, s * 16:(s + 1) * 16], ALU.max), waits=[tmx])
                    maxk_tok = tmx
                    bkP = banks[6]
                    tP = P.op("T", lambda e, qv=qv: e.matmul(
                        bkP.ap[:, 0:8], ones1r[:, :],
                        metapad[0:1, 8 * qv:8 * (qv + 1)], start=True, stop=True),
                        waits=[tok_e2, tok_pad] + bkP.fill_waits())
                    tPc = P.op("V", lambda e: e.tensor_copy(
                        sigb[:, 0:8], bkP.ap[:, 0:8]), waits=[tP])
                    bkP.used(tPc)
                hgroups = ([list(range(0, 6)), list(range(6, 12)),
                            list(range(12, 16))] if blk == 1 else
                           [list(range(0, 5)), list(range(5, 10)),
                            list(range(10, 15)), [15]])
                for hg_i, hgrp in enumerate(hgroups):
                    ctx_banks = {h: banks[i] for i, h in enumerate(hgrp)}
                    lbank = banks[5] if blk == 2 else None
                    qslots = {}
                    bias_sl = {}
                    if blk == 2:
                        btoks = {}
                    for h in hgrp:
                        qs = qring.next()
                        tq = P.dma("S", qs.ap,
                                   qspill[qv][:, h * 512:(h + 1) * 512],
                                   waits=qs.fill_waits(), sem=qs.qsem)
                        qs.ready = tq
                        qslots[h] = qs
                        if blk == 2:
                            sq = bounce.next()
                            ts = P.op("A", lambda e, qs=qs, sq=sq: e.activation(
                                sq.ap, qs.ap, AF.Square, bias=0.0, scale=1.0),
                                waits=[tq] + sq.fill_waits())
                            sq.ready = ts
                            qs.used(ts)
                            bkq = banks[6]
                            tm = P.op("T", lambda e, sq=sq, bkq=bkq: e.matmul(
                                bkq.ap[0:1, :], onesst[:, 0:1], sq.ap,
                                start=True, stop=True),
                                waits=[ts, tok_ones] + bkq.fill_waits())
                            sq.used(tm)
                            t1 = P.op("V", lambda e, bkq=bkq: e.reduce_max(
                                st8f[0:1, 2081:2082], bkq.ap[0:1, :], axis=AX.X),
                                waits=[tm])
                            bkq.used(t1)
                            t2 = P.op("A", lambda e: e.activation(
                                st8f[0:1, 2081:2082], st8f[0:1, 2081:2082], AF.Sqrt, bias=0.0,
                                scale=1.0), waits=[t1])
                            t3 = P.op("V", lambda e: e.tensor_single_scalar(
                                st8f[0:1, 2081:2082], st8f[0:1, 2081:2082], -1.01, ALU.mult),
                                waits=[t2])
                            t4 = P.op("V", lambda e, h=h: e.tensor_tensor(
                                brow[0:1, h:h + 1], st8f[0:1, 2081:2082],
                                mrow[0:1, 272 + h:273 + h], ALU.mult),
                                waits=[t3, maxk_tok, tok_brow0] +
                                brow_sl.fill_waits())
                            btoks[h] = t4
                    if blk == 2:
                        bkB = banks[6]
                        bbs = bbase_slot
                        tB = P.op("T", lambda e: e.matmul(
                            bkB.ap[:, 0:16], ones1r[:, :], brow[0:1, :],
                            start=True, stop=True),
                            waits=list(btoks.values()) + [tok_e2] +
                            bkB.fill_waits())
                        brow_sl.used(tB)
                        tBc = P.op("V", lambda e: e.tensor_copy(
                            bbase[:, :], bkB.ap[:, 0:16]),
                            waits=[tB] + bbs.fill_waits())
                        bkB.used(tBc)
                        bbs.ready = tBc
                        tsh = P.op("V", lambda e: e.tensor_single_scalar(
                            brow[0:1, :], brow[0:1, :], 14.0, ALU.add),
                            waits=list(btoks.values()) + brow_sl.fill_waits())
                        btoks["shift"] = tsh
                        if "att2b" in taps and qv == 0 and hg_i == 0:
                            tb0 = P.op("V", lambda e: e.tensor_copy(
                                tapb[:, 0:16], bbase[:, :]),
                                waits=[tBc] + tap_sl.fill_waits())
                            tbd = P.dma("S", taps["att2b"][:, :], tapb[:, 0:16],
                                        waits=[tb0], sem="dT")
                            tap_sl.used(tbd)
                        if "att2k" in taps and qv == 0 and hg_i == 0:
                            tk0 = P.op("V", lambda e: e.tensor_copy(
                                tapb[0:1, 0:128], kstatr[0:1, :]),
                                waits=[maxk_tok] + tap_sl.fill_waits())
                            tkd = P.dma("S", taps["att2k"][:, :], tapb[0:1, 0:128],
                                        waits=[tk0], sem="dT")
                            tap_sl.used(tkd)
                        bas = biasall_slot
                        btk = None
                        for hi, h in enumerate(hgrp):
                            btk = P.op("V", lambda e, hi=hi, h=h: e.tensor_scalar(
                                basb[:, hi * 8:(hi + 1) * 8], sigb[:, 0:8], -1.0,
                                bbase[:, h:h + 1], ALU.mult, ALU.add),
                                waits=[tPc, tBc] +
                                (bas.fill_waits() if hi == 0 else []))
                        bas.ready = btk
                    first_mm = {h: True for h in hgrp}
                    lfirst = [True]
                    ltok = None
                    tiles_sched = []
                    for it in iters:
                        for kt in range(4):
                            for h in hgrp:
                                tiles_sched.append((it, kt, h))
                    n_t = len(tiles_sched)
                    k_seq = []
                    v_seq = []
                    for it in iters:
                        k_seq.append(it)
                        for kt in range(4):
                            v_seq.append((it, kt))
                    kmap = {}
                    vmap = {}
                    kn = [0]
                    vn = [0]

                    def ksrc_of(it):
                        if it[0] == "diag":
                            return gin[gbase + qv], [cc_tok[gbase + qv]]
                        s = it[1]
                        gg = gbase + (0 if s < 4 else 1)
                        row = s % 4
                        return gout[gg][row * 128:(row + 1) * 128, :], [cc_tok[gg]]

                    def ensure_k(idx):
                        while kn[0] <= idx and kn[0] < len(k_seq):
                            it = k_seq[kn[0]]
                            ksrc, kwait = ksrc_of(it)
                            ks = kring.next()
                            tk = P.dma("S", ks.ap, ksrc[:, 0:KW],
                                       waits=kwait + ks.fill_waits(), sem=ks.qsem)
                            ks.ready = tk
                            kmap[id(it)] = ks
                            kn[0] += 1

                    def ensure_v(idx):
                        while vn[0] <= idx and vn[0] < len(v_seq):
                            it, kt = v_seq[vn[0]]
                            ksrc, kwait = ksrc_of(it)
                            v = vring.next()
                            tv = P.dma("S", v.ap,
                                       ksrc[:, KW + kt * D:KW + (kt + 1) * D],
                                       waits=kwait + v.fill_waits(), sem=v.qsem)
                            v.ready = tv
                            vmap[(id(it), kt)] = v
                            vn[0] += 1
                    pend = None
                    ltok = None

                    def stage_a(j):
                        it, kt, h = tiles_sched[j]
                        ki = k_seq.index(it)
                        vi = v_seq.index((it, kt))
                        ensure_k(ki + 1)
                        ensure_v(vi if h == hgrp[0] else vi + 1)
                        ks = kmap[id(it)]
                        v = vmap[(id(it), kt)]

                        qs = qslots[h]
                        sbk = banks[6 + (j % 2)]
                        tmm = P.op("T", lambda e, sbk=sbk, ks=ks, h=h, kt=kt,
                                   qs=qs: e.matmul(
                                       sbk.ap[:, :],
                                       ks.ap[:, h * 512 + kt * 128:
                                             h * 512 + (kt + 1) * 128],
                                       qs.ap, start=True, stop=True),
                                   waits=[ks.ready, qs.ready] + sbk.fill_waits(),
                                   inc=True)
                        ks.used(tmm)
                        qs.used(tmm)
                        return (it, kt, h, sbk, tmm, v)

                    def stage_b(state, jb):
                        nonlocal ltok
                        it, kt, h, sbk, tmm, v = state
                        hi = hgrp.index(h)
                        pt = ptr.next()
                        if blk == 1:
                            rs = growcol(h, qv, it[1] if it[0] == "slot" else -1,
                                         kt)
                            tev = P.op("A", lambda e, sbk=sbk, pt=pt,
                                       rs=rs: e.activation(
                                           pt.ap, sbk.ap[:, :], AF.Copy,
                                           bias=0.0, scale=rs),
                                       waits=[tmm, grow_tok[h]] + pt.fill_waits())
                        else:
                            if it[0] == "diag":
                                bias = bbase[:, h:h + 1]
                                bwait = [bbase_slot.ready]
                                bsl = bbase_slot
                            else:
                                bias = basb[:, hi * 8 + it[1]:hi * 8 + it[1] + 1]
                                bwait = [biasall_slot.ready]
                                bsl = biasall_slot
                            tev = P.op("A", lambda e, sbk=sbk, pt=pt,
                                       bias=bias: e.activation(
                                           pt.ap, sbk.ap[:, :], AF.Exp,
                                           bias=bias, scale=1.0),
                                       waits=[tmm] + bwait + pt.fill_waits())
                            bsl.used(tev)
                        sbk.used(tev)
                        pt.ready = tev
                        if it[0] == "diag":
                            tev = P.op("V", lambda e, pt=pt, kt=kt: e.tensor_tensor(
                                pt.ap, pt.ap, trim(kt), ALU.mult),
                                waits=[tev, tok_tri[kt]])
                            pt.ready = tev
                        if blk == 2:
                            llast = jb == n_t - 1
                            tl = P.op("T", lambda e, pt=pt, h=h, lf=lfirst[0],
                                      ll=llast: e.matmul(
                                          lbank.ap[0:16, :],
                                          lind[:, 16 * h:16 * (h + 1)],
                                          pt.ap, start=lf, stop=ll),
                                      waits=[pt.ready, tok_lind] +
                                      (lbank.fill_waits() if lfirst[0] else []),
                                      inc=True)
                            lfirst[0] = False
                            pt.used(tl)
                            if llast:
                                ltok = tl
                        cbk = ctx_banks[h]
                        clast = (it is iters[-1]) and kt == 3
                        tc = P.op("T", lambda e, cbk=cbk, pt=pt, h=h, kt=kt, v=v,
                                  cf=first_mm[h], cl=clast: e.matmul(
                                      cbk.ap[:, :],
                                      v.ap[:, h * 128:(h + 1) * 128],
                                      pt.ap, start=cf, stop=cl),
                                  waits=[pt.ready, v.ready] +
                                  (cbk.fill_waits() if first_mm[h] else []),
                                  inc=True)
                        first_mm[h] = False
                        pt.used(tc)
                        v.used(tc)
                        if clast:
                            ctx_banks[h] = (cbk, tc)

                    for j in range(n_t + 1):
                        if j < n_t:
                            st = stage_a(j)
                        if j > 0:
                            stage_b(pend, j - 1)
                        if j < n_t:
                            pend = st
                    if blk == 2:
                        if "att2l" in taps and qv == 0 and hg_i == 0:
                            tl0 = P.op("V", lambda e: e.tensor_copy(
                                tapb[0:16, :], lbank.ap[0:16, :]),
                                waits=[ltok] + tap_sl.fill_waits())
                            tld = P.dma("S", taps["att2l"][:, :], tapb[0:16, :],
                                        waits=[tl0], sem="dT")
                            tap_sl.used(tld)
                            lbank.used(tl0)
                        trc = P.op("V", lambda e: e.tensor_single_scalar(
                            st16[:, :], lbank.ap[0:16, :], 1e-11, ALU.add),
                            waits=[ltok] + st16_sl.fill_waits())
                        lbank.used(trc)
                        trc = P.op("V", lambda e: e.reciprocal(
                            st16[:, :], st16[:, :]), waits=[trc])
                    else:
                        trc = None
                    for h in hgrp:
                        cbk, ctok = ctx_banks[h]
                        if blk == 1:
                            fin_retention(h, qv, cbk, ctok, ztok)
                        else:
                            fin_softmax(h, qv, cbk, ctok, trc, ztok)

        def fin_retention(h, qv, cbk, ctok, ztok):
            t0 = P.op("V", lambda e, cbk=cbk: e.tensor_copy(fs1[:, :], cbk.ap[:, :]),
                      waits=[ctok])
            cbk.used(t0)
            tg = P.op("T", lambda e, h=h, cbk=cbk: e.matmul(
                cbk.ap[:, :], e16[:, 128 * h:128 * (h + 1)], gcols[0:16, :],
                start=True, stop=True), waits=[t0, tok_e16, tok_gcols])
            t1 = P.op("V", lambda e, cbk=cbk: e.tensor_tensor(
                fs1[:, :], fs1[:, :], cbk.ap[:, :], ALU.mult), waits=[tg])
            cbk.used(t1)
            b1, b2 = bounce.next(), bounce.next()
            tb = P.op("V", lambda e, b1=b1: e.tensor_copy(b1.ap, fs1[:, :]),
                      waits=[t1] + b1.fill_waits())
            b1.ready = tb
            tsq = P.op("A", lambda e, b2=b2: e.activation(
                b2.ap, fs1[:, :], AF.Square, bias=0.0, scale=1.0),
                waits=[t1] + b2.fill_waits())
            b2.ready = tsq
            tm1 = P.op("T", lambda e, b1=b1, cbk=cbk: e.matmul(
                cbk.ap[0:1, :], onesst[:, 0:1], b1.ap, start=True, stop=True),
                waits=[tb, tok_ones, t1], inc=True)
            b1.used(tm1)
            t2 = P.op("V", lambda e, cbk=cbk: e.tensor_single_scalar(
                stc(0), cbk.ap[0:1, :], -1.0 / DHD, ALU.mult),
                waits=[tm1] + st8_sl.fill_waits())
            cbk.used(t2)
            tm2 = P.op("T", lambda e, b2=b2, cbk=cbk: e.matmul(
                cbk.ap[0:1, :], onesst[:, 0:1], b2.ap, start=True, stop=True),
                waits=[tsq, t2], inc=True)
            b2.used(tm2)
            t3 = P.op("V", lambda e, cbk=cbk: e.tensor_scalar(
                stc(2), cbk.ap[0:1, :], 1.0 / DHD, EPS1, ALU.mult, ALU.add),
                waits=[tm2])
            cbk.used(t3)
            t4 = P.op("V", lambda e: e.tensor_tensor(
                stc(3), stc(0), stc(0), ALU.mult), waits=[t2])
            t5 = P.op("V", lambda e: e.tensor_tensor(
                stc(2), stc(2), stc(3), ALU.subtract), waits=[t3, t4])
            t6 = P.op("A", lambda e: e.activation(
                stc(2), stc(2), AF.Sqrt, bias=0.0, scale=1.0), waits=[t5])
            t7 = P.op("V", lambda e: e.reciprocal(stc(1), stc(2)), waits=[t6])
            tM = P.op("T", lambda e, cbk=cbk: e.matmul(
                cbk.ap[:, :], ones1r[:, :], stc(0), start=True, stop=True),
                waits=[t2, t7, tok_e2])
            st8_sl.used(tM)
            t8 = P.op("V", lambda e, cbk=cbk: e.tensor_tensor(
                fs1[:, :], fs1[:, :], cbk.ap[:, :], ALU.add), waits=[tM])
            cbk.used(t8)
            tI = P.op("T", lambda e, cbk=cbk: e.matmul(
                cbk.ap[:, :], ones1r[:, :], stc(1), start=True, stop=True),
                waits=[t8, tok_e2])
            st8_sl.used(tI)
            hgs = bounce.next()
            thg = P.dma("S", hgs.ap, hgspill[qv][:, h * 512:(h + 1) * 512],
                        waits=hgs.fill_waits() + [tok_hs], sem=hgs.qsem)
            hgs.ready = thg
            t9 = P.op("V", lambda e, cbk=cbk: e.tensor_tensor(
                fs1[:, :], fs1[:, :], cbk.ap[:, :], ALU.mult), waits=[tI])
            cbk.used(t9)
            t10 = P.op("V", lambda e, h=h, qv=qv, hgs=hgs: e.tensor_tensor(
                ht(h, qv), fs1[:, :], hgs.ap, ALU.mult), waits=[t9, thg])
            hgs.used(t10)
            ztok[h][qv] = t10
            tap_fm("retz", h, qv, ht(h, qv), t10)

        def fin_softmax(h, qv, cbk, ctok, trc, ztok):
            rb = banks[7] if cbk is not banks[7] else banks[6]
            tR = P.op("T", lambda e, h=h, rb=rb: e.matmul(
                rb.ap[:, :], e16[:, 128 * h:128 * (h + 1)], st16[0:16, :],
                start=True, stop=True),
                waits=[trc, tok_e16] + rb.fill_waits())
            st16_sl.used(tR)
            t0 = P.op("V", lambda e, cbk=cbk: e.tensor_copy(fs1[:, :], cbk.ap[:, :]),
                      waits=[ctok])
            cbk.used(t0)
            t1 = P.op("V", lambda e, rb=rb: e.tensor_tensor(
                fs1[:, :], fs1[:, :], rb.ap[:, :], ALU.mult), waits=[t0, tR])
            rb.used(t1)
            t2 = P.op("V", lambda e, h=h, qv=qv: e.tensor_scalar(
                ht(h, qv), fs1[:, :], vecs["bv2"][:, h:h + 1], None, ALU.add),
                waits=[t1, tok_vecs])
            ztok[h][qv] = t2
            tap_fm("ctx2", h, qv, ht(h, qv), t2)

        P.join()
        ztok = [[None, None] for _ in range(NH)]
        attn_mixer(1, ztok)

        def woevac1(li, oc, bk, mmtok, ow=128):
            t = P.op("V", lambda e, oc=oc, li=li, bk=bk: e.tensor_tensor(
                xt(oc, li), xt(oc, li), bk.ap[:, :], ALU.add),
                waits=[mmtok, xtok[oc][li]])
            bk.used(t)
            xtok[oc][li] = t
            tap_fm("xwo", oc, li, xt(oc, li), t)

        proj_pass(dt["a_wo"], D, D,
                  [(lambda kb: ht(kb, 0), lambda kb: ztok[kb][0]),
                   (lambda kb: ht(kb, 1), lambda kb: ztok[kb][1])],
                  woevac1, g_oc=2)

        if stage <= 2:
            P.join()
            finish_out_dummy(P, y_out, x_sb, xtok)
            return nc, dict(taps=taps)

        P.join()
        h2tok = [[None, None] for _ in range(NDB)]
        xmid_tok = [[None, None] for _ in range(NDB)]
        for u in range(2):
            rt = ln_stats(lambda db, u=u: xt(db, u), lambda db, u=u: xtok[db][u],
                          rms=False, n=D, eps=EPS1)
            tM, tI = ln_bcast(rt)
            toks = ln_apply(lambda db, u=u: ht(db, u), lambda db, u=u: xt(db, u),
                            lambda db, u=u: xtok[db][u], vecs["n2w"], vecs["n2b"],
                            tM, tI, extra=[tok_vecs])
            for db in range(NDB):
                h2tok[db][u] = toks[db]
                tap_fm("h2", db, u, ht(db, u), toks[db])
            utok = [None] * NKB1
            usl = [aslot[i] for i in range(NKB1)]

            def w1evac(li, oc, bk, mmtok, ow=128):
                s = usl[oc]
                t = P.op("A", lambda e, bk=bk, s=s, ow=ow: e.activation(
                    s.ap[0:ow, :], bk.ap[0:ow, :], AF.Silu, bias=0.0, scale=1.0),
                    waits=[mmtok] + s.fill_waits())
                bk.used(t)
                s.ready = t

            mv = [(lambda kb, u=u: ht(kb, u), lambda kb, u=u: h2tok[kb][u])]
            proj_pass(dt["a_w1"], D, HID1, mv, w1evac, g_oc=4)

            def w2evac(li, oc, bk, mmtok, ow=128):
                s = usl[oc]
                t = P.op("V", lambda e, bk=bk, s=s, ow=ow: e.tensor_tensor(
                    s.ap[0:ow, :], s.ap[0:ow, :], bk.ap[0:ow, :], ALU.mult),
                    waits=[mmtok, s.ready])
                bk.used(t)
                s.ready = t
                utok[oc] = t

            proj_pass(dt["a_w2"], D, HID1, mv, w2evac, g_oc=4)

            def w3evac(li, oc, bk, mmtok, ow=128):
                t = P.op("V", lambda e, oc=oc, u=u, bk=bk: e.tensor_tensor(
                    xt(oc, u), xt(oc, u), bk.ap[:, :], ALU.add),
                    waits=[mmtok, xtok[oc][u]])
                bk.used(t)
                xtok[oc][u] = t

            proj_pass(dt["a_w3"], HID1, D,
                      [(lambda kb: usl[kb].ap, lambda kb: utok[kb])],
                      w3evac, g_oc=4)
            for kb in range(NKB1):
                usl[kb].used(utok[kb])

            for db in range(NDB):
                sl = fsl2 if db % 2 == 0 else fsl3
                tld = P.dma("A", sl.ap[:, :],
                            dt["x_fm"][db, :, u * CH:(u + 1) * CH],
                            waits=sl.fill_waits() + [P.full("cV"), P.full("cA")],
                            sem=sl.qsem)
                t = P.op("V", lambda e, db=db, u=u, sl=sl: e.tensor_tensor(
                    xt(db, u), xt(db, u), sl.ap[:, :], ALU.add),
                    waits=[tld, xtok[db][u]])
                sl.used(t)
                xtok[db][u] = t
            rt = ln_stats(lambda db, u=u: xt(db, u), lambda db, u=u: xtok[db][u],
                          rms=False, n=D, eps=EPS1)
            tM, tI = ln_bcast(rt)
            toks = ln_apply(lambda db, u=u: xt(db, u), lambda db, u=u: xt(db, u),
                            lambda db, u=u: xtok[db][u], None, None, tM, tI)
            for db in range(NDB):
                xtok[db][u] = toks[db]
                tap_fm("xmid", db, u, xt(db, u), toks[db])
                xmid_tok[db][u] = P.dma(
                    "S", xmidspill[db, :, u * CH:(u + 1) * CH], xt(db, u),
                    waits=[toks[db]], sem=f"dXS{u}")
                xtok[db][u] = xmid_tok[db][u]

            rt = ln_stats(lambda db, u=u: xt(db, u), lambda db, u=u: xtok[db][u],
                          rms=True, n=D, eps=EPS2)
            tM, tI = ln_bcast(rt)
            toks = rms_apply(lambda db, u=u: ht(db, u), lambda db, u=u: xt(db, u),
                             lambda db, u=u: xtok[db][u], vecs["lnf1"], tI)
            for db in range(NDB):
                htok[db][u] = toks[db]

            def qevac2(li, oc, bk, mmtok, ow=128, u=u):
                s = aslot[oc]
                t0 = P.op("V", lambda e, bk=bk, oc=oc: e.tensor_scalar(
                    fs1[:, :], bk.ap[:, :], 1.0, vecs["bq2"][:, oc:oc + 1],
                    ALU.mult, ALU.add), waits=[mmtok, tok_vecs])
                t = rope_apply(u, fs1, s, [t0] + s.fill_waits())
                bk.used(t)
                s.ready = t
                tap_fm("q2", oc, u, s.ap, t, srcslot=s)

            proj_pass(dt["f_wq"], D, D,
                      [(lambda kb, u=u: ht(kb, u), lambda kb, u=u: htok[kb][u])],
                      qevac2, g_oc=3)
            rd = [aslot[i].ready for i in range(16)]
            tq2 = P.dma("S", qspill[u], arena[:, 0:16 * 512], waits=rd,
                        sem="dQS2")
            for i in range(16):
                aslot[i].used(tq2)
            kv_wave(u, 2)

        if stage <= 3:
            P.join()
            finish_out_dummy(P, y_out, x_sb, xtok)
            return nc, dict(taps=taps)

        P.join()
        ztok2 = [[None, None] for _ in range(NH)]
        attn_mixer(2, ztok2)

        def woevac2(li, oc, bk, mmtok, ow=128):
            t = P.op("V", lambda e, oc=oc, li=li, bk=bk: e.tensor_tensor(
                xt(oc, li), xt(oc, li), bk.ap[:, :], ALU.add),
                waits=[mmtok, xtok[oc][li]])
            bk.used(t)
            xtok[oc][li] = t
            tap_fm("xwo2", oc, li, xt(oc, li), t)

        proj_pass(dt["f_wo"], D, D,
                  [(lambda kb: ht(kb, 0), lambda kb: ztok2[kb][0]),
                   (lambda kb: ht(kb, 1), lambda kb: ztok2[kb][1])],
                  woevac2, g_oc=2)

        P.join()
        for u in range(2):
            rt = ln_stats(lambda db, u=u: xt(db, u), lambda db, u=u: xtok[db][u],
                          rms=True, n=D, eps=EPS2)
            tM, tI = ln_bcast(rt)
            toks = rms_apply(lambda db, u=u: ht(db, u), lambda db, u=u: xt(db, u),
                             lambda db, u=u: xtok[db][u], vecs["lnf2"], tI)
            for db in range(NDB):
                h2tok[db][u] = toks[db]
                tap_fm("h2r", db, u, ht(db, u), toks[db])

        cmb_tok = [None, None]
        for u in range(2):
            ttoks = []
            for tt in range(4):
                bk = banks[tt % 2]
                first = True
                for kb in range(NDB):
                    last = kb == NDB - 1
                    tmm = P.op("T", lambda e, bk=bk, kb=kb, tt=tt, u=u,
                               first=first, last=last: e.matmul(
                                   bk.ap[0:128, 0:8],
                                   h_sb[:, kb * 2 * CH + u * CH + tt * 128:
                                        kb * 2 * CH + u * CH + (tt + 1) * 128],
                                   router_sb[:, kb * 8:(kb + 1) * 8],
                                   start=first, stop=last),
                               waits=[h2tok[kb][u], rtok[kb]] +
                               (bk.fill_waits() if first else []), inc=True)
                    first = False
                t1 = P.op("V", lambda e, bk=bk: e.reduce_max(
                    fs1[:, 0:1], bk.ap[:, 0:8], axis=AX.X,
                    negate=True), waits=[tmm])
                bk.used(t1)
                t2 = P.op("A", lambda e, bk=bk: e.activation(
                    fs2[:, 0:8], bk.ap[:, 0:8], AF.Exp, bias=fs1[:, 0:1],
                    scale=1.0, accum_out=fs1[:, 1:2]), waits=[t1])
                bk.used(t2)
                t3 = P.op("V", lambda e: e.reciprocal(
                    fs1[:, 2:3], fs1[:, 1:2]), waits=[t2])
                t4 = P.op("V", lambda e: e.tensor_scalar(
                    fs2[:, 0:8], fs2[:, 0:8], fs1[:, 2:3], None, ALU.mult),
                    waits=[t3])
                t5 = P.op("V", lambda e: e.reduce_max(
                    fs1[:, 3:4], fs2[:, 0:8], axis=AX.X, negate=True), waits=[t4])
                t6 = P.op("V", lambda e: e.tensor_scalar(
                    fs3[:, 0:8], fs2[:, 0:8], fs1[:, 3:4], None, ALU.add),
                    waits=[t5])
                t7 = P.op("V", lambda e: e.tensor_single_scalar(
                    fs3[:, 0:8], fs3[:, 0:8], 0.0, ALU.is_equal), waits=[t6])
                t8 = P.op("V", lambda e: e.scalar_tensor_tensor(
                    fs3[:, 0:8], fs3[:, 0:8], -2.0, fs2[:, 0:8],
                    ALU.mult, ALU.add), waits=[t7])
                t9 = P.op("V", lambda e: e.reduce_max(
                    fs1[:, 4:5], fs3[:, 0:8], axis=AX.X, negate=True), waits=[t8])
                t10 = P.op("V", lambda e: e.tensor_scalar(
                    fs3[:, 0:8], fs2[:, 0:8], fs1[:, 4:5], None, ALU.add),
                    waits=[t9])
                t11 = P.op("V", lambda e: e.tensor_single_scalar(
                    fs3[:, 0:8], fs3[:, 0:8], 0.0, ALU.is_ge), waits=[t10])
                t12 = P.op("V", lambda e: e.tensor_tensor(
                    fs1[:, 8:16], fs2[:, 0:8], fs3[:, 0:8], ALU.mult),
                    waits=[t11])
                bk6 = banks[6]
                tt_ = P.op("T", lambda e, tt=tt, bk6=bk6: e.transpose(
                    bk6.ap[0:8, tt * 128:(tt + 1) * 128], fs1[:, 8:16],
                    ident[:, :]),
                    waits=[t12, tok_id] + (bk6.fill_waits() if tt == 0 else []))
                ttoks.append(tt_)
            tc = P.op("V", lambda e, u=u: e.tensor_copy(
                cmbT[0:8, u * 512:(u + 1) * 512], banks[6].ap[0:8, :]),
                waits=ttoks)
            banks[6].used(tc)
            cmb_tok[u] = tc
            if "comb" in taps:
                P.dma("S", taps["comb"][0:8, u * 512:(u + 1) * 512],
                      cmbT[0:8, u * 512:(u + 1) * 512], waits=[tc], sem="dT")

        for ei in range(NE):
            ztoks = [[None, None] for _ in range(MI // 128)]
            zsl = [[aslot[oc], aslot[11 + oc]] for oc in range(MI // 128)]
            cb = [None, None]
            for li in range(2):
                cbb = banks[3]
                dstf = fs2 if li == 0 else fs3
                tcb = P.op("T", lambda e, ei=ei, li=li, cbb=cbb: e.matmul(
                    cbb.ap[:, :], e16[0:8, 128 * ei:128 * ei + 128],
                    cmbT[0:8, li * 512:(li + 1) * 512], start=True, stop=True),
                    waits=[cmb_tok[li], tok_e16] + cbb.fill_waits())
                tcp = P.op("V", lambda e, dstf=dstf, cbb=cbb: e.tensor_copy(
                    dstf[:, :], cbb.ap[:, :]), waits=[tcb])
                cbb.used(tcp)
                cb[li] = (dstf, tcp)

            def gevac_e(li, oc, bk, mmtok, ow=128, ei=ei):
                s = zsl[oc][li]
                t = P.op("A", lambda e, bk=bk, s=s: e.activation(
                    s.ap, bk.ap[:, :], AF.Silu, bias=0.0, scale=1.0),
                    waits=[mmtok] + s.fill_waits())
                bk.used(t)
                s.ready = t

            mv2 = [(lambda kb: ht(kb, 0), lambda kb: h2tok[kb][0]),
                   (lambda kb: ht(kb, 1), lambda kb: h2tok[kb][1])]
            proj_pass(dt["f_wge"][ei], D, MI, mv2, gevac_e, g_oc=2, paired=True)

            def uevac_e(li, oc, bk, mmtok, ow=128, ei=ei):
                s = zsl[oc][li]
                dstf, ctk = cb[li]
                t0 = P.op("V", lambda e, bk=bk, s=s: e.tensor_tensor(
                    fs1[:, :], s.ap, bk.ap[:, :], ALU.mult),
                    waits=[mmtok, s.ready])
                bk.used(t0)
                t1 = P.op("V", lambda e, s=s, dstf=dstf: e.tensor_tensor(
                    s.ap, fs1[:, :], dstf[:, :], ALU.mult), waits=[t0, ctk])
                s.ready = t1
                ztoks[oc][li] = t1

            proj_pass(dt["f_wue"][ei], D, MI, mv2, uevac_e, g_oc=2, paired=True)

            def devac_e(li, oc, bk, mmtok, ow=128):
                t = P.op("V", lambda e, oc=oc, li=li, bk=bk: e.tensor_tensor(
                    xt(oc, li), xt(oc, li), bk.ap[:, :], ALU.add),
                    waits=[mmtok, xtok[oc][li]])
                bk.used(t)
                xtok[oc][li] = t

            proj_pass(dt["f_wde"][ei], MI, D,
                      [(lambda kb: zsl[kb][0].ap, lambda kb: ztoks[kb][0]),
                       (lambda kb: zsl[kb][1].ap, lambda kb: ztoks[kb][1])],
                      devac_e, g_oc=2, paired=True)
            for oc in range(MI // 128):
                for li in range(2):
                    zsl[oc][li].used(ztoks[oc][li])

        for u in range(2):
            ssl = [aslot[i] for i in range(44)]
            stoks = [None] * 44

            def shg_evac(li, oc, bk, mmtok, ow=128):
                s = ssl[oc]
                t = P.op("A", lambda e, bk=bk, s=s: e.activation(
                    s.ap, bk.ap[:, :], AF.Silu, bias=0.0, scale=1.0),
                    waits=[mmtok] + s.fill_waits())
                bk.used(t)
                s.ready = t

            mv = [(lambda kb, u=u: ht(kb, u), lambda kb, u=u: h2tok[kb][u])]
            proj_pass(dt["f_shg"], D, SI, mv, shg_evac, g_oc=4)

            def shu_evac(li, oc, bk, mmtok, ow=128):
                s = ssl[oc]
                t = P.op("V", lambda e, bk=bk, s=s: e.tensor_tensor(
                    s.ap, s.ap, bk.ap[:, :], ALU.mult), waits=[mmtok, s.ready])
                bk.used(t)
                s.ready = t
                stoks[oc] = t

            proj_pass(dt["f_shu"], D, SI, mv, shu_evac, g_oc=4)

            bk3 = banks[3]
            first = True
            for kb in range(NDB):
                slot = wring.next()
                wt = P.dma("G", slot.ap[0:128, 0:1],
                           dt["f_sheg"][kb * 128:(kb + 1) * 128, :],
                           waits=slot.fill_waits(), sem=slot.qsem)
                slot.ready = wt
                last = kb == NDB - 1
                tmm = P.op("T", lambda e, slot=slot, kb=kb, u=u, first=first,
                           last=last, bk3=bk3: e.matmul(
                               bk3.ap[0:1, :], slot.ap[0:128, 0:1],
                               ht(kb, u), start=first, stop=last),
                           waits=[wt, h2tok[kb][u]] +
                           (bk3.fill_waits() if first else []), inc=True)
                slot.used(tmm)
                first = False
            tsg = P.op("A", lambda e, u=u, bk3=bk3: e.activation(
                sigr[0:1, u * 512:(u + 1) * 512], bk3.ap[0:1, :], AF.Sigmoid,
                bias=0.0, scale=1.0),
                waits=[tmm, tok_sig0] + sigr_sl.fill_waits())
            bk3.used(tsg)
            bkS = banks[3]
            tsb = P.op("T", lambda e, u=u, bkS=bkS: e.matmul(
                bkS.ap[:, :], ones1r[:, :], sigr[0:1, u * 512:(u + 1) * 512],
                start=True, stop=True),
                waits=[tsg, tok_e2] + bkS.fill_waits())
            sigr_sl.used(tsb)
            tsb2 = P.op("V", lambda e, bkS=bkS: e.tensor_copy(
                sigb[:, :], bkS.ap[:, :]), waits=[tsb])
            bkS.used(tsb2)

            def shd_evac(li, oc, bk, mmtok, ow=128, u=u, tsb2=tsb2):
                t0 = P.op("V", lambda e, bk=bk: e.tensor_tensor(
                    fs1[:, :], sigb[:, :], bk.ap[:, :], ALU.mult),
                    waits=[mmtok, tsb2])
                bk.used(t0)
                t1 = P.op("V", lambda e, oc=oc, u=u: e.tensor_tensor(
                    xt(oc, u), xt(oc, u), fs1[:, :], ALU.add),
                    waits=[t0, xtok[oc][u]])
                xtok[oc][u] = t1
                tap_fm("moe", oc, u, xt(oc, u), t1)

            proj_pass(dt["f_shd"], SI, D,
                      [(lambda kb: ssl[kb].ap, lambda kb: stoks[kb])],
                      shd_evac, g_oc=4)
            for kb in range(44):
                ssl[kb].used(stoks[kb])

        P.join()
        out_dma = []
        for u in range(2):
            for db in range(NDB):
                sl = fsl2 if db % 2 == 0 else fsl3
                tld = P.dma("A", sl.ap[:, :],
                            xmidspill[db, :, u * CH:(u + 1) * CH],
                            waits=[xmid_tok[db][u]] + sl.fill_waits(),
                            sem=sl.qsem)
                t = P.op("V", lambda e, db=db, u=u, sl=sl: e.tensor_tensor(
                    xt(db, u), xt(db, u), sl.ap[:, :], ALU.add),
                    waits=[tld, xtok[db][u]])
                sl.used(t)
                xtok[db][u] = t
            rt = ln_stats(lambda db, u=u: xt(db, u), lambda db, u=u: xtok[db][u],
                          rms=False, n=D, eps=EPS1)
            tM, tI = ln_bcast(rt)
            toks = ln_apply(lambda db, u=u: xt(db, u), lambda db, u=u: xt(db, u),
                            lambda db, u=u: xtok[db][u], None, None, tM, tI)
            for db in range(NDB):
                xtok[db][u] = toks[db]
                out_dma.append(P.dma("A", y_out[db, :, u * CH:(u + 1) * CH],
                                     xt(db, u), waits=[toks[db]], sem="dY"))

        P.emit("S", lambda e: e.nop(),
               waits=[P.full(nm) for nm in list(P.cnt)])

    return nc, dict(taps=taps)


def finish_out_dummy(P, y_out, x_sb, xtok):
    for u in range(2):
        for db in range(16):
            P.dma("A", y_out[db, :, u * CH:(u + 1) * CH],
                  x_sb[:, db * 2 * CH + u * CH: db * 2 * CH + (u + 1) * CH],
                  waits=[xtok[db][u]], sem="dY")
    P.emit("S", lambda e: e.nop(), waits=[P.full(nm) for nm in list(P.cnt)])


def _core_meta(r):
    cA, cB = r, 7 - r
    chunks = [cA, cB]
    pos = np.zeros((1, 2 * CH), np.float32)
    for qv in range(2):
        pos[0, qv * CH:(qv + 1) * CH] = (np.arange(CH, dtype=np.float32)
                                         + 512 * chunks[qv])

    def chunk_at_slot(s):
        rho, w = s % 4, s // 4
        return rho if w == 0 else 7 - rho

    rs = np.zeros((1, 2 * NMC), np.float32)
    for qv in range(2):
        qc = chunks[qv]
        for s in range(8):
            kc = chunk_at_slot(s)
            for kt in range(4):
                c = (qv * 8 + s) * 4 + kt
                if kc < qc:
                    rs[0, c] = 512 * qc - (512 * kc + 128 * kt)
                    rs[0, NMC + c] = 1.0
    for kt in range(4):
        rs[0, 64 + kt] = -128.0 * kt
        rs[0, NMC + 64 + kt] = 1.0

    pad = np.full((1, 16), 1000.0, np.float32)
    ble = np.zeros((1, 256), np.float32)
    for qv in range(2):
        qc = chunks[qv]
        for s in range(8):
            kc = chunk_at_slot(s)
            if kc < qc:
                pad[0, qv * 8 + s] = 0.0
            for h in range(NH):
                ble[0, qv * 128 + s * 16 + h] = 1.0 if kc <= qc else 0.0
    return pos, rs, pad, ble


def _vec(a):
    return np.ascontiguousarray(np.asarray(a, np.float32).reshape(NDB, 128).T)


def kernel(**inputs):
    nc, meta = build_program()
    wmap = {
        "a_wq": "L0_attn_wq", "a_wk": "L0_attn_wk", "a_wv": "L0_attn_wv",
        "a_wg": "L0_attn_wg", "a_wo": "L0_attn_wo", "a_w1": "L0_attn_w1",
        "a_w2": "L0_attn_w2", "a_w3": "L0_attn_w3",
        "f_wq": "L0_ffn_wq", "f_wk": "L0_ffn_wk", "f_wv": "L0_ffn_wv",
        "f_wo": "L0_ffn_wo", "f_router": "L0_ffn_w_router",
        "f_wge": "L0_ffn_w_gate_e", "f_wue": "L0_ffn_w_up_e",
        "f_wde": "L0_ffn_w_down_e", "f_shg": "L0_ffn_w_sh_gate",
        "f_shu": "L0_ffn_w_sh_up", "f_shd": "L0_ffn_w_sh_down",
    }
    vmap = {
        "n1w": "L0_attn_norm1_w", "n1b": "L0_attn_norm1_b",
        "n2w": "L0_attn_norm2_w", "n2b": "L0_attn_norm2_b",
        "lnf1": "L0_ffn_w_ln1", "lnf2": "L0_ffn_w_ln2",
        "bq2": "L0_ffn_bq", "bk2": "L0_ffn_bk", "bv2": "L0_ffn_bv",
    }
    shared = {k: np.ascontiguousarray(np.asarray(inputs[v], np.float32))
              for k, v in wmap.items()}
    shared["f_sheg"] = np.ascontiguousarray(
        np.asarray(inputs["L0_ffn_w_sh_eg"], np.float32).reshape(D, 1))
    for k, v in vmap.items():
        shared[k] = _vec(inputs[v])
    X = np.asarray(inputs["L0_attn_x"], np.float32)

    in_maps = []
    for core in range(8):
        b, r = core // 4, core % 4
        cA, cB = r, 7 - r
        xa = X[b, 512 * cA:512 * cA + 512]
        xb = X[b, 512 * cB:512 * cB + 512]
        xf = np.concatenate([xa, xb], 0).T.reshape(NDB, 128, 2 * CH)
        pos, rs, pad, ble = _core_meta(r)
        m = dict(shared)
        m["x_fm"] = np.ascontiguousarray(xf)
        m["pos"] = pos
        m["rsmeta"] = rs
        m["padmeta"] = pad
        m["blerow"] = ble
        in_maps.append(m)

    res = run_bass_kernel_spmd(nc, in_maps, core_ids=list(range(8)))

    out = np.empty((2, 4096, D), np.float32)
    for core in range(8):
        b, r = core // 4, core % 4
        cA, cB = r, 7 - r
        y = np.asarray(res.results[core]["y"])
        yt = y.reshape(D, 2 * CH).T
        out[b, 512 * cA:512 * cA + 512] = yt[0:CH]
        out[b, 512 * cB:512 * cB + 512] = yt[CH:2 * CH]
    return out
